# version 02
# baseline (speedup 1.0000x reference)
from contextlib import ExitStack
import numpy as np
import concourse.bass as bass
import concourse.mybir as mybir
from concourse.bass_utils import run_bass_kernel_spmd

F32 = mybir.dt.float32
BF16 = mybir.dt.bfloat16
U8 = mybir.dt.uint8
AF = mybir.ActivationFunctionType
ALU = mybir.AluOpType
AX = mybir.AxisListType

N_CORES = 8
T = 4096
D = 1024
NT = T // 128


ENG_OF_QUEUE = {"qsp": "sp", "qact": "act", "qpool": "pool", "tsp": "sp", "tact": "act"}


class Buf:
    __slots__ = ("name", "t", "last_w", "readers", "root")

    def __init__(self, name, t, root=None):
        self.name = name
        self.t = t
        self.last_w = None
        self.readers = []
        self.root = root if root is not None else self

    def __getitem__(self, idx):
        return self.t[idx]

    @property
    def ap(self):
        return self.t[:]


class _Op:
    __slots__ = ("stream", "fn", "deps", "needs_inc", "milestone", "tag", "res")

    def __init__(self, stream, fn, tag):
        self.stream = stream
        self.fn = fn
        self.deps = []
        self.needs_inc = False
        self.milestone = 0
        self.res = 0
        self.tag = tag


class Prog:
    def __init__(self, nc):
        self.nc = nc
        self.eng = {
            "pe": nc.tensor, "act": nc.scalar, "dve": nc.vector,
            "pool": nc.gpsimd, "sp": nc.sync,
        }
        self.ops = []
        self._sb_bytes = 0
        self._psum_pool = []
        self._psum_i = 0
        self._name_i = 0
        self._csts = {}
        self._scopes = []
        self._pb = None
        self._pb_done = set()

    def push_scope(self):
        self._scopes.append(ExitStack())

    def pop_scope(self):
        self._scopes.pop().close()

    def barrier(self):
        last = {}
        for i, o in enumerate(self.ops):
            last[o.stream] = i
        self._pb = list(last.values())
        self._pb_done = set()

    def idb16(self):
        if not hasattr(self, "_idb"):
            idf = self.sbuf("idb_f32", [128, 128], F32, base=True)
            idb = self.sbuf("idb_bf", [128, 128], BF16, base=True)
            self.op("pool", lambda g, idf=idf: g.iota(
                idf[:, :], [[1, 128]], base=0, channel_multiplier=-1,
                allow_small_or_imprecise_dtypes=True), wr=[idf])
            self.op("dve", lambda v, idf=idf, idb=idb: v.tensor_scalar(
                idb[:, :], idf[:, :], 0.0, None, ALU.is_equal, ALU.bypass),
                    rd=[idf], wr=[idb])
            self._idb = idb
        return self._idb

    def tr128(self, dst_buf, dst_ap, src_buf, src_ap):
        idb = self.idb16()
        pb = self.psum_bf()
        self.op("pe", lambda t, pb=pb, src_ap=src_ap, idb=idb: t.transpose(
            pb[:, 0:128], src_ap, idb[:, :]), rd=[src_buf, idb], wr=[pb], tag="tr")
        self.op("dve", lambda v, pb=pb, dst_ap=dst_ap: v.tensor_copy(
            dst_ap, pb[:, 0:128]), rd=[pb], wr=[dst_buf], tag="trev")

    def cst(self, val):
        if val not in self._csts:
            b = self.sbuf("cst", [128, 1], F32, base=True)
            self.op("dve", lambda v, b=b, val=val: v.memset(b[:, :], float(val)),
                    wr=[b], tag="cst")
            self._csts[val] = b
        return self._csts[val]

    def _nm(self, base):
        self._name_i += 1
        return f"{base}_{self._name_i}"

    def sbuf(self, name, shape, dtype, base=False, **kw):
        if base or not self._scopes:
            t = self.nc.alloc_sbuf_tensor(self._nm(name), shape, dtype, **kw)
        else:
            t = self._scopes[-1].enter_context(
                self.nc.sbuf_tensor(self._nm(name), shape, dtype, **kw))
        return Buf(name, t)

    def dram(self, name, shape, dtype, kind="Internal"):
        t = self.nc.dram_tensor(self._nm(name), shape, dtype, kind=kind)
        return Buf(name, t.ap())

    def make_psum_pool(self, n_f32=6, n_bf16=2):
        self._psum_pool = [
            Buf("psF", self.nc.alloc_psum_tensor(f"psF{i}", [128, 512], F32))
            for i in range(n_f32)
        ]
        self._psum_bf = [
            Buf("psB", self.nc.alloc_psum_tensor(f"psB{i}", [128, 1024], BF16))
            for i in range(n_bf16)
        ]
        self._psum_i = 0
        self._psum_bi = 0

    def psum(self):
        lo, hi = getattr(self, "_psum_win", (0, len(self._psum_pool)))
        n = hi - lo
        b = self._psum_pool[lo + self._psum_i % n]
        self._psum_i += 1
        return b

    def psum_window(self, lo, hi):
        self._psum_win = (lo, hi)
        self._psum_i = 0

    def psum_bank(self, i):
        return self._psum_pool[i]

    def psum_bf(self):
        b = self._psum_bf[self._psum_bi % len(self._psum_bf)]
        self._psum_bi += 1
        return b

    def op(self, stream, fn, rd=(), wr=(), tag=""):
        idx = len(self.ops)
        o = _Op(stream, fn, tag)
        my_eng = ENG_OF_QUEUE.get(stream, stream)
        if self._pb is not None and stream not in self._pb_done:
            o.deps.extend(self._pb)
            self._pb_done.add(stream)
        def _need(pidx):
            ps = self.ops[pidx].stream
            if ps == stream == "pe":
                return False
            if ps == stream and stream in ENG_OF_QUEUE:
                return False
            return True

        for b in rd:
            if b is None:
                continue
            b = b.root
            lw = b.last_w
            if lw is not None and _need(lw):
                o.deps.append(lw)
            b.readers.append(idx)
        for b in wr:
            if b is None:
                continue
            b = b.root
            if b.last_w is not None and _need(b.last_w):
                o.deps.append(b.last_w)
            for r in b.readers:
                if r != idx and _need(r):
                    o.deps.append(r)
            b.last_w = idx
            b.readers = []
        self.ops.append(o)
        return idx

    def dma(self, queue, dst_ap, src_ap, rd=(), wr=(), tag=""):
        host = self.eng[ENG_OF_QUEUE[queue]]

        def fn(_):
            return host.dma_start(dst_ap, src_ap)

        return self.op(queue, fn, rd=rd, wr=wr, tag=tag)

    def dma_t(self, queue, dst_ap, src_ap, rd=(), wr=(), tag=""):
        queue = {"qsp": "tsp", "qact": "tact"}.get(queue, queue)
        host = self.eng[ENG_OF_QUEUE[queue]]

        def fn(_):
            return host.dma_start(dst_ap, src_ap, transpose=True)

        return self.op(queue, fn, rd=rd, wr=wr, tag=tag)

    NLANE = 4

    def emit(self):
        nc = self.nc
        NL = self.NLANE
        for o in self.ops:
            for d in o.deps:
                self.ops[d].needs_inc = True
        counts = {}
        qidx = {}
        for o in self.ops:
            if o.stream in ENG_OF_QUEUE:
                o.needs_inc = True
                j = qidx.get(o.stream, 0)
                qidx[o.stream] = j + 1
                o.res = j % NL
                key = (o.stream, o.res)
                counts[key] = counts.get(key, 0) + 16
                o.milestone = counts[key]
            elif o.needs_inc:
                key = (o.stream, 0)
                counts[key] = counts.get(key, 0) + 1
                o.milestone = counts[key]
        sems = {k: nc.alloc_semaphore(f"sem_{k[0]}_{k[1]}") for k in counts}
        watermark = {}
        n_waits = 0
        for o in self.ops:
            host_eng_name = ENG_OF_QUEUE.get(o.stream, o.stream)
            eng = self.eng[host_eng_name]
            need = {}
            for d in o.deps:
                p = self.ops[d]
                skey = (p.stream, p.res)
                key = (host_eng_name, skey)
                if watermark.get(key, 0) < p.milestone:
                    need[skey] = max(need.get(skey, 0), p.milestone)
            if o.stream in ENG_OF_QUEUE and o.milestone > 16:
                skey = (o.stream, o.res)
                v = o.milestone - 16
                if watermark.get((host_eng_name, skey), 0) < v:
                    need[skey] = max(need.get(skey, 0), v)
            for skey, v in sorted(need.items()):
                eng.wait_ge(sems[skey], v)
                watermark[(host_eng_name, skey)] = v
                n_waits += 1
            inst = o.fn(eng)
            if o.needs_inc:
                inst.then_inc(sems[(o.stream, o.res)],
                              16 if o.stream in ENG_OF_QUEUE else 1)
        for en in ("sp",):
            eng = self.eng[en]
            for skey, total in counts.items():
                if watermark.get((en, skey), 0) < total:
                    eng.wait_ge(sems[skey], total)
        self._n_waits = n_waits
        return counts



def bcast_ap(dram_buf_ap, n=128):
    return dram_buf_ap.partition_broadcast(n)


def ln_into_fm(p, x_dram, xlnf, tiles=range(NT), eps=1e-5, pool_n=3, out_tm=None,
               qs=("qsp", "qact", "qpool")):
    pool = []
    for i in range(pool_n):
        pool.append(dict(
            xt=p.sbuf("ln_xt", [128, D], F32),
            sq=p.sbuf("ln_sq", [128, D], F32),
            s=p.sbuf("ln_s", [128, 1], F32),
            mu=p.sbuf("ln_mu", [128, 1], F32),
            ss=p.sbuf("ln_ss", [128, 1], F32),
            var=p.sbuf("ln_var", [128, 1], F32),
            rstd=p.sbuf("ln_rstd", [128, 1], F32),
            xln=p.sbuf("ln_xln", [128, D], BF16, is_transpose_dest=False),
        ))
    for j, i in enumerate(tiles):
        e = pool[j % pool_n]
        xt, sq, s, mu, ss, var, rstd, xln = (e[k] for k in
                                             ("xt", "sq", "s", "mu", "ss", "var", "rstd", "xln"))
        q = qs[j % len(qs)]
        p.dma(q, xt[:, :], x_dram.t[i * 128:(i + 1) * 128, :], wr=[xt], tag="ln_ld")
        p.op("dve", lambda v, xt=xt, s=s: v.tensor_reduce(s[:, :], xt[:, :], AX.X, ALU.add),
             rd=[xt], wr=[s], tag="ln_sum")
        p.op("act", lambda a, xt=xt, sq=sq, ss=ss: a.activation(
            sq[:, :], xt[:, :], AF.Square, accum_out=ss[:, :]),
             rd=[xt], wr=[sq, ss], tag="ln_sq")
        p.op("act", lambda a, ss=ss: a.copy(ss[:, :], ss[:, :]),
             rd=[ss], wr=[ss], tag="ln_sq_guard")
        p.op("dve", lambda v, s=s, mu=mu: v.tensor_scalar(
            mu[:, :], s[:, :], 1.0 / D, None, ALU.mult, ALU.bypass),
             rd=[s], wr=[mu], tag="ln_mu")
        p.op("dve", lambda v, mu=mu, var=var: v.tensor_tensor(
            var[:, :], mu[:, :], mu[:, :], ALU.mult),
             rd=[mu], wr=[var], tag="ln_musq")
        p.op("dve", lambda v, ss=ss, var=var: v.scalar_tensor_tensor(
            var[:, :], ss[:, :], 1.0 / D, var[:, :], ALU.mult, ALU.subtract),
             rd=[ss, var], wr=[var], tag="ln_var")
        epsc = p.cst(float(eps))
        p.op("act", lambda a, var=var, rstd=rstd, epsc=epsc: a.activation(
            rstd[:, :], var[:, :], AF.Sqrt, bias=epsc[:, :]),
             rd=[var, epsc], wr=[rstd], tag="ln_sqrt")
        p.op("dve", lambda v, rstd=rstd: v.reciprocal(rstd[:, :], rstd[:, :]),
             rd=[rstd], wr=[rstd], tag="ln_recip")
        p.op("dve", lambda v, xt=xt, mu=mu, rstd=rstd, xln=xln: v.tensor_scalar(
            xln[:, :], xt[:, :], mu[:, :], rstd[:, :], ALU.subtract, ALU.mult),
             rd=[xt, mu, rstd], wr=[xln], tag="ln_norm")
        for c in range(8):
            q2 = ("qsp", "qact")[(j + c) % 2]
            p.dma_t(q2, xlnf.t[:, c * T + i * 128: c * T + (i + 1) * 128],
                    xln[:, c * 128:(c + 1) * 128], rd=[xln], wr=[xlnf], tag="ln_tr")
        if out_tm is not None:
            p.op("dve", lambda v, xln=xln, i=i: v.tensor_copy(
                out_tm.t[:, i * D:(i + 1) * D], xln[:, :]),
                 rd=[xln], wr=[out_tm], tag="ln_keep")


def residual_add_store(p, x_dram, xnew_dram, delta_psum_fn, n_out_cols=D,
                       tiles=range(NT), pool_n=3, q="qsp"):
    raise NotImplementedError



def build_l1ffn(nc, per_core):
    p = Prog(nc)
    x = Buf("x", nc.dram_tensor("x", [T, D], F32, kind="ExternalInput").ap())
    k0 = Buf("k0", nc.dram_tensor("k0", [2 * D, 4096], U8, kind="ExternalInput").ap())
    k1 = Buf("k1", nc.dram_tensor("k1", [4096, 4096], U8, kind="ExternalInput").ap())
    k2 = Buf("k2", nc.dram_tensor("k2", [4096, D], U8, kind="ExternalInput").ap())
    s0 = Buf("s0", nc.dram_tensor("s0", [1], F32, kind="ExternalInput").ap())
    s1 = Buf("s1", nc.dram_tensor("s1", [1], F32, kind="ExternalInput").ap())
    s2 = Buf("s2", nc.dram_tensor("s2", [1], F32, kind="ExternalInput").ap())
    xout = Buf("xout", nc.dram_tensor("xout", [T, D], F32, kind="ExternalOutput").ap())
    p.make_psum_pool(6, 2)
    wds = build_sign_weights(p, k0, k1, k2)
    emit_l1ffn(p, x, xout, *wds, s0, s1, s2)
    p.emit()
    return p


def build_sign_weights(p, k0, k1, k2):
    H = 4096
    w0d = p.dram("w0d", [D, H], BF16)
    w1d = p.dram("w1d", [H, H], BF16)
    w2d = p.dram("w2d", [H, D], BF16)
    if True:
        ua = p.sbuf("cv_ua", [128, H], U8)
        ub = p.sbuf("cv_ub", [128, H], U8)
        tf = p.sbuf("cv_tf", [128, H], F32)
        tb = p.sbuf("cv_tb", [128, H], BF16)
        for k in range(8):
            p.dma("qsp", ua[:, :], k0.t[k * 128:(k + 1) * 128, :], rd=[k0], wr=[ua])
            p.dma("qact", ub[:, :], k0.t[D + k * 128: D + (k + 1) * 128, :], rd=[k0], wr=[ub])
            p.op("dve", lambda v, ua=ua, ub=ub, tf=tf: v.tensor_tensor(
                tf[:, :], ua[:, :], ub[:, :], ALU.subtract), rd=[ua, ub], wr=[tf])
            p.op("dve", lambda v, tf=tf, tb=tb: v.tensor_scalar(
                tb[:, :], tf[:, :], 2.0, None, ALU.mult, ALU.bypass), rd=[tf], wr=[tb])
            p.dma("qpool", w0d.t[k * 128:(k + 1) * 128, :], tb[:, :], rd=[tb], wr=[w0d])
        for k in range(32):
            p.dma("qsp", ua[:, :], k1.t[k * 128:(k + 1) * 128, :], rd=[k1], wr=[ua])
            p.op("dve", lambda v, ua=ua, tb=tb: v.tensor_scalar(
                tb[:, :], ua[:, :], 2.0, -1.0, ALU.mult, ALU.add), rd=[ua], wr=[tb])
            p.dma("qpool", w1d.t[k * 128:(k + 1) * 128, :], tb[:, :], rd=[tb], wr=[w1d])
        for k in range(32):
            p.dma("qsp", ua[:, 0:D], k2.t[k * 128:(k + 1) * 128, :], rd=[k2], wr=[ua])
            p.op("dve", lambda v, ua=ua, tb=tb: v.tensor_scalar(
                tb[:, 0:D], ua[:, 0:D], 2.0, -1.0, ALU.mult, ALU.add), rd=[ua], wr=[tb])
            p.dma("qpool", w2d.t[k * 128:(k + 1) * 128, :], tb[:, 0:D], rd=[tb], wr=[w2d])
    return w0d, w1d, w2d


def emit_l1ffn(p, x, xout, w0d, w1d, w2d, s0, s1, s2):
    H = 4096
    scol = []
    for i, s in enumerate((s0, s1, s2)):
        c = p.sbuf("scol", [128, 1], F32)
        p.dma("qsp", c[:, :], s.t[0:1].partition_broadcast(128), rd=[s], wr=[c], tag="s_bc")
        scol.append(c)

    SW = 512
    NS = T // SW
    h1f = p.sbuf("h1f", [128, 32 * SW], BF16)
    h2f = p.sbuf("h2f", [128, 32 * SW], BF16)
    xlnf = p.sbuf("ffn_xlnf", [128, 8 * SW], BF16, is_transpose_dest=True)
    w0p = [p.sbuf("w0p", [128, 8 * 512], BF16) for _ in range(1)]
    w1p = [p.sbuf("w1p", [128, 32 * 512], BF16) for _ in range(1)]
    w2s = [p.sbuf("w2s", [128, D], BF16) for _ in range(2)]
    tmp = p.sbuf("ffn_tmp", [128, 512], F32)
    xt2s = [p.sbuf("ffn_xt", [128, D], F32) for _ in range(2)]

    lnp = [dict(
        xt=p.sbuf("f_xt", [128, D], F32), sq=p.sbuf("f_sq", [128, D], F32),
        s=p.sbuf("f_s", [128, 1], F32), mu=p.sbuf("f_mu", [128, 1], F32),
        ss=p.sbuf("f_ss", [128, 1], F32), var=p.sbuf("f_var", [128, 1], F32),
        rstd=p.sbuf("f_rstd", [128, 1], F32), xln=p.sbuf("f_xln", [128, D], BF16),
    ) for _ in range(2)]

    for si in range(NS):
        t0 = si * SW
        for jj in range(SW // 128):
            i = (t0 // 128) + jj
            e = lnp[jj % 2]
            xt, sq, s, mu, ss, var, rstd, xln = (e[k] for k in
                                                 ("xt", "sq", "s", "mu", "ss", "var", "rstd", "xln"))
            p.dma("qsp", xt[:, :], x.t[i * 128:(i + 1) * 128, :], rd=[x], wr=[xt])
            p.op("dve", lambda v, xt=xt, s=s: v.tensor_reduce(s[:, :], xt[:, :], AX.X, ALU.add),
                 rd=[xt], wr=[s])
            p.op("act", lambda a, xt=xt, sq=sq, ss=ss: a.activation(
                sq[:, :], xt[:, :], AF.Square, accum_out=ss[:, :]), rd=[xt], wr=[sq, ss])
            p.op("act", lambda a, ss=ss: a.copy(ss[:, :], ss[:, :]), rd=[ss], wr=[ss])
            p.op("dve", lambda v, s=s, mu=mu: v.tensor_scalar(
                mu[:, :], s[:, :], 1.0 / D, None, ALU.mult, ALU.bypass), rd=[s], wr=[mu])
            p.op("dve", lambda v, mu=mu, var=var: v.tensor_tensor(
                var[:, :], mu[:, :], mu[:, :], ALU.mult), rd=[mu], wr=[var])
            p.op("dve", lambda v, ss=ss, var=var: v.scalar_tensor_tensor(
                var[:, :], ss[:, :], 1.0 / D, var[:, :], ALU.mult, ALU.subtract),
                 rd=[ss, var], wr=[var])
            epsc = p.cst(1e-5)
            p.op("act", lambda a, var=var, rstd=rstd, epsc=epsc: a.activation(
                rstd[:, :], var[:, :], AF.Sqrt, bias=epsc[:, :]), rd=[var, epsc], wr=[rstd])
            p.op("dve", lambda v, rstd=rstd: v.reciprocal(rstd[:, :], rstd[:, :]),
                 rd=[rstd], wr=[rstd])
            p.op("dve", lambda v, xt=xt, mu=mu, rstd=rstd, xln=xln: v.tensor_scalar(
                xln[:, :], xt[:, :], mu[:, :], rstd[:, :], ALU.subtract, ALU.mult),
                 rd=[xt, mu, rstd], wr=[xln])
            for c in range(8):
                p.tr128(xlnf, xlnf.t[:, c * SW + jj * 128: c * SW + (jj + 1) * 128],
                        xln, xln[:, c * 128:(c + 1) * 128])

        G1 = 4
        for g in range(8):
            wp = w0p[0]
            for c in range(8):
                p.dma(("qsp", "qact", "qpool")[c % 3],
                      wp.t[:, c * 512:(c + 1) * 512],
                      w0d.t[c * 128:(c + 1) * 128, g * 512:(g + 1) * 512],
                      rd=[w0d], wr=[wp], tag="w0ld")
            pss = [p.psum() for _ in range(G1)]
            for mi in range(G1):
                m = g * G1 + mi
                for k in range(8):
                    p.op("pe", lambda t, wp=wp, k=k, mi=mi, ps=pss[mi]:
                         t.matmul(ps[:, 0:SW],
                                  wp[:, k * 512 + mi * 128: k * 512 + mi * 128 + 128],
                                  xlnf[:, k * SW:(k + 1) * SW],
                                  start=(k == 0), stop=(k == 7)),
                         rd=[wp, xlnf], wr=[pss[mi]], tag="mm1")
            for mi in range(G1):
                m = g * G1 + mi
                p.op("dve", lambda v, ps=pss[mi], sc=scol[0], tmp=tmp: v.tensor_scalar(
                    tmp[:, 0:SW], ps[:, 0:SW], sc[:, :], None, ALU.mult, ALU.bypass),
                     rd=[pss[mi], scol[0]], wr=[tmp], tag="ev1a")
                p.op("act", lambda a, tmp=tmp, m=m: a.activation(
                    h1f.t[:, m * SW:(m + 1) * SW], tmp[:, 0:SW], AF.Relu, scale=2.0),
                     rd=[tmp], wr=[h1f], tag="ev1b")

        G2 = 4
        for g in range(8):
            wp = w1p[0]
            for c in range(32):
                p.dma(("qsp", "qact", "qpool")[c % 3],
                      wp.t[:, c * 512:(c + 1) * 512],
                      w1d.t[c * 128:(c + 1) * 128, g * 512:(g + 1) * 512],
                      rd=[w1d], wr=[wp], tag="w1ld")
            pss = [p.psum() for _ in range(G2)]
            for mi in range(G2):
                for k in range(32):
                    p.op("pe", lambda t, wp=wp, k=k, mi=mi, ps=pss[mi]:
                         t.matmul(ps[:, 0:SW],
                                  wp[:, k * 512 + mi * 128: k * 512 + mi * 128 + 128],
                                  h1f[:, k * SW:(k + 1) * SW],
                                  start=(k == 0), stop=(k == 31)),
                         rd=[wp, h1f], wr=[pss[mi]], tag="mm2")
            for mi in range(G2):
                m = g * G2 + mi
                p.op("dve", lambda v, ps=pss[mi], sc=scol[1], tmp=tmp: v.tensor_scalar(
                    tmp[:, 0:SW], ps[:, 0:SW], sc[:, :], None, ALU.mult, ALU.bypass),
                     rd=[pss[mi], scol[1]], wr=[tmp], tag="ev2a")
                p.op("act", lambda a, tmp=tmp, m=m: a.activation(
                    h2f.t[:, m * SW:(m + 1) * SW], tmp[:, 0:SW], AF.Relu, scale=2.0),
                     rd=[tmp], wr=[h2f], tag="ev2b")

        for half in range(2):
            pss = [p.psum() for _ in range(4)]
            for k in range(32):
                ws = w2s[k % 2]
                p.dma("qact", ws[:, :], w2d.t[k * 128:(k + 1) * 128, :], rd=[w2d], wr=[ws], tag="w2ld")
                for tt2 in range(2):
                    tt = half * 2 + tt2
                    for cc in range(2):
                        p.op("pe", lambda t, ws=ws, tt=tt, cc=cc, k=k, ps=pss[tt2 * 2 + cc], tt2=tt2:
                             t.matmul(ps[:, 0:512],
                                      h2f[:, (k) * SW + tt * 128: (k) * SW + tt * 128 + 128],
                                      ws[:, cc * 512:(cc + 1) * 512],
                                      start=(k == 0), stop=(k == 31)),
                             rd=[ws, h2f], wr=[pss[tt2 * 2 + cc]], tag="mm3")
            for tt2 in range(2):
                tt = half * 2 + tt2
                i = (t0 // 128) + tt
                xt2 = xt2s[tt2]
                p.dma("qsp", xt2[:, :], x.t[i * 128:(i + 1) * 128, :], rd=[x], wr=[xt2])
                for cc in range(2):
                    ps = pss[tt2 * 2 + cc]
                    p.op("dve", lambda v, ps=ps, sc=scol[2], cc=cc, xt2=xt2: v.scalar_tensor_tensor(
                        xt2[:, cc * 512:(cc + 1) * 512], ps[:, 0:512], sc[:, :],
                        xt2[:, cc * 512:(cc + 1) * 512], ALU.mult, ALU.add),
                         rd=[ps, scol[2], xt2], wr=[xt2], tag="ev3")
                p.dma("qpool", xout.t[i * 128:(i + 1) * 128, :], xt2[:, :], rd=[xt2], wr=[xout])
    return xout






def emit_l1attnb(p, x, xout, wq, wk, wvT, proj):
    P = 256
    xlnf = p.sbuf("lb_xlnf", [128, 8 * T], BF16, is_transpose_dest=True)
    ln_into_fm(p, x, xlnf)

    projb = p.sbuf("lb_projb", [128, 8 * P], BF16)
    tmpf = p.sbuf("lb_tmpf", [128, 1024], F32)
    for c in range(8):
        p.dma("qsp", tmpf[:, 0:P], proj.t[c * 128:(c + 1) * 128, :], rd=[proj], wr=[tmpf])
        p.op("dve", lambda v, c=c, tmpf=tmpf: v.tensor_copy(
            projb.t[:, c * P:(c + 1) * P], tmpf[:, 0:P]), rd=[tmpf], wr=[projb])
    wqkb = p.sbuf("lb_wqkb", [128, 8 * 1024], BF16)
    wps = []
    for wsrc in (wq, wk):
        wp = p.sbuf("lb_wp", [128, 8 * P], BF16)
        for c in range(8):
            p.dma("qact", tmpf[:, :], wsrc.t[c * 128:(c + 1) * 128, :], rd=[wsrc], wr=[tmpf])
            p.op("act", lambda a, c=c, tmpf=tmpf: a.copy(
                wqkb.t[:, c * 1024:(c + 1) * 1024], tmpf[:, :]), rd=[tmpf], wr=[wqkb])
        for m in range(8):
            ps = p.psum()
            for k in range(8):
                p.op("pe", lambda t, k=k, m=m, ps=ps: t.matmul(
                    ps[:, 0:P], wqkb[:, k * 1024 + m * 128: k * 1024 + m * 128 + 128],
                    projb[:, k * P:(k + 1) * P], start=(k == 0), stop=(k == 7)),
                     rd=[wqkb, projb], wr=[ps], tag="fold")
            p.op("dve", lambda v, ps=ps, m=m, wp=wp: v.tensor_copy(
                wp.t[:, m * P:(m + 1) * P], ps[:, 0:P]), rd=[ps], wr=[wp])
        wps.append(wp)
    wqp, wkp = wps

    wvb = wqkb

    qp_fm = p.sbuf("lb_qpfm", [128, 2 * T], BF16)
    kp_tm = p.sbuf("lb_kptm", [128, NT * P], BF16)
    e1 = p.sbuf("lb_e1", [128, 512], F32)
    e2 = p.sbuf("lb_e2", [128, 512], F32)

    def elu1(ps, out_buf, out_ap, w):
        p.op("dve", lambda v, ps=ps: v.tensor_scalar(
            e1[:, 0:w], ps[:, 0:w], 0.0, None, ALU.min, ALU.bypass), rd=[ps], wr=[e1])
        p.op("act", lambda a: a.activation(e1[:, 0:w], e1[:, 0:w], AF.Exp),
             rd=[e1], wr=[e1])
        p.op("dve", lambda v, ps=ps: v.tensor_scalar(
            e2[:, 0:w], ps[:, 0:w], 0.0, None, ALU.max, ALU.bypass), rd=[ps], wr=[e2])
        p.op("dve", lambda v, out_ap=out_ap: v.tensor_tensor(
            out_ap, e1[:, 0:w], e2[:, 0:w], ALU.add), rd=[e1, e2], wr=[out_buf], tag="elu")

    for pc in range(2):
        for n in range(8):
            ps = p.psum()
            for k in range(8):
                p.op("pe", lambda t, pc=pc, n=n, k=k, ps=ps: t.matmul(
                    ps[:, 0:512], wqp[:, k * P + pc * 128: k * P + pc * 128 + 128],
                    xlnf[:, k * T + n * 512: k * T + (n + 1) * 512],
                    start=(k == 0), stop=(k == 7)), rd=[wqp, xlnf], wr=[ps], tag="qproj")
            elu1(ps, qp_fm, qp_fm.t[:, pc * T + n * 512: pc * T + (n + 1) * 512], 512)
    for i in range(NT):
        ps = p.psum()
        for k in range(8):
            p.op("pe", lambda t, i=i, k=k, ps=ps: t.matmul(
                ps[:, 0:P], xlnf[:, k * T + i * 128: k * T + (i + 1) * 128],
                wkp[:, k * P:(k + 1) * P], start=(k == 0), stop=(k == 7)),
                 rd=[xlnf, wkp], wr=[ps], tag="kproj")
        elu1(ps, kp_tm, kp_tm.t[:, i * P:(i + 1) * P], P)

    for c in range(8):
        p.dma("qsp", tmpf[:, :], wvT.t[c * 128:(c + 1) * 128, :], rd=[wvT], wr=[tmpf])
        p.op("act", lambda a, c=c, tmpf=tmpf: a.copy(
            wvb.t[:, c * 1024:(c + 1) * 1024], tmpf[:, :]), rd=[tmpf], wr=[wvb])
    kv = p.sbuf("lb_kv", [128, 2 * 1024], BF16)
    vtile = p.sbuf("lb_vtile", [128, 1024], BF16)
    p.psum_window(0, 2)
    kvps = [[p.psum_bank(2 + 2 * pc + n) for n in range(2)] for pc in range(2)]
    for i in range(NT):
        for n in range(2):
            ps = p.psum()
            for k in range(8):
                p.op("pe", lambda t, i=i, n=n, k=k, ps=ps: t.matmul(
                    ps[:, 0:512], xlnf[:, k * T + i * 128: k * T + (i + 1) * 128],
                    wvb[:, k * 1024 + n * 512: k * 1024 + (n + 1) * 512],
                    start=(k == 0), stop=(k == 7)), rd=[xlnf, wvb], wr=[ps], tag="vproj")
            p.op("dve", lambda v, ps=ps, n=n: v.tensor_copy(
                vtile.t[:, n * 512:(n + 1) * 512], ps[:, 0:512]), rd=[ps], wr=[vtile])
        for pc in range(2):
            for n in range(2):
                p.op("pe", lambda t, pc=pc, n=n, i=i, ps=kvps[pc][n]: t.matmul(
                    ps[:, 0:512], kp_tm[:, i * P + pc * 128: i * P + pc * 128 + 128],
                    vtile[:, n * 512:(n + 1) * 512],
                    start=(i == 0), stop=(i == NT - 1)), rd=[kp_tm, vtile],
                     wr=[kvps[pc][n]], tag="kv")
    for pc in range(2):
        for n in range(2):
            p.op("dve", lambda v, ps=kvps[pc][n], pc=pc, n=n: v.tensor_copy(
                kv.t[:, pc * 1024 + n * 512: pc * 1024 + (n + 1) * 512], ps[:, 0:512]),
                 rd=[kvps[pc][n]], wr=[kv])

    p.psum_window(0, 6)
    qp_tm = p.sbuf("lb_qptm", [128, P], F32)
    zc = p.sbuf("lb_zc", [128, NT], F32)
    for i in range(NT):
        ps = p.psum()
        for k in range(8):
            p.op("pe", lambda t, i=i, k=k, ps=ps: t.matmul(
                ps[:, 0:P], xlnf[:, k * T + i * 128: k * T + (i + 1) * 128],
                wqp[:, k * P:(k + 1) * P], start=(k == 0), stop=(k == 7)),
                 rd=[xlnf, wqp], wr=[ps], tag="qproj_tm")
        elu1(ps, qp_tm, qp_tm.t[:, 0:P], P)
        p.op("dve", lambda v, i=i: v.tensor_reduce(
            zc.t[:, i:i + 1], qp_tm[:, 0:P], AX.X, ALU.add), rd=[qp_tm], wr=[zc])
        p.op("dve", lambda v, i=i: v.tensor_scalar(
            zc.t[:, i:i + 1], zc.t[:, i:i + 1], 1e-8, None, ALU.add, ALU.bypass),
             rd=[zc], wr=[zc])
        p.op("dve", lambda v, i=i: v.reciprocal(zc.t[:, i:i + 1], zc.t[:, i:i + 1]),
             rd=[zc], wr=[zc])

    xt2 = p.sbuf("lb_xt2", [128, D], F32)
    for i in range(NT):
        p.dma("qsp", xt2[:, :], x.t[i * 128:(i + 1) * 128, :], rd=[x], wr=[xt2])
        for n in range(2):
            ps = p.psum()
            for pc in range(2):
                p.op("pe", lambda t, i=i, n=n, pc=pc, ps=ps: t.matmul(
                    ps[:, 0:512], qp_fm[:, pc * T + i * 128: pc * T + (i + 1) * 128],
                    kv[:, pc * 1024 + n * 512: pc * 1024 + (n + 1) * 512],
                    start=(pc == 0), stop=(pc == 1)), rd=[qp_fm, kv], wr=[ps], tag="num")
            p.op("dve", lambda v, ps=ps, i=i, n=n, xt2=xt2: v.scalar_tensor_tensor(
                xt2[:, n * 512:(n + 1) * 512], ps[:, 0:512], zc.t[:, i:i + 1],
                xt2[:, n * 512:(n + 1) * 512], ALU.mult, ALU.add),
                 rd=[ps, zc, xt2], wr=[xt2], tag="numev")
        p.dma("qpool", xout.t[i * 128:(i + 1) * 128, :], xt2[:, :], rd=[xt2], wr=[xout])


def build_l1attnb(nc):
    p = Prog(nc)
    x = Buf("x", nc.dram_tensor("x", [T, D], F32, kind="ExternalInput").ap())
    wq = Buf("wq", nc.dram_tensor("wq", [D, D], F32, kind="ExternalInput").ap())
    wk = Buf("wk", nc.dram_tensor("wk", [D, D], F32, kind="ExternalInput").ap())
    wvT = Buf("wvT", nc.dram_tensor("wvT", [D, D], F32, kind="ExternalInput").ap())
    proj = Buf("proj", nc.dram_tensor("proj", [D, 256], F32, kind="ExternalInput").ap())
    xout = Buf("xout", nc.dram_tensor("xout", [T, D], F32, kind="ExternalOutput").ap())
    p.make_psum_pool(6, 2)
    emit_l1attnb(p, x, xout, wq, wk, wvT, proj)
    p.emit()
    return p



def emit_moe(p, x, xout, wr, w1, w2, w3):
    H = 1536
    NH = H // 128
    xlnf = p.sbuf("mo_xlnf", [128, 8 * T], BF16, is_transpose_dest=True)
    ln_into_fm(p, x, xlnf)

    wrb = p.sbuf("mo_wrb", [128, 64], BF16)
    tmpw = p.sbuf("mo_tmpw", [128, 1536], F32)
    for c in range(8):
        p.dma("qsp", tmpw[:, 0:8], wr.t[c * 128:(c + 1) * 128, :], rd=[wr], wr=[tmpw])
        p.op("act", lambda a, c=c: a.copy(wrb.t[:, c * 8:(c + 1) * 8], tmpw[:, 0:8]),
             rd=[tmpw], wr=[wrb])
    combs = p.sbuf("mo_combs", [128, NT * 8], F32)
    lt = p.sbuf("mo_lt", [128, 8], F32)
    m1 = p.sbuf("mo_m1", [128, 1], F32)
    nm1 = p.sbuf("mo_nm1", [128, 1], F32)
    msk = p.sbuf("mo_msk", [128, 8], F32)
    m2 = p.sbuf("mo_m2", [128, 1], F32)
    ind = p.sbuf("mo_ind", [128, 8], F32)
    eex = p.sbuf("mo_eex", [128, 8], F32)
    den = p.sbuf("mo_den", [128, 1], F32)
    for i in range(NT):
        ps = p.psum()
        for k in range(8):
            p.op("pe", lambda t, i=i, k=k, ps=ps: t.matmul(
                ps[:, 0:8], xlnf[:, k * T + i * 128: k * T + (i + 1) * 128],
                wrb[:, k * 8:(k + 1) * 8], start=(k == 0), stop=(k == 7)),
                 rd=[xlnf, wrb], wr=[ps], tag="router")
        p.op("dve", lambda v, ps=ps: v.tensor_copy(lt[:, :], ps[:, 0:8]), rd=[ps], wr=[lt])
        p.op("dve", lambda v: v.tensor_reduce(m1[:, :], lt[:, :], AX.X, ALU.max),
             rd=[lt], wr=[m1])
        p.op("dve", lambda v: v.tensor_scalar(
            nm1[:, :], m1[:, :], -1.0, None, ALU.mult, ALU.bypass), rd=[m1], wr=[nm1])
        p.op("dve", lambda v: v.tensor_scalar(
            msk[:, :], lt[:, :], m1[:, :], None, ALU.is_ge, ALU.bypass),
             rd=[lt, m1], wr=[msk])
        p.op("dve", lambda v: v.scalar_tensor_tensor(
            msk[:, :], msk[:, :], -1e30, lt[:, :], ALU.mult, ALU.add),
             rd=[msk, lt], wr=[msk])
        p.op("dve", lambda v: v.tensor_reduce(m2[:, :], msk[:, :], AX.X, ALU.max),
             rd=[msk], wr=[m2])
        p.op("dve", lambda v: v.tensor_scalar(
            ind[:, :], lt[:, :], m2[:, :], None, ALU.is_ge, ALU.bypass),
             rd=[lt, m2], wr=[ind])
        p.op("act", lambda a: a.activation(eex[:, :], lt[:, :], AF.Exp, bias=nm1[:, :]),
             rd=[lt, nm1], wr=[eex])
        p.op("act", lambda a: a.activation(den[:, :], m2[:, :], AF.Exp, bias=nm1[:, :]),
             rd=[m2, nm1], wr=[den])
        p.op("dve", lambda v: v.tensor_scalar(
            den[:, :], den[:, :], 1.0, None, ALU.add, ALU.bypass), rd=[den], wr=[den])
        p.op("dve", lambda v: v.reciprocal(den[:, :], den[:, :]), rd=[den], wr=[den])
        p.op("dve", lambda v: v.tensor_tensor(eex[:, :], eex[:, :], ind[:, :], ALU.mult),
             rd=[eex, ind], wr=[eex])
        p.op("dve", lambda v, i=i: v.tensor_scalar(
            combs.t[:, i * 8:(i + 1) * 8], eex[:, :], den[:, :], None, ALU.mult, ALU.bypass),
             rd=[eex, den], wr=[combs])

    QW = 1024
    NQ = T // QW
    actf = p.sbuf("mo_actf", [128, NH * QW], BF16)
    out_acc = p.sbuf("mo_oacc", [128, QW], F32)
    out_accs = p.sbuf("mo_oaccs", [128, 8 * 1024], F32)
    panf = p.sbuf("mo_panf", [128, 1024], F32)
    panb = p.sbuf("mo_panb", [128, 1024], BF16)
    pan3b = p.sbuf("mo_pan3b", [128, 1024], BF16)
    w2b = p.sbuf("mo_w2b", [128, NH * 1024], BF16)
    tmph = p.sbuf("mo_tmph", [128, 512], F32)
    xt2 = p.sbuf("mo_xt2", [128, D], F32)
    for q in range(NQ):
        for e in range(8):
            for K in range(NH):
                p.dma("qsp", tmpw[:, 0:1024], w2.t[e, K * 128:(K + 1) * 128, :],
                      rd=[w2], wr=[tmpw])
                p.op("act", lambda a, K=K: a.copy(
                    w2b.t[:, K * 1024:(K + 1) * 1024], tmpw[:, 0:1024]),
                     rd=[tmpw], wr=[w2b])
            for m in range(NH):
                for k in range(8):
                    p.dma("qact", panf.t[:, k * 128:(k + 1) * 128],
                          w1.t[e, k * 128:(k + 1) * 128, m * 128:(m + 1) * 128],
                          rd=[w1], wr=[panf])
                p.op("act", lambda a: a.copy(panb[:, :], panf[:, :]), rd=[panf], wr=[panb])
                for k in range(8):
                    p.dma("qact", panf.t[:, k * 128:(k + 1) * 128],
                          w3.t[e, k * 128:(k + 1) * 128, m * 128:(m + 1) * 128],
                          rd=[w3], wr=[panf])
                p.op("act", lambda a: a.copy(pan3b[:, :], panf[:, :]), rd=[panf], wr=[pan3b])
                for n in range(QW // 512):
                    psh = p.psum()
                    for k in range(8):
                        p.op("pe", lambda t, q=q, n=n, k=k, ps=psh: t.matmul(
                            ps[:, 0:512], panb[:, k * 128:(k + 1) * 128],
                            xlnf[:, k * T + q * QW + n * 512: k * T + q * QW + (n + 1) * 512],
                            start=(k == 0), stop=(k == 7)), rd=[panb, xlnf], wr=[psh], tag="mm_h")
                    psg = p.psum()
                    for k in range(8):
                        p.op("pe", lambda t, q=q, n=n, k=k, ps=psg: t.matmul(
                            ps[:, 0:512], pan3b[:, k * 128:(k + 1) * 128],
                            xlnf[:, k * T + q * QW + n * 512: k * T + q * QW + (n + 1) * 512],
                            start=(k == 0), stop=(k == 7)), rd=[pan3b, xlnf], wr=[psg], tag="mm_g")
                    p.op("act", lambda a, ps=psh: a.activation(
                        tmph[:, 0:512], ps[:, 0:512], AF.Silu), rd=[psh], wr=[tmph])
                    p.op("dve", lambda v, ps=psg, m=m, n=n: v.tensor_tensor(
                        actf.t[:, m * QW + n * 512: m * QW + (n + 1) * 512],
                        tmph[:, 0:512], ps[:, 0:512], ALU.mult),
                         rd=[tmph, psg], wr=[actf], tag="actev")
            for tt in range(8):
                i = q * 8 + tt
                for c in range(2):
                    psy = p.psum()
                    for K in range(NH):
                        p.op("pe", lambda t, K=K, tt=tt, c=c, ps=psy: t.matmul(
                            ps[:, 0:512], actf[:, K * QW + tt * 128: K * QW + tt * 128 + 128],
                            w2b[:, K * 1024 + c * 512: K * 1024 + (c + 1) * 512],
                            start=(K == 0), stop=(K == NH - 1)), rd=[actf, w2b], wr=[psy], tag="mm_y")
                    if e == 0:
                        p.op("dve", lambda v, ps=psy, i=i, e=e, tt=tt, c=c: v.tensor_scalar(
                            out_accs.t[:, tt * 1024 + c * 512: tt * 1024 + (c + 1) * 512],
                            ps[:, 0:512], combs.t[:, i * 8 + e: i * 8 + e + 1], None,
                            ALU.mult, ALU.bypass), rd=[psy, combs], wr=[out_accs], tag="yev")
                    else:
                        p.op("dve", lambda v, ps=psy, i=i, e=e, tt=tt, c=c: v.scalar_tensor_tensor(
                            out_accs.t[:, tt * 1024 + c * 512: tt * 1024 + (c + 1) * 512],
                            ps[:, 0:512], combs.t[:, i * 8 + e: i * 8 + e + 1],
                            out_accs.t[:, tt * 1024 + c * 512: tt * 1024 + (c + 1) * 512],
                            ALU.mult, ALU.add), rd=[psy, combs, out_accs], wr=[out_accs], tag="yev")
        for tt in range(8):
            i = q * 8 + tt
            p.dma("qsp", xt2[:, :], x.t[i * 128:(i + 1) * 128, :], rd=[x], wr=[xt2])
            p.op("dve", lambda v, tt=tt: v.tensor_tensor(
                xt2[:, :], xt2[:, :], out_accs.t[:, tt * 1024:(tt + 1) * 1024], ALU.add),
                 rd=[xt2, out_accs], wr=[xt2], tag="resid")
            p.dma("qpool", xout.t[i * 128:(i + 1) * 128, :], xt2[:, :], rd=[xt2], wr=[xout])


def build_moe(nc):
    p = Prog(nc)
    x = Buf("x", nc.dram_tensor("x", [T, D], F32, kind="ExternalInput").ap())
    wr = Buf("wr", nc.dram_tensor("wr", [D, 8], F32, kind="ExternalInput").ap())
    w1 = Buf("w1", nc.dram_tensor("w1", [8, D, 1536], F32, kind="ExternalInput").ap())
    w2 = Buf("w2", nc.dram_tensor("w2", [8, 1536, D], F32, kind="ExternalInput").ap())
    w3 = Buf("w3", nc.dram_tensor("w3", [8, D, 1536], F32, kind="ExternalInput").ap())
    xout = Buf("xout", nc.dram_tensor("xout", [T, D], F32, kind="ExternalOutput").ap())
    p.make_psum_pool(6, 2)
    emit_moe(p, x, xout, wr, w1, w2, w3)
    p.emit()
    return p



def col_load(p, q, dst_ap, src_1d_ap, rd, wr):
    p.dma(q, dst_ap, src_1d_ap.rearrange("(a b) -> a b", b=1), rd=rd, wr=wr)


def emit_l1attna(p, x, xout, lng, lnb, w_inT, b_in, convw, convb, A, Bm,
                 w_cT, b_c, w_dT, b_d, log_delta, w_outT, b_out):
    E = 2048
    NE = E // 128
    x2f = p.sbuf("la_x2f", [128, 8 * T], BF16, is_transpose_dest=True)
    statb = p.sbuf("la_statb", [128, 2 * NT], F32)
    grow = p.sbuf("la_grow", [128, D], F32)
    brow = p.sbuf("la_brow", [128, D], F32)
    borow = p.sbuf("la_borow", [128, D], F32)
    p.dma("qsp", grow[:, :], lng.t[:].partition_broadcast(128), rd=[lng], wr=[grow])
    p.dma("qsp", brow[:, :], lnb.t[:].partition_broadcast(128), rd=[lnb], wr=[brow])
    p.dma("qsp", borow[:, :], b_out.t[:].partition_broadcast(128), rd=[b_out], wr=[borow])

    lnp = [dict(
        xt=p.sbuf("la_xt", [128, D], F32), sq=p.sbuf("la_sq", [128, D], F32),
        s=p.sbuf("la_s", [128, 1], F32), mu=p.sbuf("la_mu", [128, 1], F32),
        ss=p.sbuf("la_ss", [128, 1], F32), var=p.sbuf("la_var", [128, 1], F32),
        rstd=p.sbuf("la_rstd", [128, 1], F32), xln=p.sbuf("la_xln", [128, D], F32),
        x2=p.sbuf("la_x2", [128, D], F32), x2b=p.sbuf("la_x2b", [128, D], BF16),
    ) for _ in range(1)]
    epsc = p.cst(1e-5)

    def stats(e, src, sfx=""):
        xt, sq, s, mu, ss, var, rstd = (e[k] for k in ("xt", "sq", "s", "mu", "ss", "var", "rstd"))
        p.op("dve", lambda v: v.tensor_reduce(s[:, :], src[:, :], AX.X, ALU.add), rd=[src], wr=[s])
        p.op("act", lambda a: a.activation(sq[:, :], src[:, :], AF.Square, accum_out=ss[:, :]),
             rd=[src], wr=[sq, ss])
        p.op("act", lambda a: a.copy(ss[:, :], ss[:, :]), rd=[ss], wr=[ss])
        p.op("dve", lambda v: v.tensor_scalar(mu[:, :], s[:, :], 1.0 / D, None, ALU.mult, ALU.bypass),
             rd=[s], wr=[mu])
        p.op("dve", lambda v: v.tensor_tensor(var[:, :], mu[:, :], mu[:, :], ALU.mult),
             rd=[mu], wr=[var])
        p.op("dve", lambda v: v.scalar_tensor_tensor(var[:, :], ss[:, :], 1.0 / D, var[:, :],
                                                     ALU.mult, ALU.subtract), rd=[ss, var], wr=[var])
        p.op("act", lambda a: a.activation(rstd[:, :], var[:, :], AF.Sqrt, bias=epsc[:, :]),
             rd=[var, epsc], wr=[rstd])
        p.op("dve", lambda v: v.reciprocal(rstd[:, :], rstd[:, :]), rd=[rstd], wr=[rstd])

    for i in range(NT):
        e = lnp[0]
        xt, mu, rstd, xln, x2, x2b = (e[k] for k in ("xt", "mu", "rstd", "xln", "x2", "x2b"))
        p.dma("qsp", xt[:, :], x.t[i * 128:(i + 1) * 128, :], rd=[x], wr=[xt])
        stats(e, xt)
        p.op("dve", lambda v, i=i: v.tensor_copy(statb.t[:, i:i + 1], mu[:, :]),
             rd=[mu], wr=[statb])
        p.op("dve", lambda v, i=i: v.tensor_copy(statb.t[:, NT + i:NT + i + 1], rstd[:, :]),
             rd=[rstd], wr=[statb])
        p.op("dve", lambda v: v.tensor_scalar(xln[:, :], xt[:, :], mu[:, :], rstd[:, :],
                                              ALU.subtract, ALU.mult), rd=[xt, mu, rstd], wr=[xln])
        stats(e, xln)
        p.op("dve", lambda v: v.tensor_scalar(x2[:, :], xln[:, :], mu[:, :], rstd[:, :],
                                              ALU.subtract, ALU.mult), rd=[xln, mu, rstd], wr=[x2])
        p.op("dve", lambda v: v.tensor_tensor(x2[:, :], x2[:, :], grow[:, :], ALU.mult),
             rd=[x2, grow], wr=[x2])
        p.op("dve", lambda v: v.tensor_tensor(x2b[:, :], x2[:, :], brow[:, :], ALU.add),
             rd=[x2, brow], wr=[x2b])
        for c in range(8):
            p.dma_t(("qsp", "qact")[c % 2],
                    x2f.t[:, c * T + i * 128: c * T + (i + 1) * 128],
                    x2b[:, c * 128:(c + 1) * 128], rd=[x2b], wr=[x2f])

    bic = p.sbuf("la_bic", [128, 32], F32)
    for c in range(32):
        col_load(p, "qsp", bic.t[:, c:c + 1], b_in.t[c * 128:(c + 1) * 128], [b_in], [bic])
    cbc = p.sbuf("la_cbc", [128, NE], F32)
    for c in range(NE):
        col_load(p, "qsp", cbc.t[:, c:c + 1], convb.t[c * 128:(c + 1) * 128], [convb], [cbc])
    bcc = p.sbuf("la_bcc", [128, NE], F32)
    bdc = p.sbuf("la_bdc", [128, NE], F32)
    for c in range(NE):
        col_load(p, "qsp", bcc.t[:, c:c + 1], b_c.t[c * 128:(c + 1) * 128], [b_c], [bcc])
        col_load(p, "qsp", bdc.t[:, c:c + 1], b_d.t[c * 128:(c + 1) * 128], [b_d], [bdc])
    ldc = p.sbuf("la_ldc", [128, NE], F32)
    for c in range(NE):
        col_load(p, "qsp", ldc.t[:, c:c + 1], log_delta.t[c * 128:(c + 1) * 128],
                 [log_delta], [ldc])
    adc = p.sbuf("la_adc", [128, NE], F32)
    t1c = p.sbuf("la_t1c", [128, NE], F32)
    onec = p.cst(1.0)
    p.op("act", lambda a: a.activation(t1c[:, :], ldc[:, :], AF.Abs), rd=[ldc], wr=[t1c])
    p.op("act", lambda a: a.activation(t1c[:, :], t1c[:, :], AF.Exp, scale=-1.0),
         rd=[t1c], wr=[t1c])
    p.op("act", lambda a: a.activation(t1c[:, :], t1c[:, :], AF.Ln, bias=onec[:, :]),
         rd=[t1c, onec], wr=[t1c])
    p.op("dve", lambda v: v.scalar_tensor_tensor(adc[:, :], ldc[:, :], 0.0, t1c[:, :],
                                                 ALU.max, ALU.add), rd=[ldc, t1c], wr=[adc])
    p.op("act", lambda a: a.activation(adc[:, :], adc[:, :], AF.Exp, scale=-1.0),
         rd=[adc], wr=[adc])
    cbsum = p.sbuf("la_cbsum", [128, NE], F32)
    abt = p.sbuf("la_abt", [128, 16], F32)
    abt2 = p.sbuf("la_abt2", [128, 16], F32)
    for c in range(NE):
        p.dma("qsp", abt[:, :], A.t[c * 128:(c + 1) * 128, :], rd=[A], wr=[abt])
        p.dma("qact", abt2[:, :], Bm.t[c * 128:(c + 1) * 128, :], rd=[Bm], wr=[abt2])
        p.op("dve", lambda v: v.tensor_tensor(abt[:, :], abt[:, :], abt2[:, :], ALU.mult),
             rd=[abt, abt2], wr=[abt])
        p.op("dve", lambda v, c=c: v.tensor_reduce(cbsum.t[:, c:c + 1], abt[:, :], AX.X, ALU.add),
             rd=[abt], wr=[cbsum])
    cwc = p.sbuf("la_cwc", [128, NE * 4], F32)
    for c in range(NE):
        p.dma("qsp", cwc.t[:, c * 4:(c + 1) * 4], convw.t[c * 128:(c + 1) * 128, :],
              rd=[convw], wr=[cwc])

    xcd = p.dram("la_xcd", [E, T], BF16)
    sgd = p.dram("la_sgd", [E, T], BF16)
    cvf = p.sbuf("la_cvf", [128, E], F32)
    cvb = p.sbuf("la_cvb", [128, E], BF16)

    xbc = p.sbuf("la_xbc", [128, T], F32)
    acc = p.sbuf("la_acc", [128, T], F32)
    xcc = p.sbuf("la_xcc", [128, T], BF16)
    for ec in range(32):
        for k in range(8):
            p.dma("qact", cvf.t[:, k * 128:(k + 1) * 128],
                  w_inT.t[k * 128:(k + 1) * 128, ec * 128:(ec + 1) * 128],
                  rd=[w_inT], wr=[cvf])
        p.op("act", lambda a: a.copy(cvb[:, 0:1024], cvf[:, 0:1024]), rd=[cvf], wr=[cvb])
        for n in range(8):
            ps = p.psum()
            for k in range(8):
                p.op("pe", lambda t, n=n, k=k, ps=ps: t.matmul(
                    ps[:, 0:512], cvb[:, k * 128:(k + 1) * 128],
                    x2f[:, k * T + n * 512:k * T + (n + 1) * 512],
                    start=(k == 0), stop=(k == 7)), rd=[cvb, x2f], wr=[ps], tag="inproj")
            if ec < 16:
                p.op("dve", lambda v, ps=ps, ec=ec, n=n: v.tensor_scalar(
                    xbc.t[:, n * 512:(n + 1) * 512], ps[:, 0:512], bic.t[:, ec:ec + 1],
                    None, ALU.add, ALU.bypass), rd=[ps, bic], wr=[xbc], tag="xbev")
            else:
                p.op("act", lambda a, ps=ps, ec=ec, n=n: a.activation(
                    xcc.t[:, n * 512:(n + 1) * 512], ps[:, 0:512], AF.Sigmoid,
                    bias=bic.t[:, ec:ec + 1]), rd=[ps, bic], wr=[xcc], tag="sgev")
        if ec < 16:
            p.op("dve", lambda v, ec=ec: v.tensor_scalar(
                acc[:, :], xbc[:, :], cwc.t[:, ec * 4 + 3: ec * 4 + 4], None,
                ALU.mult, ALU.bypass), rd=[xbc, cwc], wr=[acc])
            for j in range(3):
                d = 3 - j
                p.op("dve", lambda v, ec=ec, j=j, d=d: v.scalar_tensor_tensor(
                    acc[:, d:T], xbc[:, 0:T - d], cwc.t[:, ec * 4 + j: ec * 4 + j + 1],
                    acc[:, d:T], ALU.mult, ALU.add), rd=[xbc, cwc, acc], wr=[acc])
            p.op("act", lambda a, ec=ec: a.activation(
                xcc[:, :], acc[:, :], AF.Silu, bias=cbc.t[:, ec:ec + 1]),
                 rd=[acc, cbc], wr=[xcc])
            p.dma("qpool", xcd.t[ec * 128:(ec + 1) * 128, :], xcc[:, :], rd=[xcc], wr=[xcd])
        else:
            p.dma("qpool", sgd.t[(ec - 16) * 128:(ec - 15) * 128, :], xcc[:, :],
                  rd=[xcc], wr=[sgd])

    wcbd = p.dram("la_wcbd", [E, E], BF16)
    wdbd = p.dram("la_wdbd", [E, E], BF16)
    for src, dst in ((w_cT, wcbd), (w_dT, wdbd)):
        for k in range(NE):
            p.dma("qsp", cvf[:, :], src.t[k * 128:(k + 1) * 128, :], rd=[src], wr=[cvf])
            p.op("act", lambda a: a.copy(cvb[:, :], cvf[:, :]), rd=[cvf], wr=[cvb])
            p.dma("qpool", dst.t[k * 128:(k + 1) * 128, :], cvb[:, :], rd=[cvb], wr=[dst])

    ygd = p.dram("la_ygd", [E, T], BF16)
    SW = 512
    NS = T // SW
    xcs = p.sbuf("la_xcs", [128, NE * SW], BF16)
    sgs = p.sbuf("la_sgs", [128, NE * SW], BF16)
    wcp = p.sbuf("la_wcp", [128, NE * 128], BF16)
    wdp = p.sbuf("la_wdp", [128, NE * 128], BF16)
    sig = p.sbuf("la_sig", [128, SW], F32)
    uu = p.sbuf("la_uu", [128, SW], F32)
    yy = p.sbuf("la_yy", [128, SW], F32)
    ygm = p.sbuf("la_ygm", [128, SW], BF16)
    stcol = p.sbuf("la_stcol", [128, NE], F32)
    p.op("dve", lambda v: v.memset(stcol[:, :], 0.0), wr=[stcol])
    adrep = p.sbuf("la_adrep", [128, SW], F32)
    onesw = p.sbuf("la_onesw", [128, SW], F32)
    p.op("dve", lambda v: v.memset(onesw[:, :], 1.0), wr=[onesw])
    for si in range(NS):
        t0 = si * SW
        p.dma("qsp", xcs.t[:].rearrange("p (c n) -> p c n", c=NE),
              xcd.t[:, t0:t0 + SW].rearrange("(c p) n -> p c n", p=128),
              rd=[xcd], wr=[xcs])
        p.dma("qact", sgs.t[:].rearrange("p (c n) -> p c n", c=NE),
              sgd.t[:, t0:t0 + SW].rearrange("(c p) n -> p c n", p=128),
              rd=[sgd], wr=[sgs])
        for m in range(NE):
            p.dma("qsp", wcp.t[:].rearrange("p (c n) -> p c n", c=NE),
                  wcbd.t[:, m * 128:(m + 1) * 128].rearrange("(c p) n -> p c n", p=128),
                  rd=[wcbd], wr=[wcp])
            p.dma("qact", wdp.t[:].rearrange("p (c n) -> p c n", c=NE),
                  wdbd.t[:, m * 128:(m + 1) * 128].rearrange("(c p) n -> p c n", p=128),
                  rd=[wdbd], wr=[wdp])
            psC = p.psum()
            for k in range(NE):
                p.op("pe", lambda t, k=k, ps=psC: t.matmul(
                    ps[:, 0:SW], wcp[:, k * 128:(k + 1) * 128], xcs[:, k * SW:(k + 1) * SW],
                    start=(k == 0), stop=(k == NE - 1)), rd=[wcp, xcs], wr=[psC], tag="mmC")
            psD = p.psum()
            for k in range(NE):
                p.op("pe", lambda t, k=k, ps=psD: t.matmul(
                    ps[:, 0:SW], wdp[:, k * 128:(k + 1) * 128], xcs[:, k * SW:(k + 1) * SW],
                    start=(k == 0), stop=(k == NE - 1)), rd=[wdp, xcs], wr=[psD], tag="mmD")
            p.op("act", lambda a, ps=psD, m=m: a.activation(
                sig[:, :], ps[:, 0:SW], AF.Sigmoid, bias=bdc.t[:, m:m + 1]),
                 rd=[psD, bdc], wr=[sig])
            p.op("dve", lambda v, ps=psC, m=m: v.tensor_scalar(
                uu[:, :], ps[:, 0:SW], bcc.t[:, m:m + 1], None, ALU.add, ALU.bypass),
                 rd=[psC, bcc], wr=[uu])
            p.op("dve", lambda v: v.tensor_tensor(uu[:, :], uu[:, :], sig[:, :], ALU.mult),
                 rd=[uu, sig], wr=[uu])
            p.op("dve", lambda v, m=m: v.tensor_scalar(
                adrep[:, :], onesw[:, :], adc.t[:, m:m + 1], None, ALU.mult, ALU.bypass),
                 rd=[onesw, adc], wr=[adrep])
            if si == 0:
                p.op("dve", lambda v, m=m: v.tensor_tensor_scan(
                    yy[:, :], adrep[:, :], uu[:, :], 0.0,
                    ALU.mult, ALU.add), rd=[adrep, uu], wr=[yy], tag="scan")
            else:
                p.op("dve", lambda v, m=m: v.tensor_tensor_scan(
                    yy[:, :], adrep[:, :], uu[:, :],
                    stcol.t[:, m:m + 1], ALU.mult, ALU.add),
                     rd=[adrep, uu, stcol], wr=[yy], tag="scan")
            p.op("dve", lambda v, m=m: v.tensor_copy(stcol.t[:, m:m + 1], yy[:, SW - 1:SW]),
                 rd=[yy], wr=[stcol])
            p.op("dve", lambda v, m=m: v.tensor_scalar(
                yy[:, :], yy[:, :], cbsum.t[:, m:m + 1], None, ALU.mult, ALU.bypass),
                 rd=[yy, cbsum], wr=[yy])
            p.op("dve", lambda v, m=m: v.tensor_tensor(
                ygm[:, :], yy[:, :], sgs[:, m * SW:(m + 1) * SW],
                ALU.mult), rd=[yy, sgs], wr=[ygm])
            p.dma("qpool", ygd.t[m * 128:(m + 1) * 128, t0:t0 + SW], ygm[:, :],
                  rd=[ygm], wr=[ygd])

    wob = x2f
    for k in range(NE):
        p.dma("qsp", cvf[:, 0:1024], w_outT.t[k * 128:(k + 1) * 128, :], rd=[w_outT], wr=[cvf])
        p.op("act", lambda a, k=k: a.copy(wob.t[:, k * 1024:(k + 1) * 1024], cvf[:, 0:1024]),
             rd=[cvf], wr=[wob])
    ygt = p.sbuf("la_ygt", [128, NE * 128], BF16)
    xt3 = Buf("la_xt3v", xbc.t, root=xbc)
    xlr = xt3
    class _V:
        pass
    for i in range(NT):
        p.dma("qsp", ygt.t[:].rearrange("p (c n) -> p c n", c=NE),
              ygd.t[:, i * 128:(i + 1) * 128].rearrange("(c p) n -> p c n", p=128),
              rd=[ygd], wr=[ygt])
        p.dma("qact", xt3.t[:, 0:D], x.t[i * 128:(i + 1) * 128, :], rd=[x], wr=[xt3])
        p.op("dve", lambda v, i=i: v.tensor_scalar(
            xt3.t[:, D:2 * D], xt3.t[:, 0:D], statb.t[:, i:i + 1],
            statb.t[:, NT + i:NT + i + 1],
            ALU.subtract, ALU.mult), rd=[xt3, statb], wr=[xlr])
        p.op("dve", lambda v: v.tensor_tensor(
            xt3.t[:, 0:D], xt3.t[:, 0:D], xt3.t[:, D:2 * D], ALU.add),
             rd=[xt3, xlr], wr=[xt3])
        for c in range(2):
            ps = p.psum()
            for k in range(NE):
                p.op("pe", lambda t, k=k, c=c, ps=ps: t.matmul(
                    ps[:, 0:512], ygt[:, k * 128:(k + 1) * 128],
                    wob[:, k * 1024 + c * 512: k * 1024 + (c + 1) * 512],
                    start=(k == 0), stop=(k == NE - 1)), rd=[ygt, wob], wr=[ps], tag="outproj")
            p.op("dve", lambda v, ps=ps, c=c: v.tensor_tensor(
                xt3.t[:, D + c * 512:D + (c + 1) * 512], ps[:, 0:512],
                borow[:, c * 512:(c + 1) * 512],
                ALU.add), rd=[ps, borow], wr=[xlr])
            p.op("dve", lambda v, c=c: v.tensor_tensor(
                xt3.t[:, c * 512:(c + 1) * 512], xt3.t[:, c * 512:(c + 1) * 512],
                xt3.t[:, D + c * 512:D + (c + 1) * 512], ALU.add), rd=[xlr, xt3], wr=[xt3])
        p.dma("qpool", xout.t[i * 128:(i + 1) * 128, :], xt3.t[:, 0:D], rd=[xt3], wr=[xout])


def build_l1attna(nc):
    p = Prog(nc)
    x = Buf("x", nc.dram_tensor("x", [T, D], F32, kind="ExternalInput").ap())
    names = dict(lng=[D], lnb=[D], w_inT=[D, 4096], b_in=[4096], convw=[2048, 4],
                 convb=[2048], A=[2048, 16], Bm=[2048, 16], w_cT=[2048, 2048],
                 b_c=[2048], w_dT=[2048, 2048], b_d=[2048], log_delta=[2048],
                 w_outT=[2048, D], b_out=[D])
    bufs = {k: Buf(k, nc.dram_tensor(k, s, F32, kind="ExternalInput").ap())
            for k, s in names.items()}
    xout = Buf("xout", nc.dram_tensor("xout", [T, D], F32, kind="ExternalOutput").ap())
    p.make_psum_pool(6, 2)
    emit_l1attna(p, x, xout, **bufs)
    p.emit()
    return p



def emit_mamba(p, x, xout, in_proj_w, conv_w, conv_b, x_proj_w, dt_proj_w,
               dt_proj_b, A_log, D_param, out_proj_w):
    E = 2048
    NE = 16
    NSTATE = 16
    xlnf = p.sbuf("mb_xlnf", [128, 8 * T], BF16, is_transpose_dest=True)
    ln_into_fm(p, x, xlnf, pool_n=2)

    cbc = p.sbuf("mb_cbc", [128, NE], F32)
    btc = p.sbuf("mb_btc", [128, NE], F32)
    Dc = p.sbuf("mb_Dc", [128, NE], F32)
    for c in range(NE):
        col_load(p, "qsp", cbc.t[:, c:c + 1], conv_b.t[c * 128:(c + 1) * 128], [conv_b], [cbc])
        col_load(p, "qsp", btc.t[:, c:c + 1], dt_proj_b.t[c * 128:(c + 1) * 128],
                 [dt_proj_b], [btc])
        col_load(p, "qsp", Dc.t[:, c:c + 1], D_param.t[c * 128:(c + 1) * 128], [D_param], [Dc])
    cwc = p.sbuf("mb_cwc", [128, NE * 4], F32)
    for c in range(NE):
        p.dma("qsp", cwc.t[:, c * 4:(c + 1) * 4], conv_w.t[c * 128:(c + 1) * 128, :],
              rd=[conv_w], wr=[cwc])
    af = p.sbuf("mb_af", [128, NE * NSTATE], F32)
    for c in range(NE):
        p.dma("qact", af.t[:, c * 16:(c + 1) * 16], A_log.t[c * 128:(c + 1) * 128, :],
              rd=[A_log], wr=[af])
    p.op("act", lambda a: a.activation(af[:, :], af[:, :], AF.Exp), rd=[af], wr=[af])
    p.op("dve", lambda v: v.tensor_scalar(af[:, :], af[:, :], -1.0, None, ALU.mult, ALU.bypass),
         rd=[af], wr=[af])
    xpwb = p.sbuf("mb_xpwb", [128, NE * 96], BF16)
    tmpl = p.sbuf("mb_tmpl", [128, 2048], F32)
    for c in range(NE):
        p.dma("qsp", tmpl[:, 0:96], x_proj_w.t[c * 128:(c + 1) * 128, :], rd=[x_proj_w],
              wr=[tmpl])
        p.op("act", lambda a, c=c: a.copy(xpwb.t[:, c * 96:(c + 1) * 96], tmpl[:, 0:96]),
             rd=[tmpl], wr=[xpwb])
    dtwf = p.sbuf("mb_dtwf", [64, E], F32)
    p.dma("qsp", dtwf[:, :], dt_proj_w.t[:, :], rd=[dt_proj_w], wr=[dtwf])

    xcd = p.dram("mb_xcd", [E, T], F32)
    szd = p.dram("mb_szd", [E, T], F32)
    ygd2 = p.dram("mb_ygd", [E, T], BF16)
    panf = p.sbuf("mb_panf", [128, 1024], F32)
    panb = p.sbuf("mb_panb", [128, 1024], BF16)
    xbc = p.sbuf("mb_xbc", [128, T], F32)
    acc = p.sbuf("mb_acc", [128, T], F32)
    xcc = p.sbuf("mb_xcc", [128, T], BF16)
    out96 = p.sbuf("mb_out96", [96, T], F32)
    for ec in range(32):
        p.dma("qact", panf.t[:].rearrange("p (c n) -> p c n", c=8),
              in_proj_w.t[:, ec * 128:(ec + 1) * 128].rearrange("(c p) n -> p c n", p=128),
              rd=[in_proj_w], wr=[panf])
        p.op("act", lambda a: a.copy(panb[:, :], panf[:, :]), rd=[panf], wr=[panb])
        for n in range(8):
            ps = p.psum()
            for k in range(8):
                p.op("pe", lambda t, n=n, k=k, ps=ps: t.matmul(
                    ps[:, 0:512], panb[:, k * 128:(k + 1) * 128],
                    xlnf[:, k * T + n * 512:k * T + (n + 1) * 512],
                    start=(k == 0), stop=(k == 7)), rd=[panb, xlnf], wr=[ps], tag="inproj")
            if ec < 16:
                p.op("dve", lambda v, ps=ps, n=n: v.tensor_copy(
                    xbc.t[:, n * 512:(n + 1) * 512], ps[:, 0:512]), rd=[ps], wr=[xbc])
            else:
                p.op("act", lambda a, ps=ps, n=n: a.activation(
                    xbc.t[:, n * 512:(n + 1) * 512], ps[:, 0:512], AF.Silu),
                     rd=[ps], wr=[xbc], tag="zev")
        if ec < 16:
            p.op("dve", lambda v, ec=ec: v.tensor_scalar(
                acc[:, :], xbc[:, :], cwc.t[:, ec * 4 + 3: ec * 4 + 4], None,
                ALU.mult, ALU.bypass), rd=[xbc, cwc], wr=[acc])
            for j in range(3):
                d = 3 - j
                p.op("dve", lambda v, ec=ec, j=j, d=d: v.scalar_tensor_tensor(
                    acc[:, d:T], xbc[:, 0:T - d], cwc.t[:, ec * 4 + j: ec * 4 + j + 1],
                    acc[:, d:T], ALU.mult, ALU.add), rd=[xbc, cwc, acc], wr=[acc])
            p.op("act", lambda a, ec=ec: a.activation(
                acc[:, :], acc[:, :], AF.Silu, bias=cbc.t[:, ec:ec + 1]),
                 rd=[acc, cbc], wr=[acc])
            p.op("dve", lambda v: v.tensor_copy(xcc[:, :], acc[:, :]), rd=[acc], wr=[xcc])
            p.dma("qpool", xcd.t[ec * 128:(ec + 1) * 128, :], acc[:, :], rd=[acc], wr=[xcd])
            for n in range(8):
                ps = p.psum()
                p.op("pe", lambda t, ec=ec, n=n, ps=ps: t.matmul(
                    ps[0:96, 0:512], xpwb[:, ec * 96:(ec + 1) * 96],
                    xcc[:, n * 512:(n + 1) * 512], start=True, stop=True),
                     rd=[xpwb, xcc], wr=[ps], tag="xproj")
                if ec == 0:
                    p.op("dve", lambda v, ps=ps, n=n: v.tensor_copy(
                        out96.t[:, n * 512:(n + 1) * 512], ps[0:96, 0:512]),
                         rd=[ps], wr=[out96])
                else:
                    p.op("dve", lambda v, ps=ps, n=n: v.tensor_tensor(
                        out96.t[:, n * 512:(n + 1) * 512],
                        out96.t[:, n * 512:(n + 1) * 512], ps[0:96, 0:512], ALU.add),
                         rd=[ps, out96], wr=[out96])
        else:
            p.dma("qpool", szd.t[(ec - 16) * 128:(ec - 15) * 128, :], xbc[:, :],
                  rd=[xbc], wr=[szd])

    bcd = p.dram("mb_bcd", [32, T], F32)
    p.dma("qsp", bcd.t[0:16, :], out96.t[64:80, :], rd=[out96], wr=[bcd])
    p.dma("qsp", bcd.t[16:32, :], out96.t[80:96, :], rd=[out96], wr=[bcd])


    opwb = p.sbuf("mb_opwb", [128, NE * 1024], BF16)
    for k in range(NE):
        p.dma("qsp", tmpl[:, 0:1024], out_proj_w.t[k * 128:(k + 1) * 128, :], rd=[out_proj_w],
              wr=[tmpl])
        p.op("act", lambda a, k=k: a.copy(opwb.t[:, k * 1024:(k + 1) * 1024], tmpl[:, 0:1024]),
             rd=[tmpl], wr=[opwb])

    SW = 512
    NS = T // SW
    xvf = xlnf.t[:].bitcast(F32)
    xcs = Buf("mb_xcs", xlnf.t, root=xlnf)
    szs = Buf("mb_szs", xlnf.t, root=xlnf)
    yfm = p.sbuf("mb_yfm", [128, SW], BF16)
    XC0, SZ0 = 0, NE * SW
    brep = Buf("mb_brep", panf.t, root=panf)
    crep = Buf("mb_crep", panf.t, root=panf)
    dtm = Buf("mb_dtm", tmpl.t, root=tmpl)
    pm = Buf("mb_pm", tmpl.t, root=tmpl)
    daf = Buf("mb_daf", tmpl.t, root=tmpl)
    dbuf = Buf("mb_dbuf", tmpl.t, root=tmpl)
    sp1 = p.sbuf("mb_sp1", [128, SW], F32)
    stf = p.sbuf("mb_stf", [128, SW], F32)
    yacc = p.sbuf("mb_yacc", [128, SW], F32)
    stcols = p.sbuf("mb_stcols", [128, NE * NSTATE], F32)
    p.op("dve", lambda v: v.memset(stcols[:, :], 0.0), wr=[stcols])
    xt4 = Buf("mb_xt4", xbc.t, root=xbc)
    for si in range(NS):
        t0 = si * SW
        p.dma("qsp", xvf[:, XC0:XC0 + NE * SW].rearrange("p (c n) -> p c n", c=NE),
              xcd.t[:, t0:t0 + SW].rearrange("(c p) n -> p c n", p=128), rd=[xcd], wr=[xcs])
        p.dma("qact", xvf[:, SZ0:SZ0 + NE * SW].rearrange("p (c n) -> p c n", c=NE),
              szd.t[:, t0:t0 + SW].rearrange("(c p) n -> p c n", p=128), rd=[szd], wr=[szs])
        for m in range(NE):
            ps = p.psum()
            p.op("pe", lambda t, m=m, t0=t0, ps=ps: t.matmul(
                ps[:, 0:SW], dtwf[:, m * 128:(m + 1) * 128], out96.t[0:64, t0:t0 + SW],
                start=True, stop=True), rd=[dtwf, out96], wr=[ps], tag="dtmm")
            p.op("act", lambda a, ps=ps, m=m: a.activation(
                sp1[:, :], ps[:, 0:SW], AF.Abs, bias=btc.t[:, m:m + 1]),
                 rd=[ps, btc], wr=[sp1])
            p.op("act", lambda a: a.activation(sp1[:, :], sp1[:, :], AF.Exp, scale=-1.0),
                 rd=[sp1], wr=[sp1])
            onec = p.cst(1.0)
            p.op("act", lambda a, onec=onec: a.activation(
                sp1[:, :], sp1[:, :], AF.Ln, bias=onec[:, :]), rd=[sp1, onec], wr=[sp1])
            p.op("act", lambda a, ps=ps, m=m: a.activation(
                dtm.t[:, 0:SW], ps[:, 0:SW], AF.Relu, bias=btc.t[:, m:m + 1]),
                 rd=[ps, btc], wr=[dtm])
            p.op("dve", lambda v: v.tensor_tensor(dtm.t[:, 0:SW], sp1[:, :], dtm.t[:, 0:SW], ALU.add),
                 rd=[sp1, dtm], wr=[dtm])
            p.op("pool", lambda g, m=m: g.tensor_tensor(
                pm.t[:, SW:2 * SW], dtm.t[:, 0:SW], xvf[:, XC0 + m * SW:XC0 + (m + 1) * SW], ALU.mult),
                 rd=[dtm, xcs], wr=[pm])
            for s in range(NSTATE):
                p.dma("qsp", brep.t[:, 0:SW], bcd.t[s, t0:t0 + SW].partition_broadcast(128),
                      rd=[bcd], wr=[brep])
                p.dma("qact", crep.t[:, SW:2 * SW], bcd.t[16 + s, t0:t0 + SW].partition_broadcast(128),
                      rd=[bcd], wr=[crep])
                p.op("act", lambda a, m=m, s=s: a.activation(
                    daf.t[:, 2 * SW:3 * SW], dtm.t[:, 0:SW], AF.Exp, scale=af.t[:, m * 16 + s:m * 16 + s + 1]),
                     rd=[dtm, af], wr=[daf])
                p.op("pool", lambda g: g.tensor_tensor(
                    dbuf.t[:, 3 * SW:4 * SW], pm.t[:, SW:2 * SW], brep.t[:, 0:SW], ALU.mult), rd=[pm, brep], wr=[dbuf])
                if si == 0:
                    p.op("dve", lambda v: v.tensor_tensor_scan(
                        stf[:, :], daf.t[:, 2 * SW:3 * SW], dbuf.t[:, 3 * SW:4 * SW], 0.0, ALU.mult, ALU.add),
                         rd=[daf, dbuf], wr=[stf], tag="scan")
                else:
                    p.op("dve", lambda v, m=m, s=s: v.tensor_tensor_scan(
                        stf[:, :], daf.t[:, 2 * SW:3 * SW], dbuf.t[:, 3 * SW:4 * SW],
                        stcols.t[:, m * 16 + s:m * 16 + s + 1], ALU.mult, ALU.add),
                         rd=[daf, dbuf, stcols], wr=[stf], tag="scan")
                p.op("dve", lambda v, m=m, s=s: v.tensor_copy(
                    stcols.t[:, m * 16 + s:m * 16 + s + 1], stf[:, SW - 1:SW]),
                     rd=[stf], wr=[stcols])
                p.op("pool", lambda g: g.tensor_tensor(
                    stf[:, :], stf[:, :], crep.t[:, SW:2 * SW], ALU.mult), rd=[stf, crep], wr=[stf])
                if s == 0:
                    p.op("dve", lambda v: v.tensor_copy(yacc[:, :], stf[:, :]),
                         rd=[stf], wr=[yacc])
                else:
                    p.op("dve", lambda v: v.tensor_tensor(
                        yacc[:, :], yacc[:, :], stf[:, :], ALU.add), rd=[yacc, stf], wr=[yacc])
            p.op("dve", lambda v, m=m: v.scalar_tensor_tensor(
                yacc[:, :], xvf[:, XC0 + m * SW:XC0 + (m + 1) * SW], Dc.t[:, m:m + 1],
                yacc[:, :], ALU.mult, ALU.add), rd=[xcs, Dc, yacc], wr=[yacc])
            p.op("dve", lambda v, m=m: v.tensor_tensor(
                yfm[:, :], yacc[:, :],
                xvf[:, SZ0 + m * SW:SZ0 + (m + 1) * SW], ALU.mult),
                 rd=[yacc, szs], wr=[yfm])
            p.dma("qpool", ygd2.t[m * 128:(m + 1) * 128, t0:t0 + SW], yfm[:, :],
                  rd=[yfm], wr=[ygd2])
    ygt2 = Buf("mb_ygt", xcc.t, root=xcc)
    for i in range(NT):
        p.dma("qsp", ygt2.t[:, 0:NE * 128].rearrange("p (c n) -> p c n", c=NE),
              ygd2.t[:, i * 128:(i + 1) * 128].rearrange("(c p) n -> p c n", p=128),
              rd=[ygd2], wr=[ygt2])
        p.dma("qact", xt4.t[:, 0:D], x.t[i * 128:(i + 1) * 128, :], rd=[x], wr=[xt4])
        for c in range(2):
            ps = p.psum()
            for k in range(NE):
                p.op("pe", lambda t, k=k, c=c, ps=ps: t.matmul(
                    ps[:, 0:512], ygt2[:, k * 128:(k + 1) * 128],
                    opwb[:, k * 1024 + c * 512: k * 1024 + (c + 1) * 512],
                    start=(k == 0), stop=(k == NE - 1)), rd=[ygt2, opwb], wr=[ps], tag="outproj")
            p.op("dve", lambda v, ps=ps, c=c: v.tensor_tensor(
                xt4.t[:, c * 512:(c + 1) * 512], xt4.t[:, c * 512:(c + 1) * 512],
                ps[:, 0:512], ALU.add), rd=[ps, xt4], wr=[xt4])
        p.dma("qpool", xout.t[i * 128:(i + 1) * 128, :], xt4.t[:, 0:D], rd=[xt4], wr=[xout])


def build_mamba(nc):
    p = Prog(nc)
    x = Buf("x", nc.dram_tensor("x", [T, D], F32, kind="ExternalInput").ap())
    names = dict(in_proj_w=[D, 4096], conv_w=[2048, 4], conv_b=[2048],
                 x_proj_w=[2048, 96], dt_proj_w=[64, 2048], dt_proj_b=[2048],
                 A_log=[2048, 16], D_param=[2048], out_proj_w=[2048, D])
    bufs = {k: Buf(k, nc.dram_tensor(k, s, F32, kind="ExternalInput").ap())
            for k, s in names.items()}
    xout = Buf("xout", nc.dram_tensor("xout", [T, D], F32, kind="ExternalOutput").ap())
    p.make_psum_pool(6, 2)
    emit_mamba(p, x, xout, **bufs)
    p.emit()
    return p



def retnet_tables():
    H, dk, dv, half = 4, 256, 512, 128
    scale = 1.0 / np.sqrt(dk)
    logdec = np.log1p(-np.exp2(-5.0 - np.arange(H, dtype=np.float64)))
    gam = np.exp(logdec)
    inv_freq = 1.0 / 10000.0 ** (np.arange(0, dk, 2, dtype=np.float64) / dk)
    t = np.arange(T, dtype=np.float64)
    ang = inv_freq[:, None] * t[None, :]
    cosT = np.cos(ang).astype(np.float32)
    sinT = np.sin(ang).astype(np.float32)
    i_ = np.arange(128, dtype=np.float64)
    m0 = np.zeros((128, 4 * 128), np.float64)
    grow = np.zeros((128, 4 * 128), np.float64)
    kdec = np.zeros((128, 4), np.float64)
    for h in range(H):
        diff = i_[None, :] - i_[:, None]
        m0[:, h * 128:(h + 1) * 128] = np.where(diff >= 0, scale * gam[h] ** diff, 0.0)
        grow[:, h * 128:(h + 1) * 128] = scale * gam[h] ** (i_[None, :] + 1.0)
        kdec[:, h] = gam[h] ** (127.0 - i_)
    g128 = gam ** 128.0
    return (cosT, sinT, m0.astype(np.float32), grow.astype(np.float32),
            kdec.astype(np.float32), [float(v) for v in g128])


def emit_retnet(p, x, xout, wqT, wkT, wvT, wgT, woT, gnw, cosT, sinT, m0, grow, kdec,
                g128, ident):
    H, dv = 4, 512
    xlnf = p.sbuf("rn_xlnf", [128, 8 * T], BF16, is_transpose_dest=True)
    ln_into_fm(p, x, xlnf, pool_n=2)
    cosb = p.sbuf("rn_cosb", [128, 512], F32)
    sinb = p.sbuf("rn_sinb", [128, 512], F32)
    m0b = p.sbuf("rn_m0b", [128, 512], F32)
    grb = p.sbuf("rn_grb", [128, 512], F32)
    kdb = p.sbuf("rn_kdb", [128, 4], F32)
    gnr = p.sbuf("rn_gnr", [128, dv], F32)
    p.dma("qsp", m0b[:, :], m0.t[:, :], rd=[m0], wr=[m0b])
    p.dma("qsp", grb[:, :], grow.t[:, :], rd=[grow], wr=[grb])
    p.dma("qsp", kdb[:, :], kdec.t[:, :], rd=[kdec], wr=[kdb])
    p.dma("qsp", gnr[:, :], gnw.t[:].partition_broadcast(128), rd=[gnw], wr=[gnr])
    idb = p.sbuf("rn_idb", [128, 128], BF16)
    idf = p.sbuf("rn_idf", [128, 128], F32)
    p.dma("qsp", idf[:, :], ident.t[:, :], rd=[ident], wr=[idf])
    p.op("act", lambda a: a.copy(idb[:, :], idf[:, :]), rd=[idf], wr=[idb])

    import os as _os
    _dbg = _os.environ.get("RN_DBG", "0") == "1"
    if _dbg:
        ogd = Buf("rn_ogd", p.nc.dram_tensor("rn_ogd_d", [2048, T], BF16,
                                             kind="ExternalOutput").ap())
    else:
        ogd = p.dram("rn_ogd", [2048, T], BF16)
    if _dbg:
        qkdump = Buf("qkdump", p.nc.dram_tensor("qkdump", [8, 128, 2 * T], BF16,
                                                kind="ExternalOutput").ap())
        emit_retnet.qkdump = qkdump
        emit_retnet.ogd = ogd
    raw = p.sbuf("rn_raw", [128, 2 * T], BF16)
    qro = p.sbuf("rn_qro", [128, 2 * T], BF16)
    kro = p.sbuf("rn_kro", [128, 2 * T], BF16)
    panq = p.sbuf("rn_panq", [128, 8 * 256], BF16)
    panvg = p.sbuf("rn_panvg", [128, 8 * 512], BF16)
    pangg = p.sbuf("rn_pangg", [128, 8 * 512], BF16)
    panf = p.sbuf("rn_panf", [128, 1024], F32)
    panfA = Buf("rn_panfA", panf.t, root=panf)
    panfB = Buf("rn_panfB", panf.t, root=panf)
    Sf = p.sbuf("rn_Sf", [128, 2 * 512], F32)
    Sb = p.sbuf("rn_Sb", [128, 2 * 512], BF16)
    vt = p.sbuf("rn_vt", [128, 512], BF16)
    vdec = p.sbuf("rn_vdec", [128, 512], BF16)
    mk = p.sbuf("rn_mk", [128, 128], BF16)
    qt = p.sbuf("rn_qt", [128, 256], BF16)
    gs = p.sbuf("rn_gs", [128, 512], F32)
    sq = p.sbuf("rn_sq", [128, 512], F32)
    o1 = p.sbuf("rn_o1", [128, 512], F32)
    msc = p.sbuf("rn_msc", [128, 1], F32)
    rms = p.sbuf("rn_rms", [128, 1], F32)
    ogt = p.sbuf("rn_ogt", [128, 512], BF16)
    ogfc = p.sbuf("rn_ogfc", [128, 512], BF16, is_transpose_dest=True)
    ktm = p.sbuf("rn_ktm", [128, 256], BF16, is_transpose_dest=True)
    epsc = p.cst(1e-5)

    def load_panel(dst, wsrc, col0, ncols, q="qact"):
        for k in range(8):
            st = panfA if k % 2 == 0 else panfB
            off = (k % 2) * 512
            p.dma(q, st.t[:, off:off + ncols],
                  wsrc.t[k * 128:(k + 1) * 128, col0:col0 + ncols], rd=[wsrc], wr=[st])
            p.op("act", lambda a, k=k, ncols=ncols, st=st, off=off: a.copy(
                dst[:, k * ncols:(k + 1) * ncols], st.t[:, off:off + ncols]),
                 rd=[st], wr=[dst])

    for h in range(H):
        for which, dst in (("q", qro), ("k", kro)):
            wsrc = wqT if which == "q" else wkT
            load_panel(panq, wsrc, h * 256, 256)
            for m in range(2):
                for n in range(8):
                    ps = p.psum()
                    for k in range(8):
                        p.op("pe", lambda t, m=m, n=n, k=k, ps=ps: t.matmul(
                            ps[:, 0:512], panq[:, k * 256 + m * 128: k * 256 + m * 128 + 128],
                            xlnf[:, k * T + n * 512: k * T + (n + 1) * 512],
                            start=(k == 0), stop=(k == 7)), rd=[panq, xlnf], wr=[ps], tag="qk")
                    p.op("dve", lambda v, ps=ps, m=m, n=n: v.tensor_copy(
                        raw.t[:, m * T + n * 512: m * T + (n + 1) * 512], ps[:, 0:512]),
                         rd=[ps], wr=[raw])
            for n in range(8):
                sl = slice(n * 512, (n + 1) * 512)
                p.dma("qsp", cosb[:, :], cosT.t[:, n * 512:(n + 1) * 512], rd=[cosT], wr=[cosb])
                p.dma("qact", sinb[:, :], sinT.t[:, n * 512:(n + 1) * 512], rd=[sinT], wr=[sinb])
                r1 = raw.t[:, n * 512:(n + 1) * 512]
                r2 = raw.t[:, T + n * 512: T + (n + 1) * 512]
                p.op("dve", lambda v, r1=r1: v.tensor_tensor(
                    o1[:, 0:512], r1, cosb.t[:, :], ALU.mult), rd=[raw, cosb], wr=[o1])
                p.op("pool", lambda g, r2=r2: g.tensor_tensor(
                    sq[:, 0:512], r2, sinb.t[:, :], ALU.mult), rd=[raw, sinb], wr=[sq])
                p.op("dve", lambda v, sl=sl, dst=dst: v.tensor_tensor(
                    dst.t[:, sl], o1[:, 0:512], sq[:, 0:512], ALU.subtract),
                     rd=[o1, sq], wr=[dst])
                p.op("dve", lambda v, r1=r1: v.tensor_tensor(
                    o1[:, 0:512], r1, sinb.t[:, :], ALU.mult), rd=[raw, sinb], wr=[o1])
                p.op("pool", lambda g, r2=r2: g.tensor_tensor(
                    sq[:, 0:512], r2, cosb.t[:, :], ALU.mult), rd=[raw, cosb], wr=[sq])
                p.op("dve", lambda v, n=n, dst=dst: v.tensor_tensor(
                    dst.t[:, T + n * 512: T + (n + 1) * 512], o1[:, 0:512], sq[:, 0:512],
                    ALU.add), rd=[o1, sq], wr=[dst])
        if _dbg:
            p.dma("qpool", qkdump.t[2 * h], qro.t[:, :], rd=[qro], wr=[qkdump])
            p.dma("qpool", qkdump.t[2 * h + 1], kro.t[:, :], rd=[kro], wr=[qkdump])
        load_panel(panvg, wvT, h * 512, 512, q="qsp")
        load_panel(pangg, wgT, h * 512, 512, q="qact")
        p.op("dve", lambda v: v.memset(Sf[:, :], 0.0), wr=[Sf])
        p.op("dve", lambda v: v.memset(Sb[:, :], 0.0), wr=[Sb])
        for c in range(NT):
            csl = slice(c * 128, (c + 1) * 128)
            psV = p.psum()
            for k in range(8):
                p.op("pe", lambda t, c=c, k=k, ps=psV: t.matmul(
                    ps[:, 0:512], xlnf[:, k * T + c * 128: k * T + (c + 1) * 128],
                    panvg[:, k * 512:(k + 1) * 512], start=(k == 0), stop=(k == 7)),
                     rd=[xlnf, panvg], wr=[psV], tag="vproj")
            p.op("dve", lambda v, ps=psV: v.tensor_copy(vt[:, :], ps[:, 0:512]),
                 rd=[psV], wr=[vt])
            p.op("dve", lambda v, ps=psV, h=h: v.tensor_scalar(
                vdec[:, :], ps[:, 0:512], kdb.t[:, h:h + 1], None, ALU.mult, ALU.bypass),
                 rd=[psV, kdb], wr=[vdec])
            psS = p.psum()
            for kc in range(2):
                p.op("pe", lambda t, kc=kc, c=c, ps=psS: t.matmul(
                    ps[:, 0:128], kro[:, kc * T + c * 128: kc * T + (c + 1) * 128],
                    qro[:, kc * T + c * 128: kc * T + (c + 1) * 128],
                    start=(kc == 0), stop=(kc == 1)), rd=[kro, qro], wr=[psS], tag="scores")
            p.op("dve", lambda v, ps=psS, h=h: v.tensor_tensor(
                mk[:, :], ps[:, 0:128], m0b.t[:, h * 128:(h + 1) * 128], ALU.mult),
                 rd=[psS, m0b], wr=[mk])
            for kc in range(2):
                p.op("dve", lambda v, kc=kc, c=c, h=h: v.tensor_tensor(
                    qt.t[:, kc * 128:(kc + 1) * 128],
                    qro.t[:, kc * T + c * 128: kc * T + (c + 1) * 128],
                    grb.t[:, h * 128:(h + 1) * 128], ALU.mult), rd=[qro, grb], wr=[qt])
            psO = p.psum()
            for kc in range(2):
                p.op("pe", lambda t, kc=kc, ps=psO: t.matmul(
                    ps[:, 0:512], qt[:, kc * 128:(kc + 1) * 128], Sb[:, kc * 512:(kc + 1) * 512],
                    start=(kc == 0), stop=False), rd=[qt, Sb], wr=[psO], tag="ointer")
            p.op("pe", lambda t, ps=psO: t.matmul(
                ps[:, 0:512], mk[:, :], vt[:, :], start=False, stop=True),
                 rd=[mk, vt], wr=[psO], tag="ointra")
            p.op("act", lambda a, ps=psO: a.activation(
                sq[:, 0:512], ps[:, 0:512], AF.Square, accum_out=msc[:, :]),
                 rd=[psO], wr=[sq, msc])
            p.op("act", lambda a: a.copy(msc[:, :], msc[:, :]), rd=[msc], wr=[msc])
            p.op("dve", lambda v: v.tensor_scalar(
                rms[:, :], msc[:, :], 1.0 / dv, None, ALU.mult, ALU.bypass),
                 rd=[msc], wr=[rms])
            p.op("act", lambda a: a.activation(rms[:, :], rms[:, :], AF.Sqrt, bias=epsc[:, :]),
                 rd=[rms, epsc], wr=[rms])
            p.op("dve", lambda v: v.reciprocal(rms[:, :], rms[:, :]), rd=[rms], wr=[rms])
            psG = p.psum()
            for k in range(8):
                p.op("pe", lambda t, c=c, k=k, ps=psG: t.matmul(
                    ps[:, 0:512], xlnf[:, k * T + c * 128: k * T + (c + 1) * 128],
                    pangg[:, k * 512:(k + 1) * 512], start=(k == 0), stop=(k == 7)),
                     rd=[xlnf, pangg], wr=[psG], tag="gproj")
            p.op("act", lambda a, ps=psG: a.activation(gs[:, 0:512], ps[:, 0:512], AF.Silu),
                 rd=[psG], wr=[gs])
            p.op("dve", lambda v, ps=psO: v.scalar_tensor_tensor(
                o1[:, 0:512], ps[:, 0:512], rms[:, :], gnr[:, :], ALU.mult, ALU.mult),
                 rd=[psO, rms, gnr], wr=[o1])
            import os as _os2
            if "ogtcopy" in _os2.environ.get("RN_ABLATE", ""):
                p.op("dve", lambda v: v.tensor_copy(ogt[:, :], vt[:, :]), rd=[vt], wr=[ogt])
            else:
                p.op("dve", lambda v: v.tensor_tensor(ogt[:, :], o1[:, 0:512], gs[:, 0:512],
                                                      ALU.mult), rd=[o1, gs], wr=[ogt])
            for j in range(4):
                pb = p.psum_bf()
                p.op("pe", lambda t, j=j, pb=pb: t.transpose(
                    pb[:, 0:128], ogt[:, j * 128:(j + 1) * 128], idb[:, :]),
                     rd=[ogt, idb], wr=[pb], tag="ogtr")
                p.op("dve", lambda v, j=j, pb=pb: v.tensor_copy(
                    ogfc.t[:, j * 128:(j + 1) * 128], pb[:, 0:128]), rd=[pb], wr=[ogfc])
            for j in range(4):
                p.dma("qpool", ogd.t[h * 512 + j * 128: h * 512 + (j + 1) * 128, csl],
                      ogfc.t[:, j * 128:(j + 1) * 128], rd=[ogfc], wr=[ogd])
            for kc in range(2):
                pb = p.psum_bf()
                p.op("pe", lambda t, kc=kc, c=c, pb=pb: t.transpose(
                    pb[:, 0:128], kro.t[:, kc * T + c * 128: kc * T + (c + 1) * 128],
                    idb[:, :]), rd=[kro, idb], wr=[pb], tag="ktr")
                p.op("dve", lambda v, kc=kc, pb=pb: v.tensor_copy(
                    ktm.t[:, kc * 128:(kc + 1) * 128], pb[:, 0:128]), rd=[pb], wr=[ktm])
            for kc in range(2):
                psU = p.psum()
                p.op("pe", lambda t, kc=kc, ps=psU: t.matmul(
                    ps[:, 0:512], ktm[:, kc * 128:(kc + 1) * 128], vdec[:, :],
                    start=True, stop=True), rd=[ktm, vdec], wr=[psU], tag="supd")
                p.op("dve", lambda v, kc=kc, h=h: v.tensor_scalar(
                    Sf.t[:, kc * 512:(kc + 1) * 512], Sf.t[:, kc * 512:(kc + 1) * 512],
                    g128[h], None, ALU.mult, ALU.bypass), rd=[Sf], wr=[Sf])
                p.op("dve", lambda v, kc=kc, ps=psU: v.tensor_tensor(
                    Sf.t[:, kc * 512:(kc + 1) * 512], Sf.t[:, kc * 512:(kc + 1) * 512],
                    ps[:, 0:512], ALU.add), rd=[Sf, psU], wr=[Sf])
                p.op("act", lambda a, kc=kc: a.copy(
                    Sb.t[:, kc * 512:(kc + 1) * 512], Sf.t[:, kc * 512:(kc + 1) * 512]),
                     rd=[Sf], wr=[Sb])

    wob = Buf("rn_wob", xlnf.t, root=xlnf)
    for k in range(16):
        p.dma("qsp", panf[:, 0:1024], woT.t[k * 128:(k + 1) * 128, :], rd=[woT], wr=[panf])
        p.op("act", lambda a, k=k: a.copy(wob.t[:, k * 1024:(k + 1) * 1024], panf[:, 0:1024]),
             rd=[panf], wr=[wob])
    ogt2 = Buf("rn_ogt2", raw.t, root=raw)
    xt5 = p.sbuf("rn_xt5", [128, D], F32)
    for i in range(NT):
        p.dma("qsp", ogt2.t[:, 0:2048].rearrange("p (c n) -> p c n", c=16),
              ogd.t[:, i * 128:(i + 1) * 128].rearrange("(c p) n -> p c n", p=128),
              rd=[ogd], wr=[ogt2])
        p.dma("qact", xt5[:, :], x.t[i * 128:(i + 1) * 128, :], rd=[x], wr=[xt5])
        for cc in range(2):
            ps = p.psum()
            for k in range(16):
                p.op("pe", lambda t, k=k, cc=cc, ps=ps: t.matmul(
                    ps[:, 0:512], ogt2.t[:, k * 128:(k + 1) * 128],
                    wob.t[:, k * 1024 + cc * 512: k * 1024 + (cc + 1) * 512],
                    start=(k == 0), stop=(k == 15)), rd=[ogt2, wob], wr=[ps], tag="wo")
            p.op("dve", lambda v, ps=ps, cc=cc: v.tensor_tensor(
                xt5[:, cc * 512:(cc + 1) * 512], xt5[:, cc * 512:(cc + 1) * 512],
                ps[:, 0:512], ALU.add), rd=[ps, xt5], wr=[xt5])
        p.dma("qpool", xout.t[i * 128:(i + 1) * 128, :], xt5[:, :], rd=[xt5], wr=[xout])


def build_retnet(nc):
    p = Prog(nc)
    x = Buf("x", nc.dram_tensor("x", [T, D], F32, kind="ExternalInput").ap())
    wqT = Buf("wqT", nc.dram_tensor("wqT", [D, 1024], F32, kind="ExternalInput").ap())
    wkT = Buf("wkT", nc.dram_tensor("wkT", [D, 1024], F32, kind="ExternalInput").ap())
    wvT = Buf("wvT", nc.dram_tensor("wvT", [D, 2048], F32, kind="ExternalInput").ap())
    wgT = Buf("wgT", nc.dram_tensor("wgT", [D, 2048], F32, kind="ExternalInput").ap())
    woT = Buf("woT", nc.dram_tensor("woT", [2048, 1024], F32, kind="ExternalInput").ap())
    gnw = Buf("gnw", nc.dram_tensor("gnw", [512], F32, kind="ExternalInput").ap())
    cosT = Buf("cosT", nc.dram_tensor("cosT", [128, T], F32, kind="ExternalInput").ap())
    sinT = Buf("sinT", nc.dram_tensor("sinT", [128, T], F32, kind="ExternalInput").ap())
    m0 = Buf("m0", nc.dram_tensor("m0", [128, 512], F32, kind="ExternalInput").ap())
    grow = Buf("grow", nc.dram_tensor("grow", [128, 512], F32, kind="ExternalInput").ap())
    kdec = Buf("kdec", nc.dram_tensor("kdec", [128, 4], F32, kind="ExternalInput").ap())
    ident = Buf("ident", nc.dram_tensor("ident", [128, 128], F32, kind="ExternalInput").ap())
    xout = Buf("xout", nc.dram_tensor("xout", [T, D], F32, kind="ExternalOutput").ap())
    g128 = retnet_tables()[5]
    p.make_psum_pool(6, 2)
    emit_retnet(p, x, xout, wqT, wkT, wvT, wgT, woT, gnw, cosT, sinT, m0, grow, kdec, g128,
                ident)
    p.emit()
    return p



def _tr(a):
    return np.ascontiguousarray(np.asarray(a).T)


def kernel(**ins):
    return kernel_fused(**ins)


def kernel_multilaunch(**ins):
    xs = [np.ascontiguousarray(np.asarray(ins["L0_attn_a_x"])[b]) for b in range(N_CORES)]
    cores = list(range(N_CORES))

    cosT, sinT, m0, grow, kdec, g128 = retnet_tables()

    nc = bass.Bass("TRN2", target_bir_lowering=False, debug=False)
    build_retnet(nc)
    wmap = {"wqT": _tr(ins["L0_attn_a_Wq"]), "wkT": _tr(ins["L0_attn_a_Wk"]),
            "wvT": _tr(ins["L0_attn_a_Wv"]), "wgT": _tr(ins["L0_attn_a_Wg"]),
            "woT": _tr(ins["L0_attn_a_Wo"]), "gnw": np.asarray(ins["L0_attn_a_g_norm_w"]),
            "cosT": cosT, "sinT": sinT, "m0": m0, "grow": grow, "kdec": kdec,
            "ident": np.eye(128, dtype=np.float32)}
    res = run_bass_kernel_spmd(nc, [dict(x=xs[b], **wmap) for b in cores], core_ids=cores)
    xs = [res.results[b]["xout"] for b in cores]

    nc = bass.Bass("TRN2", target_bir_lowering=False, debug=False)
    build_moe(nc)
    wmap = {"wr": np.asarray(ins["L0_ffn_Wr"]), "w1": np.asarray(ins["L0_ffn_W1"]),
            "w2": np.asarray(ins["L0_ffn_W2"]), "w3": np.asarray(ins["L0_ffn_W3"])}
    res = run_bass_kernel_spmd(nc, [dict(x=xs[b], **wmap) for b in cores], core_ids=cores)
    xs = [res.results[b]["xout"] for b in cores]

    nc = bass.Bass("TRN2", target_bir_lowering=False, debug=False)
    build_mamba(nc)
    wmap = {"in_proj_w": np.asarray(ins["L0_attn_b_in_proj_w"]),
            "conv_w": np.ascontiguousarray(np.asarray(ins["L0_attn_b_conv_w"]).reshape(2048, 4)),
            "conv_b": np.asarray(ins["L0_attn_b_conv_b"]),
            "x_proj_w": np.asarray(ins["L0_attn_b_x_proj_w"]),
            "dt_proj_w": np.asarray(ins["L0_attn_b_dt_proj_w"]),
            "dt_proj_b": np.asarray(ins["L0_attn_b_dt_proj_b"]),
            "A_log": np.asarray(ins["L0_attn_b_A_log"]),
            "D_param": np.asarray(ins["L0_attn_b_D_param"]),
            "out_proj_w": np.asarray(ins["L0_attn_b_out_proj_w"])}
    res = run_bass_kernel_spmd(nc, [dict(x=xs[b], **wmap) for b in cores], core_ids=cores)
    xs = [res.results[b]["xout"] for b in cores]

    nc = bass.Bass("TRN2", target_bir_lowering=False, debug=False)
    build_moe(nc)
    wmap = {"wr": np.asarray(ins["L0_ffn_Wr"]), "w1": np.asarray(ins["L0_ffn_W1"]),
            "w2": np.asarray(ins["L0_ffn_W2"]), "w3": np.asarray(ins["L0_ffn_W3"])}
    res = run_bass_kernel_spmd(nc, [dict(x=xs[b], **wmap) for b in cores], core_ids=cores)
    xs = [res.results[b]["xout"] for b in cores]

    nc = bass.Bass("TRN2", target_bir_lowering=False, debug=False)
    build_l1attna(nc)
    wmap = {"lng": np.asarray(ins["L1_attn_a_ln_g"]), "lnb": np.asarray(ins["L1_attn_a_ln_b"]),
            "w_inT": _tr(ins["L1_attn_a_W_in"]), "b_in": np.asarray(ins["L1_attn_a_b_in"]),
            "convw": np.ascontiguousarray(np.asarray(ins["L1_attn_a_conv_w"]).reshape(2048, 4)),
            "convb": np.asarray(ins["L1_attn_a_conv_b"]),
            "A": np.asarray(ins["L1_attn_a_A"]), "Bm": np.asarray(ins["L1_attn_a_Bm"]),
            "w_cT": _tr(ins["L1_attn_a_W_c"]), "b_c": np.asarray(ins["L1_attn_a_b_c"]),
            "w_dT": _tr(ins["L1_attn_a_W_d"]), "b_d": np.asarray(ins["L1_attn_a_b_d"]),
            "log_delta": np.asarray(ins["L1_attn_a_log_delta"]),
            "w_outT": _tr(ins["L1_attn_a_W_out"]), "b_out": np.asarray(ins["L1_attn_a_b_out"])}
    res = run_bass_kernel_spmd(nc, [dict(x=xs[b], **wmap) for b in cores], core_ids=cores)
    xs = [res.results[b]["xout"] for b in cores]

    ffn_map = {"k0": np.asarray(ins["L1_ffn_k0"]).view(np.uint8),
               "k1": np.asarray(ins["L1_ffn_k1"]).view(np.uint8),
               "k2": np.asarray(ins["L1_ffn_k2"]).view(np.uint8),
               "s0": np.asarray(ins["L1_ffn_s0"]).reshape(1),
               "s1": np.asarray(ins["L1_ffn_s1"]).reshape(1),
               "s2": np.asarray(ins["L1_ffn_s2"]).reshape(1)}
    nc = bass.Bass("TRN2", target_bir_lowering=False, debug=False)
    build_l1ffn(nc, None)
    res = run_bass_kernel_spmd(nc, [dict(x=xs[b], **ffn_map) for b in cores], core_ids=cores)
    xs = [res.results[b]["xout"] for b in cores]

    nc = bass.Bass("TRN2", target_bir_lowering=False, debug=False)
    build_l1attnb(nc)
    wmap = {"wq": np.asarray(ins["L1_attn_b_Wq"]), "wk": np.asarray(ins["L1_attn_b_Wk"]),
            "wvT": _tr(ins["L1_attn_b_Wv"]), "proj": np.asarray(ins["L1_attn_b_proj"])}
    res = run_bass_kernel_spmd(nc, [dict(x=xs[b], **wmap) for b in cores], core_ids=cores)
    xs = [res.results[b]["xout"] for b in cores]

    nc = bass.Bass("TRN2", target_bir_lowering=False, debug=False)
    build_l1ffn(nc, None)
    res = run_bass_kernel_spmd(nc, [dict(x=xs[b], **ffn_map) for b in cores], core_ids=cores)
    xs = [res.results[b]["xout"] for b in cores]

    return np.stack(xs, axis=0)



def build_fused(nc):
    p = Prog(nc)
    p.make_psum_pool(6, 2)
    p.cst(1e-5)
    p.cst(1.0)
    p.idb16()
    di = lambda n, s, dt=F32: Buf(n, nc.dram_tensor(n, s, dt, kind="ExternalInput").ap())
    x0 = di("x", [T, D])
    xout = Buf("xout", nc.dram_tensor("xout", [T, D], F32, kind="ExternalOutput").ap())
    xc = [x0] + [p.dram(f"xchain{i}", [T, D], F32) for i in range(7)] + [xout]

    rn = dict(wqT=di("wqT", [D, 1024]), wkT=di("wkT", [D, 1024]), wvT=di("rn_wvT", [D, 2048]),
              wgT=di("wgT", [D, 2048]), woT=di("woT", [2048, 1024]), gnw=di("gnw", [512]),
              cosT=di("cosT", [128, T]), sinT=di("sinT", [128, T]), m0=di("m0", [128, 512]),
              grow=di("grow", [128, 512]), kdec=di("kdec", [128, 4]),
              g128=retnet_tables()[5], ident=di("ident", [128, 128]))
    mo = dict(wr=di("wr", [D, 8]), w1=di("w1", [8, D, 1536]), w2=di("w2", [8, 1536, D]),
              w3=di("w3", [8, D, 1536]))
    mb = dict(in_proj_w=di("in_proj_w", [D, 4096]), conv_w=di("mb_conv_w", [2048, 4]),
              conv_b=di("mb_conv_b", [2048]), x_proj_w=di("x_proj_w", [2048, 96]),
              dt_proj_w=di("dt_proj_w", [64, 2048]), dt_proj_b=di("dt_proj_b", [2048]),
              A_log=di("A_log", [2048, 16]), D_param=di("D_param", [2048]),
              out_proj_w=di("out_proj_w", [2048, D]))
    la = dict(lng=di("lng", [D]), lnb=di("lnb", [D]), w_inT=di("w_inT", [D, 4096]),
              b_in=di("b_in", [4096]), convw=di("la_convw", [2048, 4]),
              convb=di("la_convb", [2048]), A=di("A", [2048, 16]), Bm=di("Bm", [2048, 16]),
              w_cT=di("w_cT", [2048, 2048]), b_c=di("b_c", [2048]),
              w_dT=di("w_dT", [2048, 2048]), b_d=di("b_d", [2048]),
              log_delta=di("log_delta", [2048]), w_outT=di("w_outT", [2048, D]),
              b_out=di("b_out", [D]))
    lb = dict(wq=di("lb_wq", [D, D]), wk=di("lb_wk", [D, D]), wvT=di("lb_wvT", [D, D]),
              proj=di("proj", [D, 256]))
    lf = dict(k0=di("k0", [2 * D, 4096], U8), k1=di("k1", [4096, 4096], U8),
              k2=di("k2", [4096, D], U8), s0=di("s0", [1]), s1=di("s1", [1]),
              s2=di("s2", [1]))

    def scoped(fn, *a, **kw):
        p.push_scope()
        fn(*a, **kw)
        p.pop_scope()
        p.barrier()

    scoped(emit_retnet, p, xc[0], xc[1], **rn)
    scoped(emit_moe, p, xc[1], xc[2], **mo)
    scoped(emit_mamba, p, xc[2], xc[3], **mb)
    scoped(emit_moe, p, xc[3], xc[4], **mo)
    scoped(emit_l1attna, p, xc[4], xc[5], **la)
    p.push_scope()
    wds = build_sign_weights(p, lf["k0"], lf["k1"], lf["k2"])
    emit_l1ffn(p, xc[5], xc[6], *wds, lf["s0"], lf["s1"], lf["s2"])
    p.pop_scope()
    p.barrier()
    scoped(emit_l1attnb, p, xc[6], xc[7], lb["wq"], lb["wk"], lb["wvT"], lb["proj"])
    p.push_scope()
    emit_l1ffn(p, xc[7], xc[8], *wds, lf["s0"], lf["s1"], lf["s2"])
    p.pop_scope()
    p.emit()
    return p


def kernel_fused(**ins):
    xs = [np.ascontiguousarray(np.asarray(ins["L0_attn_a_x"])[b]) for b in range(N_CORES)]
    cores = list(range(N_CORES))
    cosT, sinT, m0, grow, kdec, _ = retnet_tables()
    nc = bass.Bass("TRN2", target_bir_lowering=False, debug=False)
    build_fused(nc)
    wmap = {
        "wqT": _tr(ins["L0_attn_a_Wq"]), "wkT": _tr(ins["L0_attn_a_Wk"]),
        "rn_wvT": _tr(ins["L0_attn_a_Wv"]), "wgT": _tr(ins["L0_attn_a_Wg"]),
        "woT": _tr(ins["L0_attn_a_Wo"]), "gnw": np.asarray(ins["L0_attn_a_g_norm_w"]),
        "cosT": cosT, "sinT": sinT, "m0": m0, "grow": grow, "kdec": kdec,
        "ident": np.eye(128, dtype=np.float32),
        "wr": np.asarray(ins["L0_ffn_Wr"]), "w1": np.asarray(ins["L0_ffn_W1"]),
        "w2": np.asarray(ins["L0_ffn_W2"]), "w3": np.asarray(ins["L0_ffn_W3"]),
        "in_proj_w": np.asarray(ins["L0_attn_b_in_proj_w"]),
        "mb_conv_w": np.ascontiguousarray(np.asarray(ins["L0_attn_b_conv_w"]).reshape(2048, 4)),
        "mb_conv_b": np.asarray(ins["L0_attn_b_conv_b"]),
        "x_proj_w": np.asarray(ins["L0_attn_b_x_proj_w"]),
        "dt_proj_w": np.asarray(ins["L0_attn_b_dt_proj_w"]),
        "dt_proj_b": np.asarray(ins["L0_attn_b_dt_proj_b"]),
        "A_log": np.asarray(ins["L0_attn_b_A_log"]),
        "D_param": np.asarray(ins["L0_attn_b_D_param"]),
        "out_proj_w": np.asarray(ins["L0_attn_b_out_proj_w"]),
        "lng": np.asarray(ins["L1_attn_a_ln_g"]), "lnb": np.asarray(ins["L1_attn_a_ln_b"]),
        "w_inT": _tr(ins["L1_attn_a_W_in"]), "b_in": np.asarray(ins["L1_attn_a_b_in"]),
        "la_convw": np.ascontiguousarray(np.asarray(ins["L1_attn_a_conv_w"]).reshape(2048, 4)),
        "la_convb": np.asarray(ins["L1_attn_a_conv_b"]),
        "A": np.asarray(ins["L1_attn_a_A"]), "Bm": np.asarray(ins["L1_attn_a_Bm"]),
        "w_cT": _tr(ins["L1_attn_a_W_c"]), "b_c": np.asarray(ins["L1_attn_a_b_c"]),
        "w_dT": _tr(ins["L1_attn_a_W_d"]), "b_d": np.asarray(ins["L1_attn_a_b_d"]),
        "log_delta": np.asarray(ins["L1_attn_a_log_delta"]),
        "w_outT": _tr(ins["L1_attn_a_W_out"]), "b_out": np.asarray(ins["L1_attn_a_b_out"]),
        "lb_wq": np.asarray(ins["L1_attn_b_Wq"]), "lb_wk": np.asarray(ins["L1_attn_b_Wk"]),
        "lb_wvT": _tr(ins["L1_attn_b_Wv"]), "proj": np.asarray(ins["L1_attn_b_proj"]),
        "k0": np.asarray(ins["L1_ffn_k0"]).view(np.uint8),
        "k1": np.asarray(ins["L1_ffn_k1"]).view(np.uint8),
        "k2": np.asarray(ins["L1_ffn_k2"]).view(np.uint8),
        "s0": np.asarray(ins["L1_ffn_s0"]).reshape(1),
        "s1": np.asarray(ins["L1_ffn_s1"]).reshape(1),
        "s2": np.asarray(ins["L1_ffn_s2"]).reshape(1),
    }
    res = run_bass_kernel_spmd(nc, [dict(x=xs[b], **wmap) for b in cores], core_ids=cores)
    return np.stack([res.results[b]["xout"] for b in cores], axis=0)


# version 04 (final)
# speedup vs baseline: 1.4070x; 1.4070x over previous
from contextlib import ExitStack
import numpy as np
import concourse.bass as bass
import concourse.mybir as mybir
from concourse.bass_utils import run_bass_kernel_spmd

F32 = mybir.dt.float32
BF16 = mybir.dt.bfloat16
U8 = mybir.dt.uint8
AF = mybir.ActivationFunctionType
ALU = mybir.AluOpType
AX = mybir.AxisListType

N_CORES = 8
T = 4096
D = 1024
NT = T // 128


ENG_OF_QUEUE = {"qsp": "sp", "qact": "act", "qpool": "pool", "tsp": "sp", "tact": "act"}


class Buf:
    __slots__ = ("name", "t", "last_w", "readers", "root")

    def __init__(self, name, t, root=None):
        self.name = name
        self.t = t
        self.last_w = None
        self.readers = []
        self.root = root if root is not None else self

    def __getitem__(self, idx):
        return self.t[idx]

    @property
    def ap(self):
        return self.t[:]


class _Op:
    __slots__ = ("stream", "fn", "deps", "needs_inc", "milestone", "tag", "res", "scope")

    def __init__(self, stream, fn, tag):
        self.stream = stream
        self.fn = fn
        self.deps = []
        self.needs_inc = False
        self.milestone = 0
        self.res = 0
        self.tag = tag
        self.scope = None


class Prog:
    def __init__(self, nc):
        self.nc = nc
        self.eng = {
            "pe": nc.tensor, "act": nc.scalar, "dve": nc.vector,
            "pool": nc.gpsimd, "sp": nc.sync,
        }
        self.ops = []
        self._sb_bytes = 0
        self._psum_pool = []
        self._psum_i = 0
        self._name_i = 0
        self._csts = {}
        self._scopes = []
        self._pb = None
        self._pb_done = set()

    def push_scope(self):
        self._scopes.append(ExitStack())

    def pop_scope(self):
        self._scopes.pop().close()

    def set_scope(self, name):
        self._cur_scope = name

    def barrier(self):
        last = {}
        for i, o in enumerate(self.ops):
            last[o.stream] = i
        self._pb = list(last.values())
        self._pb_done = set()

    def idb16(self):
        if not hasattr(self, "_idb"):
            idf = self.sbuf("idb_f32", [128, 128], F32, base=True)
            idb = self.sbuf("idb_bf", [128, 128], BF16, base=True)
            self.op("pool", lambda g, idf=idf: g.iota(
                idf[:, :], [[1, 128]], base=0, channel_multiplier=-1,
                allow_small_or_imprecise_dtypes=True), wr=[idf])
            self.op("dve", lambda v, idf=idf, idb=idb: v.tensor_scalar(
                idb[:, :], idf[:, :], 0.0, None, ALU.is_equal, ALU.bypass),
                    rd=[idf], wr=[idb])
            self._idb = idb
        return self._idb

    def tr128(self, dst_buf, dst_ap, src_buf, src_ap):
        idb = self.idb16()
        pb = self.psum_bf()
        self.op("pe", lambda t, pb=pb, src_ap=src_ap, idb=idb: t.transpose(
            pb[:, 0:128], src_ap, idb[:, :]), rd=[src_buf, idb], wr=[pb], tag="tr")
        self.op("dve", lambda v, pb=pb, dst_ap=dst_ap: v.tensor_copy(
            dst_ap, pb[:, 0:128]), rd=[pb], wr=[dst_buf], tag="trev")

    def cst(self, val):
        if val not in self._csts:
            b = self.sbuf("cst", [128, 1], F32, base=True)
            self.op("dve", lambda v, b=b, val=val: v.memset(b[:, :], float(val)),
                    wr=[b], tag="cst")
            self._csts[val] = b
        return self._csts[val]

    def _nm(self, base):
        self._name_i += 1
        return f"{base}_{self._name_i}"

    def sbuf(self, name, shape, dtype, base=False, **kw):
        if base or not self._scopes:
            t = self.nc.alloc_sbuf_tensor(self._nm(name), shape, dtype, **kw)
        else:
            t = self._scopes[-1].enter_context(
                self.nc.sbuf_tensor(self._nm(name), shape, dtype, **kw))
        return Buf(name, t)

    def dram(self, name, shape, dtype, kind="Internal"):
        t = self.nc.dram_tensor(self._nm(name), shape, dtype, kind=kind)
        return Buf(name, t.ap())

    def make_psum_pool(self, n_f32=6, n_bf16=2):
        self._psum_pool = [
            Buf("psF", self.nc.alloc_psum_tensor(f"psF{i}", [128, 512], F32))
            for i in range(n_f32)
        ]
        self._psum_bf = [
            Buf("psB", self.nc.alloc_psum_tensor(f"psB{i}", [128, 1024], BF16))
            for i in range(n_bf16)
        ]
        self._psum_i = 0
        self._psum_bi = 0

    def psum(self):
        lo, hi = getattr(self, "_psum_win", (0, len(self._psum_pool)))
        n = hi - lo
        b = self._psum_pool[lo + self._psum_i % n]
        self._psum_i += 1
        return b

    def psum_window(self, lo, hi):
        self._psum_win = (lo, hi)
        self._psum_i = 0

    def psum_bank(self, i):
        return self._psum_pool[i]

    def psum_bf(self):
        b = self._psum_bf[self._psum_bi % len(self._psum_bf)]
        self._psum_bi += 1
        return b

    def op(self, stream, fn, rd=(), wr=(), tag=""):
        idx = len(self.ops)
        o = _Op(stream, fn, tag)
        my_eng = ENG_OF_QUEUE.get(stream, stream)
        o.scope = getattr(self, "_cur_scope", None)
        if self._pb is not None and stream not in self._pb_done:
            o.deps.extend(self._pb)
            self._pb_done.add(stream)
        def _need(pidx):
            ps = self.ops[pidx].stream
            if ps == stream == "pe":
                return False
            if ps == stream and stream in ENG_OF_QUEUE:
                return False
            return True

        for b in rd:
            if b is None:
                continue
            b = b.root
            lw = b.last_w
            if lw is not None and _need(lw):
                o.deps.append(lw)
            b.readers.append(idx)
        for b in wr:
            if b is None:
                continue
            b = b.root
            if b.last_w is not None and _need(b.last_w):
                o.deps.append(b.last_w)
            for r in b.readers:
                if r != idx and _need(r):
                    o.deps.append(r)
            b.last_w = idx
            b.readers = []
        self.ops.append(o)
        return idx

    def dma(self, queue, dst_ap, src_ap, rd=(), wr=(), tag=""):
        host = self.eng[ENG_OF_QUEUE[queue]]

        def fn(_):
            return host.dma_start(dst_ap, src_ap)

        return self.op(queue, fn, rd=rd, wr=wr, tag=tag)

    def dma_t(self, queue, dst_ap, src_ap, rd=(), wr=(), tag=""):
        queue = {"qsp": "tsp", "qact": "tact"}.get(queue, queue)
        host = self.eng[ENG_OF_QUEUE[queue]]

        def fn(_):
            return host.dma_start(dst_ap, src_ap, transpose=True)

        return self.op(queue, fn, rd=rd, wr=wr, tag=tag)

    NLANE = 8

    def emit(self):
        nc = self.nc
        NL = self.NLANE
        for o in self.ops:
            for d in o.deps:
                self.ops[d].needs_inc = True
        counts = {}
        qidx = {}
        for o in self.ops:
            if o.stream in ENG_OF_QUEUE:
                o.needs_inc = True
                j = qidx.get(o.stream, 0)
                qidx[o.stream] = j + 1
                o.res = j % NL
                key = (o.stream, o.res)
                counts[key] = counts.get(key, 0) + 16
                o.milestone = counts[key]
            elif o.needs_inc:
                key = (o.stream, 0)
                counts[key] = counts.get(key, 0) + 1
                o.milestone = counts[key]
        sems = {k: nc.alloc_semaphore(f"sem_{k[0]}_{k[1]}") for k in counts}
        watermark = {}
        n_waits = 0
        cur_scope = [None]
        scope_ctx = [None]
        for o in self.ops:
            if o.scope != cur_scope[0]:
                if scope_ctx[0] is not None:
                    scope_ctx[0].__exit__(None, None, None)
                    scope_ctx[0] = None
                cur_scope[0] = o.scope
                if o.scope is not None:
                    scope_ctx[0] = nc.named_scope(o.scope)
                    scope_ctx[0].__enter__()
            host_eng_name = ENG_OF_QUEUE.get(o.stream, o.stream)
            eng = self.eng[host_eng_name]
            need = {}
            for d in o.deps:
                p = self.ops[d]
                skey = (p.stream, p.res)
                key = (host_eng_name, skey)
                if watermark.get(key, 0) < p.milestone:
                    need[skey] = max(need.get(skey, 0), p.milestone)
            if o.stream in ENG_OF_QUEUE and o.milestone > 16:
                skey = (o.stream, o.res)
                v = o.milestone - 16
                if watermark.get((host_eng_name, skey), 0) < v:
                    need[skey] = max(need.get(skey, 0), v)
            for skey, v in sorted(need.items()):
                eng.wait_ge(sems[skey], v)
                watermark[(host_eng_name, skey)] = v
                n_waits += 1
            inst = o.fn(eng)
            if o.needs_inc:
                inst.then_inc(sems[(o.stream, o.res)],
                              16 if o.stream in ENG_OF_QUEUE else 1)
        if scope_ctx[0] is not None:
            scope_ctx[0].__exit__(None, None, None)
        for en in ("sp",):
            eng = self.eng[en]
            for skey, total in counts.items():
                if watermark.get((en, skey), 0) < total:
                    eng.wait_ge(sems[skey], total)
        self._n_waits = n_waits
        return counts



def bcast_ap(dram_buf_ap, n=128):
    return dram_buf_ap.partition_broadcast(n)


def ln_into_fm(p, x_dram, xlnf, tiles=range(NT), eps=1e-5, pool_n=3, out_tm=None,
               qs=("qsp", "qact", "qpool")):
    pool = []
    for i in range(pool_n):
        pool.append(dict(
            xt=p.sbuf("ln_xt", [128, D], F32),
            sq=p.sbuf("ln_sq", [128, D], F32),
            s=p.sbuf("ln_s", [128, 1], F32),
            mu=p.sbuf("ln_mu", [128, 1], F32),
            ss=p.sbuf("ln_ss", [128, 1], F32),
            var=p.sbuf("ln_var", [128, 1], F32),
            rstd=p.sbuf("ln_rstd", [128, 1], F32),
            xln=p.sbuf("ln_xln", [128, D], BF16, is_transpose_dest=False),
        ))
    for j, i in enumerate(tiles):
        e = pool[j % pool_n]
        xt, sq, s, mu, ss, var, rstd, xln = (e[k] for k in
                                             ("xt", "sq", "s", "mu", "ss", "var", "rstd", "xln"))
        q = qs[j % len(qs)]
        p.dma(q, xt[:, :], x_dram.t[i * 128:(i + 1) * 128, :], wr=[xt], tag="ln_ld")
        p.op("dve", lambda v, xt=xt, s=s: v.tensor_reduce(s[:, :], xt[:, :], AX.X, ALU.add),
             rd=[xt], wr=[s], tag="ln_sum")
        p.op("act", lambda a, xt=xt, sq=sq, ss=ss: a.activation(
            sq[:, :], xt[:, :], AF.Square, accum_out=ss[:, :]),
             rd=[xt], wr=[sq, ss], tag="ln_sq")
        p.op("act", lambda a, ss=ss: a.copy(ss[:, :], ss[:, :]),
             rd=[ss], wr=[ss], tag="ln_sq_guard")
        p.op("dve", lambda v, s=s, mu=mu: v.tensor_scalar(
            mu[:, :], s[:, :], 1.0 / D, None, ALU.mult, ALU.bypass),
             rd=[s], wr=[mu], tag="ln_mu")
        p.op("dve", lambda v, mu=mu, var=var: v.tensor_tensor(
            var[:, :], mu[:, :], mu[:, :], ALU.mult),
             rd=[mu], wr=[var], tag="ln_musq")
        p.op("dve", lambda v, ss=ss, var=var: v.scalar_tensor_tensor(
            var[:, :], ss[:, :], 1.0 / D, var[:, :], ALU.mult, ALU.subtract),
             rd=[ss, var], wr=[var], tag="ln_var")
        epsc = p.cst(float(eps))
        p.op("act", lambda a, var=var, rstd=rstd, epsc=epsc: a.activation(
            rstd[:, :], var[:, :], AF.Sqrt, bias=epsc[:, :]),
             rd=[var, epsc], wr=[rstd], tag="ln_sqrt")
        p.op("dve", lambda v, rstd=rstd: v.reciprocal(rstd[:, :], rstd[:, :]),
             rd=[rstd], wr=[rstd], tag="ln_recip")
        p.op("dve", lambda v, xt=xt, mu=mu, rstd=rstd, xln=xln: v.tensor_scalar(
            xln[:, :], xt[:, :], mu[:, :], rstd[:, :], ALU.subtract, ALU.mult),
             rd=[xt, mu, rstd], wr=[xln], tag="ln_norm")
        for c in range(8):
            q2 = ("qsp", "qact")[(j + c) % 2]
            p.dma_t(q2, xlnf.t[:, c * T + i * 128: c * T + (i + 1) * 128],
                    xln[:, c * 128:(c + 1) * 128], rd=[xln], wr=[xlnf], tag="ln_tr")
        if out_tm is not None:
            p.op("dve", lambda v, xln=xln, i=i: v.tensor_copy(
                out_tm.t[:, i * D:(i + 1) * D], xln[:, :]),
                 rd=[xln], wr=[out_tm], tag="ln_keep")


def residual_add_store(p, x_dram, xnew_dram, delta_psum_fn, n_out_cols=D,
                       tiles=range(NT), pool_n=3, q="qsp"):
    raise NotImplementedError



def build_l1ffn(nc, per_core):
    p = Prog(nc)
    x = Buf("x", nc.dram_tensor("x", [T, D], F32, kind="ExternalInput").ap())
    k0 = Buf("k0", nc.dram_tensor("k0", [2 * D, 4096], U8, kind="ExternalInput").ap())
    k1 = Buf("k1", nc.dram_tensor("k1", [4096, 4096], U8, kind="ExternalInput").ap())
    k2 = Buf("k2", nc.dram_tensor("k2", [4096, D], U8, kind="ExternalInput").ap())
    s0 = Buf("s0", nc.dram_tensor("s0", [1], F32, kind="ExternalInput").ap())
    s1 = Buf("s1", nc.dram_tensor("s1", [1], F32, kind="ExternalInput").ap())
    s2 = Buf("s2", nc.dram_tensor("s2", [1], F32, kind="ExternalInput").ap())
    xout = Buf("xout", nc.dram_tensor("xout", [T, D], F32, kind="ExternalOutput").ap())
    p.make_psum_pool(6, 2)
    wds = build_sign_weights(p, k0, k1, k2)
    emit_l1ffn(p, x, xout, *wds, s0, s1, s2)
    p.emit()
    return p


def build_sign_weights(p, k0, k1, k2):
    H = 4096
    w0d = p.dram("w0d", [D, H], BF16)
    w1d = p.dram("w1d", [H, H], BF16)
    w2d = p.dram("w2d", [H, D], BF16)
    if True:
        ua = p.sbuf("cv_ua", [128, H], U8)
        ub = p.sbuf("cv_ub", [128, H], U8)
        tf = p.sbuf("cv_tf", [128, H], F32)
        tb = p.sbuf("cv_tb", [128, H], BF16)
        for k in range(8):
            p.dma("qsp", ua[:, :], k0.t[k * 128:(k + 1) * 128, :], rd=[k0], wr=[ua])
            p.dma("qact", ub[:, :], k0.t[D + k * 128: D + (k + 1) * 128, :], rd=[k0], wr=[ub])
            p.op("dve", lambda v, ua=ua, ub=ub, tf=tf: v.tensor_tensor(
                tf[:, :], ua[:, :], ub[:, :], ALU.subtract), rd=[ua, ub], wr=[tf])
            p.op("dve", lambda v, tf=tf, tb=tb: v.tensor_scalar(
                tb[:, :], tf[:, :], 2.0, None, ALU.mult, ALU.bypass), rd=[tf], wr=[tb])
            p.dma("qpool", w0d.t[k * 128:(k + 1) * 128, :], tb[:, :], rd=[tb], wr=[w0d])
        for k in range(32):
            p.dma("qsp", ua[:, :], k1.t[k * 128:(k + 1) * 128, :], rd=[k1], wr=[ua])
            p.op("dve", lambda v, ua=ua, tb=tb: v.tensor_scalar(
                tb[:, :], ua[:, :], 2.0, -1.0, ALU.mult, ALU.add), rd=[ua], wr=[tb])
            p.dma("qpool", w1d.t[k * 128:(k + 1) * 128, :], tb[:, :], rd=[tb], wr=[w1d])
        for k in range(32):
            p.dma("qsp", ua[:, 0:D], k2.t[k * 128:(k + 1) * 128, :], rd=[k2], wr=[ua])
            p.op("dve", lambda v, ua=ua, tb=tb: v.tensor_scalar(
                tb[:, 0:D], ua[:, 0:D], 2.0, -1.0, ALU.mult, ALU.add), rd=[ua], wr=[tb])
            p.dma("qpool", w2d.t[k * 128:(k + 1) * 128, :], tb[:, 0:D], rd=[tb], wr=[w2d])
    return w0d, w1d, w2d


def emit_l1ffn(p, x, xout, w0d, w1d, w2d, s0, s1, s2):
    H = 4096
    scol = []
    for i, s in enumerate((s0, s1, s2)):
        c = p.sbuf("scol", [128, 1], F32)
        p.dma("qsp", c[:, :], s.t[0:1].partition_broadcast(128), rd=[s], wr=[c], tag="s_bc")
        scol.append(c)

    SW = 512
    NS = T // SW
    h1f = p.sbuf("h1f", [128, 32 * SW], BF16)
    h2f = p.sbuf("h2f", [128, 32 * SW], BF16)
    xlnf = p.sbuf("ffn_xlnf", [128, 8 * SW], BF16, is_transpose_dest=True)
    w0p = [p.sbuf("w0p", [128, 8 * 512], BF16) for _ in range(1)]
    w1p = [p.sbuf("w1p", [128, 32 * 512], BF16) for _ in range(2)]
    w2s = [p.sbuf("w2s", [128, D], BF16) for _ in range(3)]
    tmps = [p.sbuf("ffn_tmp", [128, 512], F32) for _ in range(2)]
    xt2s = [p.sbuf("ffn_xt", [128, D], F32) for _ in range(2)]

    _sq = p.sbuf("f_sq", [128, D], F32)
    lnp = [dict(
        xt=p.sbuf("f_xt", [128, D], F32), sq=_sq,
        s=p.sbuf("f_s", [128, 1], F32), mu=p.sbuf("f_mu", [128, 1], F32),
        ss=p.sbuf("f_ss", [128, 1], F32), var=p.sbuf("f_var", [128, 1], F32),
        rstd=p.sbuf("f_rstd", [128, 1], F32), xln=p.sbuf("f_xln", [128, D], BF16),
    ) for _ in range(1)]

    for si in range(NS):
        t0 = si * SW
        for jj in range(SW // 128):
            i = (t0 // 128) + jj
            e = lnp[0]
            xt, sq, s, mu, ss, var, rstd, xln = (e[k] for k in
                                                 ("xt", "sq", "s", "mu", "ss", "var", "rstd", "xln"))
            p.dma("qsp", xt[:, :], x.t[i * 128:(i + 1) * 128, :], rd=[x], wr=[xt])
            p.op("dve", lambda v, xt=xt, s=s: v.tensor_reduce(s[:, :], xt[:, :], AX.X, ALU.add),
                 rd=[xt], wr=[s])
            p.op("act", lambda a, xt=xt, sq=sq, ss=ss: a.activation(
                sq[:, :], xt[:, :], AF.Square, accum_out=ss[:, :]), rd=[xt], wr=[sq, ss])
            p.op("act", lambda a, ss=ss: a.copy(ss[:, :], ss[:, :]), rd=[ss], wr=[ss])
            p.op("dve", lambda v, s=s, mu=mu: v.tensor_scalar(
                mu[:, :], s[:, :], 1.0 / D, None, ALU.mult, ALU.bypass), rd=[s], wr=[mu])
            p.op("dve", lambda v, mu=mu, var=var: v.tensor_tensor(
                var[:, :], mu[:, :], mu[:, :], ALU.mult), rd=[mu], wr=[var])
            p.op("dve", lambda v, ss=ss, var=var: v.scalar_tensor_tensor(
                var[:, :], ss[:, :], 1.0 / D, var[:, :], ALU.mult, ALU.subtract),
                 rd=[ss, var], wr=[var])
            epsc = p.cst(1e-5)
            p.op("act", lambda a, var=var, rstd=rstd, epsc=epsc: a.activation(
                rstd[:, :], var[:, :], AF.Sqrt, bias=epsc[:, :]), rd=[var, epsc], wr=[rstd])
            p.op("dve", lambda v, rstd=rstd: v.reciprocal(rstd[:, :], rstd[:, :]),
                 rd=[rstd], wr=[rstd])
            p.op("dve", lambda v, xt=xt, mu=mu, rstd=rstd, xln=xln: v.tensor_scalar(
                xln[:, :], xt[:, :], mu[:, :], rstd[:, :], ALU.subtract, ALU.mult),
                 rd=[xt, mu, rstd], wr=[xln])
            for c in range(8):
                p.tr128(xlnf, xlnf.t[:, c * SW + jj * 128: c * SW + (jj + 1) * 128],
                        xln, xln[:, c * 128:(c + 1) * 128])

        G1 = 4
        for g in range(8):
            wp = w0p[0]
            for c in range(8):
                p.dma(("qsp", "qact", "qpool")[c % 3],
                      wp.t[:, c * 512:(c + 1) * 512],
                      w0d.t[c * 128:(c + 1) * 128, g * 512:(g + 1) * 512],
                      rd=[w0d], wr=[wp], tag="w0ld")
            pss = [p.psum() for _ in range(G1)]
            for mi in range(G1):
                m = g * G1 + mi
                for k in range(8):
                    p.op("pe", lambda t, wp=wp, k=k, mi=mi, ps=pss[mi]:
                         t.matmul(ps[:, 0:SW],
                                  wp[:, k * 512 + mi * 128: k * 512 + mi * 128 + 128],
                                  xlnf[:, k * SW:(k + 1) * SW],
                                  start=(k == 0), stop=(k == 7)),
                         rd=[wp, xlnf], wr=[pss[mi]], tag="mm1")
            for mi in range(G1):
                m = g * G1 + mi
                tmp = tmps[mi % 2]
                p.op("dve", lambda v, ps=pss[mi], sc=scol[0], tmp=tmp: v.tensor_scalar(
                    tmp[:, 0:SW], ps[:, 0:SW], sc[:, :], None, ALU.mult, ALU.bypass),
                     rd=[pss[mi], scol[0]], wr=[tmp], tag="ev1a")
                p.op("act", lambda a, tmp=tmp, m=m: a.activation(
                    h1f.t[:, m * SW:(m + 1) * SW], tmp[:, 0:SW], AF.Relu, scale=2.0),
                     rd=[tmp], wr=[h1f], tag="ev1b")

        G2 = 4
        for g in range(8):
            wp = w1p[g % 2]
            for c in range(32):
                p.dma(("qsp", "qact", "qpool")[c % 3],
                      wp.t[:, c * 512:(c + 1) * 512],
                      w1d.t[c * 128:(c + 1) * 128, g * 512:(g + 1) * 512],
                      rd=[w1d], wr=[wp], tag="w1ld")
            pss = [p.psum() for _ in range(G2)]
            for mi in range(G2):
                for k in range(32):
                    p.op("pe", lambda t, wp=wp, k=k, mi=mi, ps=pss[mi]:
                         t.matmul(ps[:, 0:SW],
                                  wp[:, k * 512 + mi * 128: k * 512 + mi * 128 + 128],
                                  h1f[:, k * SW:(k + 1) * SW],
                                  start=(k == 0), stop=(k == 31)),
                         rd=[wp, h1f], wr=[pss[mi]], tag="mm2")
            for mi in range(G2):
                m = g * G2 + mi
                tmp = tmps[mi % 2]
                p.op("dve", lambda v, ps=pss[mi], sc=scol[1], tmp=tmp: v.tensor_scalar(
                    tmp[:, 0:SW], ps[:, 0:SW], sc[:, :], None, ALU.mult, ALU.bypass),
                     rd=[pss[mi], scol[1]], wr=[tmp], tag="ev2a")
                p.op("act", lambda a, tmp=tmp, m=m: a.activation(
                    h2f.t[:, m * SW:(m + 1) * SW], tmp[:, 0:SW], AF.Relu, scale=2.0),
                     rd=[tmp], wr=[h2f], tag="ev2b")

        for half in range(2):
            pss = [p.psum() for _ in range(4)]
            for k in range(32):
                ws = w2s[k % 3]
                p.dma("qact", ws[:, :], w2d.t[k * 128:(k + 1) * 128, :], rd=[w2d], wr=[ws], tag="w2ld")
                for tt2 in range(2):
                    tt = half * 2 + tt2
                    for cc in range(2):
                        p.op("pe", lambda t, ws=ws, tt=tt, cc=cc, k=k, ps=pss[tt2 * 2 + cc], tt2=tt2:
                             t.matmul(ps[:, 0:512],
                                      h2f[:, (k) * SW + tt * 128: (k) * SW + tt * 128 + 128],
                                      ws[:, cc * 512:(cc + 1) * 512],
                                      start=(k == 0), stop=(k == 31)),
                             rd=[ws, h2f], wr=[pss[tt2 * 2 + cc]], tag="mm3")
            for tt2 in range(2):
                tt = half * 2 + tt2
                i = (t0 // 128) + tt
                xt2 = xt2s[tt2]
                p.dma("qsp", xt2[:, :], x.t[i * 128:(i + 1) * 128, :], rd=[x], wr=[xt2])
                for cc in range(2):
                    ps = pss[tt2 * 2 + cc]
                    p.op("dve", lambda v, ps=ps, sc=scol[2], cc=cc, xt2=xt2: v.scalar_tensor_tensor(
                        xt2[:, cc * 512:(cc + 1) * 512], ps[:, 0:512], sc[:, :],
                        xt2[:, cc * 512:(cc + 1) * 512], ALU.mult, ALU.add),
                         rd=[ps, scol[2], xt2], wr=[xt2], tag="ev3")
                p.dma("qpool", xout.t[i * 128:(i + 1) * 128, :], xt2[:, :], rd=[xt2], wr=[xout])
    return xout






def emit_l1attnb(p, x, xout, wq, wk, wvT, proj):
    P = 256
    xlnf = p.sbuf("lb_xlnf", [128, 8 * T], BF16, is_transpose_dest=True)
    ln_into_fm(p, x, xlnf)

    projb = p.sbuf("lb_projb", [128, 8 * P], BF16)
    tmpf = p.sbuf("lb_tmpf", [128, 1024], F32)
    for c in range(8):
        p.dma("qsp", tmpf[:, 0:P], proj.t[c * 128:(c + 1) * 128, :], rd=[proj], wr=[tmpf])
        p.op("dve", lambda v, c=c, tmpf=tmpf: v.tensor_copy(
            projb.t[:, c * P:(c + 1) * P], tmpf[:, 0:P]), rd=[tmpf], wr=[projb])
    wqkb = p.sbuf("lb_wqkb", [128, 8 * 1024], BF16)
    wps = []
    for wsrc in (wq, wk):
        wp = p.sbuf("lb_wp", [128, 8 * P], BF16)
        for c in range(8):
            p.dma("qact", tmpf[:, :], wsrc.t[c * 128:(c + 1) * 128, :], rd=[wsrc], wr=[tmpf])
            p.op("act", lambda a, c=c, tmpf=tmpf: a.copy(
                wqkb.t[:, c * 1024:(c + 1) * 1024], tmpf[:, :]), rd=[tmpf], wr=[wqkb])
        for m in range(8):
            ps = p.psum()
            for k in range(8):
                p.op("pe", lambda t, k=k, m=m, ps=ps: t.matmul(
                    ps[:, 0:P], wqkb[:, k * 1024 + m * 128: k * 1024 + m * 128 + 128],
                    projb[:, k * P:(k + 1) * P], start=(k == 0), stop=(k == 7)),
                     rd=[wqkb, projb], wr=[ps], tag="fold")
            p.op("dve", lambda v, ps=ps, m=m, wp=wp: v.tensor_copy(
                wp.t[:, m * P:(m + 1) * P], ps[:, 0:P]), rd=[ps], wr=[wp])
        wps.append(wp)
    wqp, wkp = wps

    wvb = wqkb

    qp_fm = p.sbuf("lb_qpfm", [128, 2 * T], BF16)
    kp_tm = p.sbuf("lb_kptm", [128, NT * P], BF16)
    e1 = p.sbuf("lb_e1", [128, 512], F32)
    e2 = p.sbuf("lb_e2", [128, 512], F32)

    def elu1(ps, out_buf, out_ap, w):
        p.op("dve", lambda v, ps=ps: v.tensor_scalar(
            e1[:, 0:w], ps[:, 0:w], 0.0, None, ALU.min, ALU.bypass), rd=[ps], wr=[e1])
        p.op("act", lambda a: a.activation(e1[:, 0:w], e1[:, 0:w], AF.Exp),
             rd=[e1], wr=[e1])
        p.op("dve", lambda v, ps=ps: v.tensor_scalar(
            e2[:, 0:w], ps[:, 0:w], 0.0, None, ALU.max, ALU.bypass), rd=[ps], wr=[e2])
        p.op("dve", lambda v, out_ap=out_ap: v.tensor_tensor(
            out_ap, e1[:, 0:w], e2[:, 0:w], ALU.add), rd=[e1, e2], wr=[out_buf], tag="elu")

    for pc in range(2):
        for n in range(8):
            ps = p.psum()
            for k in range(8):
                p.op("pe", lambda t, pc=pc, n=n, k=k, ps=ps: t.matmul(
                    ps[:, 0:512], wqp[:, k * P + pc * 128: k * P + pc * 128 + 128],
                    xlnf[:, k * T + n * 512: k * T + (n + 1) * 512],
                    start=(k == 0), stop=(k == 7)), rd=[wqp, xlnf], wr=[ps], tag="qproj")
            elu1(ps, qp_fm, qp_fm.t[:, pc * T + n * 512: pc * T + (n + 1) * 512], 512)
    for i in range(NT):
        ps = p.psum()
        for k in range(8):
            p.op("pe", lambda t, i=i, k=k, ps=ps: t.matmul(
                ps[:, 0:P], xlnf[:, k * T + i * 128: k * T + (i + 1) * 128],
                wkp[:, k * P:(k + 1) * P], start=(k == 0), stop=(k == 7)),
                 rd=[xlnf, wkp], wr=[ps], tag="kproj")
        elu1(ps, kp_tm, kp_tm.t[:, i * P:(i + 1) * P], P)

    for c in range(8):
        p.dma("qsp", tmpf[:, :], wvT.t[c * 128:(c + 1) * 128, :], rd=[wvT], wr=[tmpf])
        p.op("act", lambda a, c=c, tmpf=tmpf: a.copy(
            wvb.t[:, c * 1024:(c + 1) * 1024], tmpf[:, :]), rd=[tmpf], wr=[wvb])
    kv = p.sbuf("lb_kv", [128, 2 * 1024], BF16)
    vtile = p.sbuf("lb_vtile", [128, 1024], BF16)
    p.psum_window(0, 2)
    kvps = [[p.psum_bank(2 + 2 * pc + n) for n in range(2)] for pc in range(2)]
    for i in range(NT):
        for n in range(2):
            ps = p.psum()
            for k in range(8):
                p.op("pe", lambda t, i=i, n=n, k=k, ps=ps: t.matmul(
                    ps[:, 0:512], xlnf[:, k * T + i * 128: k * T + (i + 1) * 128],
                    wvb[:, k * 1024 + n * 512: k * 1024 + (n + 1) * 512],
                    start=(k == 0), stop=(k == 7)), rd=[xlnf, wvb], wr=[ps], tag="vproj")
            p.op("dve", lambda v, ps=ps, n=n: v.tensor_copy(
                vtile.t[:, n * 512:(n + 1) * 512], ps[:, 0:512]), rd=[ps], wr=[vtile])
        for pc in range(2):
            for n in range(2):
                p.op("pe", lambda t, pc=pc, n=n, i=i, ps=kvps[pc][n]: t.matmul(
                    ps[:, 0:512], kp_tm[:, i * P + pc * 128: i * P + pc * 128 + 128],
                    vtile[:, n * 512:(n + 1) * 512],
                    start=(i == 0), stop=(i == NT - 1)), rd=[kp_tm, vtile],
                     wr=[kvps[pc][n]], tag="kv")
    for pc in range(2):
        for n in range(2):
            p.op("dve", lambda v, ps=kvps[pc][n], pc=pc, n=n: v.tensor_copy(
                kv.t[:, pc * 1024 + n * 512: pc * 1024 + (n + 1) * 512], ps[:, 0:512]),
                 rd=[kvps[pc][n]], wr=[kv])

    p.psum_window(0, 6)
    qp_tm = p.sbuf("lb_qptm", [128, P], F32)
    zc = p.sbuf("lb_zc", [128, NT], F32)
    for i in range(NT):
        ps = p.psum()
        for k in range(8):
            p.op("pe", lambda t, i=i, k=k, ps=ps: t.matmul(
                ps[:, 0:P], xlnf[:, k * T + i * 128: k * T + (i + 1) * 128],
                wqp[:, k * P:(k + 1) * P], start=(k == 0), stop=(k == 7)),
                 rd=[xlnf, wqp], wr=[ps], tag="qproj_tm")
        elu1(ps, qp_tm, qp_tm.t[:, 0:P], P)
        p.op("dve", lambda v, i=i: v.tensor_reduce(
            zc.t[:, i:i + 1], qp_tm[:, 0:P], AX.X, ALU.add), rd=[qp_tm], wr=[zc])
        p.op("dve", lambda v, i=i: v.tensor_scalar(
            zc.t[:, i:i + 1], zc.t[:, i:i + 1], 1e-8, None, ALU.add, ALU.bypass),
             rd=[zc], wr=[zc])
        p.op("dve", lambda v, i=i: v.reciprocal(zc.t[:, i:i + 1], zc.t[:, i:i + 1]),
             rd=[zc], wr=[zc])

    xt2 = p.sbuf("lb_xt2", [128, D], F32)
    for i in range(NT):
        p.dma("qsp", xt2[:, :], x.t[i * 128:(i + 1) * 128, :], rd=[x], wr=[xt2])
        for n in range(2):
            ps = p.psum()
            for pc in range(2):
                p.op("pe", lambda t, i=i, n=n, pc=pc, ps=ps: t.matmul(
                    ps[:, 0:512], qp_fm[:, pc * T + i * 128: pc * T + (i + 1) * 128],
                    kv[:, pc * 1024 + n * 512: pc * 1024 + (n + 1) * 512],
                    start=(pc == 0), stop=(pc == 1)), rd=[qp_fm, kv], wr=[ps], tag="num")
            p.op("dve", lambda v, ps=ps, i=i, n=n, xt2=xt2: v.scalar_tensor_tensor(
                xt2[:, n * 512:(n + 1) * 512], ps[:, 0:512], zc.t[:, i:i + 1],
                xt2[:, n * 512:(n + 1) * 512], ALU.mult, ALU.add),
                 rd=[ps, zc, xt2], wr=[xt2], tag="numev")
        p.dma("qpool", xout.t[i * 128:(i + 1) * 128, :], xt2[:, :], rd=[xt2], wr=[xout])


def build_l1attnb(nc):
    p = Prog(nc)
    x = Buf("x", nc.dram_tensor("x", [T, D], F32, kind="ExternalInput").ap())
    wq = Buf("wq", nc.dram_tensor("wq", [D, D], F32, kind="ExternalInput").ap())
    wk = Buf("wk", nc.dram_tensor("wk", [D, D], F32, kind="ExternalInput").ap())
    wvT = Buf("wvT", nc.dram_tensor("wvT", [D, D], F32, kind="ExternalInput").ap())
    proj = Buf("proj", nc.dram_tensor("proj", [D, 256], F32, kind="ExternalInput").ap())
    xout = Buf("xout", nc.dram_tensor("xout", [T, D], F32, kind="ExternalOutput").ap())
    p.make_psum_pool(6, 2)
    emit_l1attnb(p, x, xout, wq, wk, wvT, proj)
    p.emit()
    return p



def emit_moe(p, x, xout, wr, w1, w2, w3):
    H = 1536
    NH = H // 128
    xlnf = p.sbuf("mo_xlnf", [128, 8 * T], BF16, is_transpose_dest=True)
    ln_into_fm(p, x, xlnf, pool_n=2)

    wrb = p.sbuf("mo_wrb", [128, 64], BF16)
    tmpw = p.sbuf("mo_tmpw", [128, 1536], F32)
    for c in range(8):
        p.dma("qsp", tmpw[:, 0:8], wr.t[c * 128:(c + 1) * 128, :], rd=[wr], wr=[tmpw])
        p.op("act", lambda a, c=c: a.copy(wrb.t[:, c * 8:(c + 1) * 8], tmpw[:, 0:8]),
             rd=[tmpw], wr=[wrb])
    combs = p.sbuf("mo_combs", [128, NT * 8], F32)
    lt = p.sbuf("mo_lt", [128, 8], F32)
    m1 = p.sbuf("mo_m1", [128, 1], F32)
    nm1 = p.sbuf("mo_nm1", [128, 1], F32)
    msk = p.sbuf("mo_msk", [128, 8], F32)
    m2 = p.sbuf("mo_m2", [128, 1], F32)
    ind = p.sbuf("mo_ind", [128, 8], F32)
    eex = p.sbuf("mo_eex", [128, 8], F32)
    den = p.sbuf("mo_den", [128, 1], F32)
    for i in range(NT):
        ps = p.psum()
        for k in range(8):
            p.op("pe", lambda t, i=i, k=k, ps=ps: t.matmul(
                ps[:, 0:8], xlnf[:, k * T + i * 128: k * T + (i + 1) * 128],
                wrb[:, k * 8:(k + 1) * 8], start=(k == 0), stop=(k == 7)),
                 rd=[xlnf, wrb], wr=[ps], tag="router")
        p.op("dve", lambda v, ps=ps: v.tensor_copy(lt[:, :], ps[:, 0:8]), rd=[ps], wr=[lt])
        p.op("dve", lambda v: v.tensor_reduce(m1[:, :], lt[:, :], AX.X, ALU.max),
             rd=[lt], wr=[m1])
        p.op("dve", lambda v: v.tensor_scalar(
            nm1[:, :], m1[:, :], -1.0, None, ALU.mult, ALU.bypass), rd=[m1], wr=[nm1])
        p.op("dve", lambda v: v.tensor_scalar(
            msk[:, :], lt[:, :], m1[:, :], None, ALU.is_ge, ALU.bypass),
             rd=[lt, m1], wr=[msk])
        p.op("dve", lambda v: v.scalar_tensor_tensor(
            msk[:, :], msk[:, :], -1e30, lt[:, :], ALU.mult, ALU.add),
             rd=[msk, lt], wr=[msk])
        p.op("dve", lambda v: v.tensor_reduce(m2[:, :], msk[:, :], AX.X, ALU.max),
             rd=[msk], wr=[m2])
        p.op("dve", lambda v: v.tensor_scalar(
            ind[:, :], lt[:, :], m2[:, :], None, ALU.is_ge, ALU.bypass),
             rd=[lt, m2], wr=[ind])
        p.op("act", lambda a: a.activation(eex[:, :], lt[:, :], AF.Exp, bias=nm1[:, :]),
             rd=[lt, nm1], wr=[eex])
        p.op("act", lambda a: a.activation(den[:, :], m2[:, :], AF.Exp, bias=nm1[:, :]),
             rd=[m2, nm1], wr=[den])
        p.op("dve", lambda v: v.tensor_scalar(
            den[:, :], den[:, :], 1.0, None, ALU.add, ALU.bypass), rd=[den], wr=[den])
        p.op("dve", lambda v: v.reciprocal(den[:, :], den[:, :]), rd=[den], wr=[den])
        p.op("dve", lambda v: v.tensor_tensor(eex[:, :], eex[:, :], ind[:, :], ALU.mult),
             rd=[eex, ind], wr=[eex])
        p.op("dve", lambda v, i=i: v.tensor_scalar(
            combs.t[:, i * 8:(i + 1) * 8], eex[:, :], den[:, :], None, ALU.mult, ALU.bypass),
             rd=[eex, den], wr=[combs])

    w1bf = p.dram("mo_w1bf", [8, D, 1536], BF16)
    w3bf = p.dram("mo_w3bf", [8, D, 1536], BF16)
    w2bf = p.dram("mo_w2bf", [8, 1536, D], BF16)
    cvf = p.sbuf("mo_cvf", [128, 1536], F32)
    cvb = p.sbuf("mo_cvb", [128, 1536], BF16)
    for e in range(8):
        for src, dst, nk, w in ((w1, w1bf, 8, 1536), (w3, w3bf, 8, 1536), (w2, w2bf, 12, D)):
            for k in range(nk):
                p.dma(("qsp", "qact")[k % 2], cvf.t[:, 0:w],
                      src.t[e, k * 128:(k + 1) * 128, :], rd=[src], wr=[cvf])
                p.op("act", lambda a, w=w: a.copy(cvb[:, 0:w], cvf[:, 0:w]),
                     rd=[cvf], wr=[cvb])
                p.dma("qpool", dst.t[e, k * 128:(k + 1) * 128, :], cvb[:, 0:w],
                      rd=[cvb], wr=[dst])

    QW = 512
    NQ = T // QW
    actf = p.sbuf("mo_actf", [128, NH * QW], BF16)
    out_accs = p.sbuf("mo_oaccs", [128, QW // 128 * 1024], F32)
    w1b = [p.sbuf("mo_w1b", [128, 8 * 1536], BF16) for _ in range(1)]
    w3b = [p.sbuf("mo_w3b", [128, 8 * 1536], BF16) for _ in range(1)]
    w2b = [p.sbuf("mo_w2b", [128, NH * 1024], BF16) for _ in range(1)]
    tmph = p.sbuf("mo_tmph", [128, 512], F32)
    xt2 = p.sbuf("mo_xt2", [128, D], F32)
    for q in range(NQ):
        for e in range(8):
            wb1, wb3, wb2 = w1b[0], w3b[0], w2b[0]
            for k in range(8):
                p.dma(("qsp", "qact")[k % 2], wb1.t[:, k * 1536:(k + 1) * 1536],
                      w1bf.t[e, k * 128:(k + 1) * 128, :], rd=[w1bf], wr=[wb1], tag="w1ld")
                p.dma(("qact", "qsp")[k % 2], wb3.t[:, k * 1536:(k + 1) * 1536],
                      w3bf.t[e, k * 128:(k + 1) * 128, :], rd=[w3bf], wr=[wb3], tag="w3ld")
            for k in range(NH):
                p.dma("qpool", wb2.t[:, k * 1024:(k + 1) * 1024],
                      w2bf.t[e, k * 128:(k + 1) * 128, :], rd=[w2bf], wr=[wb2], tag="w2ld")
            for m in range(NH):
                psh = p.psum()
                for k in range(8):
                    p.op("pe", lambda t, q=q, k=k, m=m, ps=psh, wb1=wb1: t.matmul(
                        ps[:, 0:512], wb1[:, k * 1536 + m * 128: k * 1536 + m * 128 + 128],
                        xlnf[:, k * T + q * QW: k * T + (q + 1) * QW],
                        start=(k == 0), stop=(k == 7)), rd=[wb1, xlnf], wr=[psh], tag="mm_h")
                psg = p.psum()
                for k in range(8):
                    p.op("pe", lambda t, q=q, k=k, m=m, ps=psg, wb3=wb3: t.matmul(
                        ps[:, 0:512], wb3[:, k * 1536 + m * 128: k * 1536 + m * 128 + 128],
                        xlnf[:, k * T + q * QW: k * T + (q + 1) * QW],
                        start=(k == 0), stop=(k == 7)), rd=[wb3, xlnf], wr=[psg], tag="mm_g")
                p.op("act", lambda a, ps=psh: a.activation(
                    tmph[:, 0:512], ps[:, 0:512], AF.Silu), rd=[psh], wr=[tmph])
                p.op("dve", lambda v, ps=psg, m=m: v.tensor_tensor(
                    actf.t[:, m * QW:(m + 1) * QW],
                    tmph[:, 0:512], ps[:, 0:512], ALU.mult),
                     rd=[tmph, psg], wr=[actf], tag="actev")
            for tt in range(QW // 128):
                i = q * (QW // 128) + tt
                for c in range(2):
                    psy = p.psum()
                    for K in range(NH):
                        p.op("pe", lambda t, K=K, tt=tt, c=c, ps=psy, wb2=wb2: t.matmul(
                            ps[:, 0:512], actf[:, K * QW + tt * 128: K * QW + tt * 128 + 128],
                            wb2[:, K * 1024 + c * 512: K * 1024 + (c + 1) * 512],
                            start=(K == 0), stop=(K == NH - 1)), rd=[actf, wb2], wr=[psy], tag="mm_y")
                    if e == 0:
                        p.op("dve", lambda v, ps=psy, i=i, e=e, tt=tt, c=c: v.tensor_scalar(
                            out_accs.t[:, tt * 1024 + c * 512: tt * 1024 + (c + 1) * 512],
                            ps[:, 0:512], combs.t[:, i * 8 + e: i * 8 + e + 1], None,
                            ALU.mult, ALU.bypass), rd=[psy, combs], wr=[out_accs], tag="yev")
                    else:
                        p.op("dve", lambda v, ps=psy, i=i, e=e, tt=tt, c=c: v.scalar_tensor_tensor(
                            out_accs.t[:, tt * 1024 + c * 512: tt * 1024 + (c + 1) * 512],
                            ps[:, 0:512], combs.t[:, i * 8 + e: i * 8 + e + 1],
                            out_accs.t[:, tt * 1024 + c * 512: tt * 1024 + (c + 1) * 512],
                            ALU.mult, ALU.add), rd=[psy, combs, out_accs], wr=[out_accs], tag="yev")
        for tt in range(QW // 128):
            i = q * (QW // 128) + tt
            p.dma("qsp", xt2[:, :], x.t[i * 128:(i + 1) * 128, :], rd=[x], wr=[xt2])
            p.op("dve", lambda v, tt=tt: v.tensor_tensor(
                xt2[:, :], xt2[:, :], out_accs.t[:, tt * 1024:(tt + 1) * 1024], ALU.add),
                 rd=[xt2, out_accs], wr=[xt2], tag="resid")
            p.dma("qpool", xout.t[i * 128:(i + 1) * 128, :], xt2[:, :], rd=[xt2], wr=[xout])


def build_moe(nc):
    p = Prog(nc)
    x = Buf("x", nc.dram_tensor("x", [T, D], F32, kind="ExternalInput").ap())
    wr = Buf("wr", nc.dram_tensor("wr", [D, 8], F32, kind="ExternalInput").ap())
    w1 = Buf("w1", nc.dram_tensor("w1", [8, D, 1536], F32, kind="ExternalInput").ap())
    w2 = Buf("w2", nc.dram_tensor("w2", [8, 1536, D], F32, kind="ExternalInput").ap())
    w3 = Buf("w3", nc.dram_tensor("w3", [8, D, 1536], F32, kind="ExternalInput").ap())
    xout = Buf("xout", nc.dram_tensor("xout", [T, D], F32, kind="ExternalOutput").ap())
    p.make_psum_pool(6, 2)
    emit_moe(p, x, xout, wr, w1, w2, w3)
    p.emit()
    return p



def col_load(p, q, dst_ap, src_1d_ap, rd, wr):
    p.dma(q, dst_ap, src_1d_ap.rearrange("(a b) -> a b", b=1), rd=rd, wr=wr)


def emit_l1attna(p, x, xout, lng, lnb, w_inT, b_in, convw, convb, A, Bm,
                 w_cT, b_c, w_dT, b_d, log_delta, w_outT, b_out):
    E = 2048
    NE = E // 128
    x2f = p.sbuf("la_x2f", [128, 8 * T], BF16, is_transpose_dest=True)
    statb = p.sbuf("la_statb", [128, 2 * NT], F32)
    grow = p.sbuf("la_grow", [128, D], F32)
    brow = p.sbuf("la_brow", [128, D], F32)
    borow = p.sbuf("la_borow", [128, D], F32)
    p.dma("qsp", grow[:, :], lng.t[:].partition_broadcast(128), rd=[lng], wr=[grow])
    p.dma("qsp", brow[:, :], lnb.t[:].partition_broadcast(128), rd=[lnb], wr=[brow])
    p.dma("qsp", borow[:, :], b_out.t[:].partition_broadcast(128), rd=[b_out], wr=[borow])

    lnp = [dict(
        xt=p.sbuf("la_xt", [128, D], F32), sq=p.sbuf("la_sq", [128, D], F32),
        s=p.sbuf("la_s", [128, 1], F32), mu=p.sbuf("la_mu", [128, 1], F32),
        ss=p.sbuf("la_ss", [128, 1], F32), var=p.sbuf("la_var", [128, 1], F32),
        rstd=p.sbuf("la_rstd", [128, 1], F32), xln=p.sbuf("la_xln", [128, D], F32),
        x2=p.sbuf("la_x2", [128, D], F32), x2b=p.sbuf("la_x2b", [128, D], BF16),
    ) for _ in range(1)]
    epsc = p.cst(1e-5)

    def stats(e, src, sfx=""):
        xt, sq, s, mu, ss, var, rstd = (e[k] for k in ("xt", "sq", "s", "mu", "ss", "var", "rstd"))
        p.op("dve", lambda v: v.tensor_reduce(s[:, :], src[:, :], AX.X, ALU.add), rd=[src], wr=[s])
        p.op("act", lambda a: a.activation(sq[:, :], src[:, :], AF.Square, accum_out=ss[:, :]),
             rd=[src], wr=[sq, ss])
        p.op("act", lambda a: a.copy(ss[:, :], ss[:, :]), rd=[ss], wr=[ss])
        p.op("dve", lambda v: v.tensor_scalar(mu[:, :], s[:, :], 1.0 / D, None, ALU.mult, ALU.bypass),
             rd=[s], wr=[mu])
        p.op("dve", lambda v: v.tensor_tensor(var[:, :], mu[:, :], mu[:, :], ALU.mult),
             rd=[mu], wr=[var])
        p.op("dve", lambda v: v.scalar_tensor_tensor(var[:, :], ss[:, :], 1.0 / D, var[:, :],
                                                     ALU.mult, ALU.subtract), rd=[ss, var], wr=[var])
        p.op("act", lambda a: a.activation(rstd[:, :], var[:, :], AF.Sqrt, bias=epsc[:, :]),
             rd=[var, epsc], wr=[rstd])
        p.op("dve", lambda v: v.reciprocal(rstd[:, :], rstd[:, :]), rd=[rstd], wr=[rstd])

    for i in range(NT):
        e = lnp[0]
        xt, mu, rstd, xln, x2, x2b = (e[k] for k in ("xt", "mu", "rstd", "xln", "x2", "x2b"))
        p.dma("qsp", xt[:, :], x.t[i * 128:(i + 1) * 128, :], rd=[x], wr=[xt])
        stats(e, xt)
        p.op("dve", lambda v, i=i: v.tensor_copy(statb.t[:, i:i + 1], mu[:, :]),
             rd=[mu], wr=[statb])
        p.op("dve", lambda v, i=i: v.tensor_copy(statb.t[:, NT + i:NT + i + 1], rstd[:, :]),
             rd=[rstd], wr=[statb])
        p.op("dve", lambda v: v.tensor_scalar(xln[:, :], xt[:, :], mu[:, :], rstd[:, :],
                                              ALU.subtract, ALU.mult), rd=[xt, mu, rstd], wr=[xln])
        stats(e, xln)
        p.op("dve", lambda v: v.tensor_scalar(x2[:, :], xln[:, :], mu[:, :], rstd[:, :],
                                              ALU.subtract, ALU.mult), rd=[xln, mu, rstd], wr=[x2])
        p.op("dve", lambda v: v.tensor_tensor(x2[:, :], x2[:, :], grow[:, :], ALU.mult),
             rd=[x2, grow], wr=[x2])
        p.op("dve", lambda v: v.tensor_tensor(x2b[:, :], x2[:, :], brow[:, :], ALU.add),
             rd=[x2, brow], wr=[x2b])
        for c in range(8):
            p.dma_t(("qsp", "qact")[c % 2],
                    x2f.t[:, c * T + i * 128: c * T + (i + 1) * 128],
                    x2b[:, c * 128:(c + 1) * 128], rd=[x2b], wr=[x2f])

    bic = p.sbuf("la_bic", [128, 32], F32)
    for c in range(32):
        col_load(p, "qsp", bic.t[:, c:c + 1], b_in.t[c * 128:(c + 1) * 128], [b_in], [bic])
    cbc = p.sbuf("la_cbc", [128, NE], F32)
    for c in range(NE):
        col_load(p, "qsp", cbc.t[:, c:c + 1], convb.t[c * 128:(c + 1) * 128], [convb], [cbc])
    bcc = p.sbuf("la_bcc", [128, NE], F32)
    bdc = p.sbuf("la_bdc", [128, NE], F32)
    for c in range(NE):
        col_load(p, "qsp", bcc.t[:, c:c + 1], b_c.t[c * 128:(c + 1) * 128], [b_c], [bcc])
        col_load(p, "qsp", bdc.t[:, c:c + 1], b_d.t[c * 128:(c + 1) * 128], [b_d], [bdc])
    ldc = p.sbuf("la_ldc", [128, NE], F32)
    for c in range(NE):
        col_load(p, "qsp", ldc.t[:, c:c + 1], log_delta.t[c * 128:(c + 1) * 128],
                 [log_delta], [ldc])
    adc = p.sbuf("la_adc", [128, NE], F32)
    t1c = p.sbuf("la_t1c", [128, NE], F32)
    onec = p.cst(1.0)
    p.op("act", lambda a: a.activation(t1c[:, :], ldc[:, :], AF.Abs), rd=[ldc], wr=[t1c])
    p.op("act", lambda a: a.activation(t1c[:, :], t1c[:, :], AF.Exp, scale=-1.0),
         rd=[t1c], wr=[t1c])
    p.op("act", lambda a: a.activation(t1c[:, :], t1c[:, :], AF.Ln, bias=onec[:, :]),
         rd=[t1c, onec], wr=[t1c])
    p.op("dve", lambda v: v.scalar_tensor_tensor(adc[:, :], ldc[:, :], 0.0, t1c[:, :],
                                                 ALU.max, ALU.add), rd=[ldc, t1c], wr=[adc])
    p.op("act", lambda a: a.activation(adc[:, :], adc[:, :], AF.Exp, scale=-1.0),
         rd=[adc], wr=[adc])
    cbsum = p.sbuf("la_cbsum", [128, NE], F32)
    abt = p.sbuf("la_abt", [128, 16], F32)
    abt2 = p.sbuf("la_abt2", [128, 16], F32)
    for c in range(NE):
        p.dma("qsp", abt[:, :], A.t[c * 128:(c + 1) * 128, :], rd=[A], wr=[abt])
        p.dma("qact", abt2[:, :], Bm.t[c * 128:(c + 1) * 128, :], rd=[Bm], wr=[abt2])
        p.op("dve", lambda v: v.tensor_tensor(abt[:, :], abt[:, :], abt2[:, :], ALU.mult),
             rd=[abt, abt2], wr=[abt])
        p.op("dve", lambda v, c=c: v.tensor_reduce(cbsum.t[:, c:c + 1], abt[:, :], AX.X, ALU.add),
             rd=[abt], wr=[cbsum])
    cwc = p.sbuf("la_cwc", [128, NE * 4], F32)
    for c in range(NE):
        p.dma("qsp", cwc.t[:, c * 4:(c + 1) * 4], convw.t[c * 128:(c + 1) * 128, :],
              rd=[convw], wr=[cwc])

    xcd = p.dram("la_xcd", [E, T], BF16)
    sgd = p.dram("la_sgd", [E, T], BF16)
    cvf = p.sbuf("la_cvf", [128, E], F32)
    cvb = p.sbuf("la_cvb", [128, E], BF16)

    xbc = p.sbuf("la_xbc", [128, T], F32)
    acc = p.sbuf("la_acc", [128, T], F32)
    xcc = p.sbuf("la_xcc", [128, T], BF16)
    for ec in range(32):
        for k in range(8):
            p.dma("qact", cvf.t[:, k * 128:(k + 1) * 128],
                  w_inT.t[k * 128:(k + 1) * 128, ec * 128:(ec + 1) * 128],
                  rd=[w_inT], wr=[cvf])
        p.op("act", lambda a: a.copy(cvb[:, 0:1024], cvf[:, 0:1024]), rd=[cvf], wr=[cvb])
        for n in range(8):
            ps = p.psum()
            for k in range(8):
                p.op("pe", lambda t, n=n, k=k, ps=ps: t.matmul(
                    ps[:, 0:512], cvb[:, k * 128:(k + 1) * 128],
                    x2f[:, k * T + n * 512:k * T + (n + 1) * 512],
                    start=(k == 0), stop=(k == 7)), rd=[cvb, x2f], wr=[ps], tag="inproj")
            if ec < 16:
                p.op("dve", lambda v, ps=ps, ec=ec, n=n: v.tensor_scalar(
                    xbc.t[:, n * 512:(n + 1) * 512], ps[:, 0:512], bic.t[:, ec:ec + 1],
                    None, ALU.add, ALU.bypass), rd=[ps, bic], wr=[xbc], tag="xbev")
            else:
                p.op("act", lambda a, ps=ps, ec=ec, n=n: a.activation(
                    xcc.t[:, n * 512:(n + 1) * 512], ps[:, 0:512], AF.Sigmoid,
                    bias=bic.t[:, ec:ec + 1]), rd=[ps, bic], wr=[xcc], tag="sgev")
        if ec < 16:
            p.op("dve", lambda v, ec=ec: v.tensor_scalar(
                acc[:, :], xbc[:, :], cwc.t[:, ec * 4 + 3: ec * 4 + 4], None,
                ALU.mult, ALU.bypass), rd=[xbc, cwc], wr=[acc])
            for j in range(3):
                d = 3 - j
                p.op("dve", lambda v, ec=ec, j=j, d=d: v.scalar_tensor_tensor(
                    acc[:, d:T], xbc[:, 0:T - d], cwc.t[:, ec * 4 + j: ec * 4 + j + 1],
                    acc[:, d:T], ALU.mult, ALU.add), rd=[xbc, cwc, acc], wr=[acc])
            p.op("act", lambda a, ec=ec: a.activation(
                xcc[:, :], acc[:, :], AF.Silu, bias=cbc.t[:, ec:ec + 1]),
                 rd=[acc, cbc], wr=[xcc])
            p.dma("qpool", xcd.t[ec * 128:(ec + 1) * 128, :], xcc[:, :], rd=[xcc], wr=[xcd])
        else:
            p.dma("qpool", sgd.t[(ec - 16) * 128:(ec - 15) * 128, :], xcc[:, :],
                  rd=[xcc], wr=[sgd])

    wcbd = p.dram("la_wcbd", [NE * 128, E], BF16)
    wdbd = p.dram("la_wdbd", [NE * 128, E], BF16)
    for src, dst in ((w_cT, wcbd), (w_dT, wdbd)):
        for k in range(NE):
            p.dma("qsp", cvf[:, :], src.t[k * 128:(k + 1) * 128, :], rd=[src], wr=[cvf])
            p.op("act", lambda a: a.copy(cvb[:, :], cvf[:, :]), rd=[cvf], wr=[cvb])
            for m in range(NE):
                p.dma(("qpool", "qact")[m % 2],
                      dst.t[m * 128:(m + 1) * 128, k * 128:(k + 1) * 128],
                      cvb[:, m * 128:(m + 1) * 128], rd=[cvb], wr=[dst])

    ygd = p.dram("la_ygd", [E, T], BF16)
    SW = 512
    NS = T // SW
    xcs = p.sbuf("la_xcs", [128, NE * SW], BF16)
    sgs = p.sbuf("la_sgs", [128, NE * SW], BF16)
    wcp2 = [p.sbuf("la_wcp", [128, NE * 128], BF16) for _ in range(2)]
    wdp2 = [p.sbuf("la_wdp", [128, NE * 128], BF16) for _ in range(1)]
    sig = p.sbuf("la_sig", [128, SW], F32)
    uu = p.sbuf("la_uu", [128, SW], F32)
    yy = p.sbuf("la_yy", [128, SW], F32)
    ygm = p.sbuf("la_ygm", [128, SW], BF16)
    stcol = p.sbuf("la_stcol", [128, NE], F32)
    p.op("dve", lambda v: v.memset(stcol[:, :], 0.0), wr=[stcol])
    adrep = p.sbuf("la_adrep", [128, SW], F32)
    onesw = p.sbuf("la_onesw", [128, SW], F32)
    p.op("dve", lambda v: v.memset(onesw[:, :], 1.0), wr=[onesw])
    for si in range(NS):
        t0 = si * SW
        p.dma("qsp", xcs.t[:].rearrange("p (c n) -> p c n", c=NE),
              xcd.t[:, t0:t0 + SW].rearrange("(c p) n -> p c n", p=128),
              rd=[xcd], wr=[xcs])
        p.dma("qact", sgs.t[:].rearrange("p (c n) -> p c n", c=NE),
              sgd.t[:, t0:t0 + SW].rearrange("(c p) n -> p c n", p=128),
              rd=[sgd], wr=[sgs])
        for m in range(NE):
            wcp, wdp = wcp2[m % 2], wdp2[0]
            p.dma("qsp", wcp.t[:, :], wcbd.t[m * 128:(m + 1) * 128, :], rd=[wcbd], wr=[wcp])
            p.dma("qact", wdp.t[:, :], wdbd.t[m * 128:(m + 1) * 128, :], rd=[wdbd], wr=[wdp])
            psC = p.psum()
            for k in range(NE):
                p.op("pe", lambda t, k=k, ps=psC: t.matmul(
                    ps[:, 0:SW], wcp[:, k * 128:(k + 1) * 128], xcs[:, k * SW:(k + 1) * SW],
                    start=(k == 0), stop=(k == NE - 1)), rd=[wcp, xcs], wr=[psC], tag="mmC")
            psD = p.psum()
            for k in range(NE):
                p.op("pe", lambda t, k=k, ps=psD: t.matmul(
                    ps[:, 0:SW], wdp[:, k * 128:(k + 1) * 128], xcs[:, k * SW:(k + 1) * SW],
                    start=(k == 0), stop=(k == NE - 1)), rd=[wdp, xcs], wr=[psD], tag="mmD")
            p.op("act", lambda a, ps=psD, m=m: a.activation(
                sig[:, :], ps[:, 0:SW], AF.Sigmoid, bias=bdc.t[:, m:m + 1]),
                 rd=[psD, bdc], wr=[sig])
            p.op("dve", lambda v, ps=psC, m=m: v.tensor_scalar(
                uu[:, :], ps[:, 0:SW], bcc.t[:, m:m + 1], None, ALU.add, ALU.bypass),
                 rd=[psC, bcc], wr=[uu])
            p.op("dve", lambda v: v.tensor_tensor(uu[:, :], uu[:, :], sig[:, :], ALU.mult),
                 rd=[uu, sig], wr=[uu])
            p.op("dve", lambda v, m=m: v.tensor_scalar(
                adrep[:, :], onesw[:, :], adc.t[:, m:m + 1], None, ALU.mult, ALU.bypass),
                 rd=[onesw, adc], wr=[adrep])
            if si == 0:
                p.op("dve", lambda v, m=m: v.tensor_tensor_scan(
                    yy[:, :], adrep[:, :], uu[:, :], 0.0,
                    ALU.mult, ALU.add), rd=[adrep, uu], wr=[yy], tag="scan")
            else:
                p.op("dve", lambda v, m=m: v.tensor_tensor_scan(
                    yy[:, :], adrep[:, :], uu[:, :],
                    stcol.t[:, m:m + 1], ALU.mult, ALU.add),
                     rd=[adrep, uu, stcol], wr=[yy], tag="scan")
            p.op("dve", lambda v, m=m: v.tensor_copy(stcol.t[:, m:m + 1], yy[:, SW - 1:SW]),
                 rd=[yy], wr=[stcol])
            p.op("dve", lambda v, m=m: v.tensor_scalar(
                yy[:, :], yy[:, :], cbsum.t[:, m:m + 1], None, ALU.mult, ALU.bypass),
                 rd=[yy, cbsum], wr=[yy])
            p.op("dve", lambda v, m=m: v.tensor_tensor(
                ygm[:, :], yy[:, :], sgs[:, m * SW:(m + 1) * SW],
                ALU.mult), rd=[yy, sgs], wr=[ygm])
            p.dma("qpool", ygd.t[m * 128:(m + 1) * 128, t0:t0 + SW], ygm[:, :],
                  rd=[ygm], wr=[ygd])

    wob = x2f
    for k in range(NE):
        p.dma("qsp", cvf[:, 0:1024], w_outT.t[k * 128:(k + 1) * 128, :], rd=[w_outT], wr=[cvf])
        p.op("act", lambda a, k=k: a.copy(wob.t[:, k * 1024:(k + 1) * 1024], cvf[:, 0:1024]),
             rd=[cvf], wr=[wob])
    ygt = p.sbuf("la_ygt", [128, NE * 128], BF16)
    xt3 = Buf("la_xt3v", xbc.t, root=xbc)
    xlr = xt3
    class _V:
        pass
    for i in range(NT):
        p.dma("qsp", ygt.t[:].rearrange("p (c n) -> p c n", c=NE),
              ygd.t[:, i * 128:(i + 1) * 128].rearrange("(c p) n -> p c n", p=128),
              rd=[ygd], wr=[ygt])
        p.dma("qact", xt3.t[:, 0:D], x.t[i * 128:(i + 1) * 128, :], rd=[x], wr=[xt3])
        p.op("dve", lambda v, i=i: v.tensor_scalar(
            xt3.t[:, D:2 * D], xt3.t[:, 0:D], statb.t[:, i:i + 1],
            statb.t[:, NT + i:NT + i + 1],
            ALU.subtract, ALU.mult), rd=[xt3, statb], wr=[xlr])
        p.op("dve", lambda v: v.tensor_tensor(
            xt3.t[:, 0:D], xt3.t[:, 0:D], xt3.t[:, D:2 * D], ALU.add),
             rd=[xt3, xlr], wr=[xt3])
        for c in range(2):
            ps = p.psum()
            for k in range(NE):
                p.op("pe", lambda t, k=k, c=c, ps=ps: t.matmul(
                    ps[:, 0:512], ygt[:, k * 128:(k + 1) * 128],
                    wob[:, k * 1024 + c * 512: k * 1024 + (c + 1) * 512],
                    start=(k == 0), stop=(k == NE - 1)), rd=[ygt, wob], wr=[ps], tag="outproj")
            p.op("dve", lambda v, ps=ps, c=c: v.tensor_tensor(
                xt3.t[:, D + c * 512:D + (c + 1) * 512], ps[:, 0:512],
                borow[:, c * 512:(c + 1) * 512],
                ALU.add), rd=[ps, borow], wr=[xlr])
            p.op("dve", lambda v, c=c: v.tensor_tensor(
                xt3.t[:, c * 512:(c + 1) * 512], xt3.t[:, c * 512:(c + 1) * 512],
                xt3.t[:, D + c * 512:D + (c + 1) * 512], ALU.add), rd=[xlr, xt3], wr=[xt3])
        p.dma("qpool", xout.t[i * 128:(i + 1) * 128, :], xt3.t[:, 0:D], rd=[xt3], wr=[xout])


def build_l1attna(nc):
    p = Prog(nc)
    x = Buf("x", nc.dram_tensor("x", [T, D], F32, kind="ExternalInput").ap())
    names = dict(lng=[D], lnb=[D], w_inT=[D, 4096], b_in=[4096], convw=[2048, 4],
                 convb=[2048], A=[2048, 16], Bm=[2048, 16], w_cT=[2048, 2048],
                 b_c=[2048], w_dT=[2048, 2048], b_d=[2048], log_delta=[2048],
                 w_outT=[2048, D], b_out=[D])
    bufs = {k: Buf(k, nc.dram_tensor(k, s, F32, kind="ExternalInput").ap())
            for k, s in names.items()}
    xout = Buf("xout", nc.dram_tensor("xout", [T, D], F32, kind="ExternalOutput").ap())
    p.make_psum_pool(6, 2)
    emit_l1attna(p, x, xout, **bufs)
    p.emit()
    return p



def emit_mamba(p, x, xout, in_proj_w, conv_w, conv_b, x_proj_w, dt_proj_w,
               dt_proj_b, A_log, D_param, out_proj_w):
    E = 2048
    NE = 16
    NSTATE = 16
    xlnf = p.sbuf("mb_xlnf", [128, 8 * T], BF16, is_transpose_dest=True)
    ln_into_fm(p, x, xlnf, pool_n=2)

    cbc = p.sbuf("mb_cbc", [128, NE], F32)
    btc = p.sbuf("mb_btc", [128, NE], F32)
    Dc = p.sbuf("mb_Dc", [128, NE], F32)
    for c in range(NE):
        col_load(p, "qsp", cbc.t[:, c:c + 1], conv_b.t[c * 128:(c + 1) * 128], [conv_b], [cbc])
        col_load(p, "qsp", btc.t[:, c:c + 1], dt_proj_b.t[c * 128:(c + 1) * 128],
                 [dt_proj_b], [btc])
        col_load(p, "qsp", Dc.t[:, c:c + 1], D_param.t[c * 128:(c + 1) * 128], [D_param], [Dc])
    cwc = p.sbuf("mb_cwc", [128, NE * 4], F32)
    for c in range(NE):
        p.dma("qsp", cwc.t[:, c * 4:(c + 1) * 4], conv_w.t[c * 128:(c + 1) * 128, :],
              rd=[conv_w], wr=[cwc])
    af = p.sbuf("mb_af", [128, NE * NSTATE], F32)
    for c in range(NE):
        p.dma("qact", af.t[:, c * 16:(c + 1) * 16], A_log.t[c * 128:(c + 1) * 128, :],
              rd=[A_log], wr=[af])
    p.op("act", lambda a: a.activation(af[:, :], af[:, :], AF.Exp), rd=[af], wr=[af])
    p.op("dve", lambda v: v.tensor_scalar(af[:, :], af[:, :], -1.0, None, ALU.mult, ALU.bypass),
         rd=[af], wr=[af])
    xpwb = p.sbuf("mb_xpwb", [128, NE * 96], BF16)
    tmpl = p.sbuf("mb_tmpl", [128, 2048], F32)
    for c in range(NE):
        p.dma("qsp", tmpl[:, 0:96], x_proj_w.t[c * 128:(c + 1) * 128, :], rd=[x_proj_w],
              wr=[tmpl])
        p.op("act", lambda a, c=c: a.copy(xpwb.t[:, c * 96:(c + 1) * 96], tmpl[:, 0:96]),
             rd=[tmpl], wr=[xpwb])
    dtwf = p.sbuf("mb_dtwf", [64, E], F32)
    p.dma("qsp", dtwf[:, :], dt_proj_w.t[:, :], rd=[dt_proj_w], wr=[dtwf])

    xcd = p.dram("mb_xcd", [E, T], F32)
    szd = p.dram("mb_szd", [E, T], F32)
    ygd2 = p.dram("mb_ygd", [E, T], BF16)
    panf = p.sbuf("mb_panf", [128, 1024], F32)
    panb = p.sbuf("mb_panb", [128, 1024], BF16)
    xbc = p.sbuf("mb_xbc", [128, T], F32)
    acc = p.sbuf("mb_acc", [128, T], F32)
    xcc = p.sbuf("mb_xcc", [128, T], BF16)
    out96 = p.sbuf("mb_out96", [96, T], F32)
    for ec in range(32):
        p.dma("qact", panf.t[:].rearrange("p (c n) -> p c n", c=8),
              in_proj_w.t[:, ec * 128:(ec + 1) * 128].rearrange("(c p) n -> p c n", p=128),
              rd=[in_proj_w], wr=[panf])
        p.op("act", lambda a: a.copy(panb[:, :], panf[:, :]), rd=[panf], wr=[panb])
        for n in range(8):
            ps = p.psum()
            for k in range(8):
                p.op("pe", lambda t, n=n, k=k, ps=ps: t.matmul(
                    ps[:, 0:512], panb[:, k * 128:(k + 1) * 128],
                    xlnf[:, k * T + n * 512:k * T + (n + 1) * 512],
                    start=(k == 0), stop=(k == 7)), rd=[panb, xlnf], wr=[ps], tag="inproj")
            if ec < 16:
                p.op("dve", lambda v, ps=ps, n=n: v.tensor_copy(
                    xbc.t[:, n * 512:(n + 1) * 512], ps[:, 0:512]), rd=[ps], wr=[xbc])
            else:
                p.op("act", lambda a, ps=ps, n=n: a.activation(
                    xbc.t[:, n * 512:(n + 1) * 512], ps[:, 0:512], AF.Silu),
                     rd=[ps], wr=[xbc], tag="zev")
        if ec < 16:
            p.op("dve", lambda v, ec=ec: v.tensor_scalar(
                acc[:, :], xbc[:, :], cwc.t[:, ec * 4 + 3: ec * 4 + 4], None,
                ALU.mult, ALU.bypass), rd=[xbc, cwc], wr=[acc])
            for j in range(3):
                d = 3 - j
                p.op("dve", lambda v, ec=ec, j=j, d=d: v.scalar_tensor_tensor(
                    acc[:, d:T], xbc[:, 0:T - d], cwc.t[:, ec * 4 + j: ec * 4 + j + 1],
                    acc[:, d:T], ALU.mult, ALU.add), rd=[xbc, cwc, acc], wr=[acc])
            p.op("act", lambda a, ec=ec: a.activation(
                acc[:, :], acc[:, :], AF.Silu, bias=cbc.t[:, ec:ec + 1]),
                 rd=[acc, cbc], wr=[acc])
            p.op("dve", lambda v: v.tensor_copy(xcc[:, :], acc[:, :]), rd=[acc], wr=[xcc])
            p.dma("qpool", xcd.t[ec * 128:(ec + 1) * 128, :], acc[:, :], rd=[acc], wr=[xcd])
            for n in range(8):
                ps = p.psum()
                p.op("pe", lambda t, ec=ec, n=n, ps=ps: t.matmul(
                    ps[0:96, 0:512], xpwb[:, ec * 96:(ec + 1) * 96],
                    xcc[:, n * 512:(n + 1) * 512], start=True, stop=True),
                     rd=[xpwb, xcc], wr=[ps], tag="xproj")
                if ec == 0:
                    p.op("dve", lambda v, ps=ps, n=n: v.tensor_copy(
                        out96.t[:, n * 512:(n + 1) * 512], ps[0:96, 0:512]),
                         rd=[ps], wr=[out96])
                else:
                    p.op("dve", lambda v, ps=ps, n=n: v.tensor_tensor(
                        out96.t[:, n * 512:(n + 1) * 512],
                        out96.t[:, n * 512:(n + 1) * 512], ps[0:96, 0:512], ALU.add),
                         rd=[ps, out96], wr=[out96])
        else:
            p.dma("qpool", szd.t[(ec - 16) * 128:(ec - 15) * 128, :], xbc[:, :],
                  rd=[xbc], wr=[szd])

    bcd = p.dram("mb_bcd", [32, T], F32)
    p.dma("qsp", bcd.t[0:16, :], out96.t[64:80, :], rd=[out96], wr=[bcd])
    p.dma("qsp", bcd.t[16:32, :], out96.t[80:96, :], rd=[out96], wr=[bcd])


    opwbf = p.dram("mb_opwbf", [E, D], BF16)
    for k in range(NE):
        p.dma("qsp", tmpl[:, 0:1024], out_proj_w.t[k * 128:(k + 1) * 128, :],
              rd=[out_proj_w], wr=[tmpl])
        p.op("act", lambda a: a.copy(panb[:, 0:1024], tmpl[:, 0:1024]), rd=[tmpl], wr=[panb])
        p.dma("qpool", opwbf.t[k * 128:(k + 1) * 128, :], panb[:, 0:1024],
              rd=[panb], wr=[opwbf])

    SW = 512
    NS = T // SW
    MB = 8
    xvf = xlnf.t[:].bitcast(F32)
    xcs = Buf("mb_xcs", xlnf.t, root=xlnf)
    szs = Buf("mb_szs", xlnf.t, root=xlnf)
    XC0, SZ0 = 0, NE * SW
    dtm_all = p.sbuf("mb_dtmall", [128, MB * SW], F32)
    P_all = Buf("mb_Pall", xbc.t, root=xbc)
    yacc_all = Buf("mb_yaccall", acc.t, root=acc)
    sp1 = p.sbuf("mb_sp1", [128, SW], F32)
    dafs = [p.sbuf("mb_daf", [128, SW], F32) for _ in range(2)]
    dbufs = [p.sbuf("mb_dbuf", [128, SW], F32) for _ in range(2)]
    stfs = [p.sbuf("mb_stf", [128, SW], F32) for _ in range(2)]
    breps = [p.sbuf("mb_brep", [128, SW], F32) for _ in range(1)]
    creps = [p.sbuf("mb_crep", [128, SW], F32) for _ in range(1)]
    yfm = p.sbuf("mb_yfm", [128, SW], BF16)
    stcols = p.sbuf("mb_stcols", [128, NE * NSTATE], F32)
    p.op("dve", lambda v: v.memset(stcols[:, :], 0.0), wr=[stcols])
    for si in range(NS):
        t0 = si * SW
        p.dma("qsp", xvf[:, XC0:XC0 + NE * SW].rearrange("p (c n) -> p c n", c=NE),
              xcd.t[:, t0:t0 + SW].rearrange("(c p) n -> p c n", p=128), rd=[xcd], wr=[xcs])
        p.dma("qact", xvf[:, SZ0:SZ0 + NE * SW].rearrange("p (c n) -> p c n", c=NE),
              szd.t[:, t0:t0 + SW].rearrange("(c p) n -> p c n", p=128), rd=[szd], wr=[szs])
        for mb in range(NE // MB):
            for mi in range(MB):
                m = mb * MB + mi
                ps = p.psum()
                p.op("pe", lambda t, m=m, t0=t0, ps=ps: t.matmul(
                    ps[:, 0:SW], dtwf[:, m * 128:(m + 1) * 128], out96.t[0:64, t0:t0 + SW],
                    start=True, stop=True), rd=[dtwf, out96], wr=[ps], tag="dtmm")
                p.op("act", lambda a, ps=ps, m=m: a.activation(
                    sp1[:, :], ps[:, 0:SW], AF.Abs, bias=btc.t[:, m:m + 1]),
                     rd=[ps, btc], wr=[sp1])
                p.op("act", lambda a: a.activation(sp1[:, :], sp1[:, :], AF.Exp, scale=-1.0),
                     rd=[sp1], wr=[sp1])
                onec = p.cst(1.0)
                p.op("act", lambda a, onec=onec: a.activation(
                    sp1[:, :], sp1[:, :], AF.Ln, bias=onec[:, :]), rd=[sp1, onec], wr=[sp1])
                p.op("act", lambda a, ps=ps, m=m, mi=mi: a.activation(
                    dtm_all.t[:, mi * SW:(mi + 1) * SW], ps[:, 0:SW], AF.Relu,
                    bias=btc.t[:, m:m + 1]), rd=[ps, btc], wr=[dtm_all])
                p.op("dve", lambda v, mi=mi: v.tensor_tensor(
                    dtm_all.t[:, mi * SW:(mi + 1) * SW], sp1[:, :],
                    dtm_all.t[:, mi * SW:(mi + 1) * SW], ALU.add),
                     rd=[sp1, dtm_all], wr=[dtm_all])
                p.op("pool", lambda g, m=m, mi=mi: g.tensor_tensor(
                    xbc.t[:, mi * SW:(mi + 1) * SW], dtm_all.t[:, mi * SW:(mi + 1) * SW],
                    xvf[:, XC0 + m * SW:XC0 + (m + 1) * SW], ALU.mult),
                     rd=[dtm_all, xcs], wr=[P_all])
            for ss in range(NSTATE):
                br, cr = breps[0], creps[0]
                p.dma("qsp", br[:, :], bcd.t[ss, t0:t0 + SW].partition_broadcast(128),
                      rd=[bcd], wr=[br])
                p.dma("qact", cr[:, :], bcd.t[16 + ss, t0:t0 + SW].partition_broadcast(128),
                      rd=[bcd], wr=[cr])
                for mi in range(MB):
                    m = mb * MB + mi
                    da, db, st = dafs[mi % 2], dbufs[mi % 2], stfs[mi % 2]
                    p.op("act", lambda a, m=m, mi=mi, ss=ss, da=da: a.activation(
                        da[:, :], dtm_all.t[:, mi * SW:(mi + 1) * SW], AF.Exp,
                        scale=af.t[:, m * 16 + ss:m * 16 + ss + 1]),
                         rd=[dtm_all, af], wr=[da])
                    p.op("pool", lambda g, mi=mi, db=db, br=br: g.tensor_tensor(
                        db[:, :], xbc.t[:, mi * SW:(mi + 1) * SW], br[:, :], ALU.mult),
                         rd=[P_all, br], wr=[db])
                    if si == 0:
                        p.op("dve", lambda v, da=da, db=db, st=st: v.tensor_tensor_scan(
                            st[:, :], da[:, :], db[:, :], 0.0, ALU.mult, ALU.add),
                             rd=[da, db], wr=[st], tag="scan")
                    else:
                        p.op("dve", lambda v, da=da, db=db, st=st, m=m, ss=ss: v.tensor_tensor_scan(
                            st[:, :], da[:, :], db[:, :],
                            stcols.t[:, m * 16 + ss:m * 16 + ss + 1], ALU.mult, ALU.add),
                             rd=[da, db, stcols], wr=[st], tag="scan")
                    p.op("dve", lambda v, m=m, ss=ss, st=st: v.tensor_copy(
                        stcols.t[:, m * 16 + ss:m * 16 + ss + 1], st[:, SW - 1:SW]),
                         rd=[st], wr=[stcols])
                    p.op("pool", lambda g, st=st, cr=cr: g.tensor_tensor(
                        st[:, :], st[:, :], cr[:, :], ALU.mult), rd=[st, cr], wr=[st])
                    if ss == 0:
                        p.op("dve", lambda v, mi=mi, st=st: v.tensor_copy(
                            acc.t[:, mi * SW:(mi + 1) * SW], st[:, :]),
                             rd=[st], wr=[yacc_all])
                    else:
                        p.op("dve", lambda v, mi=mi, st=st: v.tensor_tensor(
                            acc.t[:, mi * SW:(mi + 1) * SW],
                            acc.t[:, mi * SW:(mi + 1) * SW], st[:, :], ALU.add),
                             rd=[yacc_all, st], wr=[yacc_all])
            for mi in range(MB):
                m = mb * MB + mi
                p.op("dve", lambda v, m=m, mi=mi: v.scalar_tensor_tensor(
                    acc.t[:, mi * SW:(mi + 1) * SW], xvf[:, XC0 + m * SW:XC0 + (m + 1) * SW],
                    Dc.t[:, m:m + 1], acc.t[:, mi * SW:(mi + 1) * SW], ALU.mult, ALU.add),
                     rd=[xcs, Dc, yacc_all], wr=[yacc_all])
                p.op("dve", lambda v, m=m, mi=mi: v.tensor_tensor(
                    yfm[:, :], acc.t[:, mi * SW:(mi + 1) * SW],
                    xvf[:, SZ0 + m * SW:SZ0 + (m + 1) * SW], ALU.mult),
                     rd=[yacc_all, szs], wr=[yfm])
                p.dma("qpool", ygd2.t[m * 128:(m + 1) * 128, t0:t0 + SW], yfm[:, :],
                      rd=[yfm], wr=[ygd2])
    ygt2 = Buf("mb_ygt", xcc.t, root=xcc)
    ops2 = [p.sbuf("mb_ops", [128, 1024], BF16) for _ in range(2)]
    xt4 = Buf("mb_xt4", tmpl.t, root=tmpl)
    for i in range(NT):
        p.dma("qsp", ygt2.t[:, 0:NE * 128].rearrange("p (c n) -> p c n", c=NE),
              ygd2.t[:, i * 128:(i + 1) * 128].rearrange("(c p) n -> p c n", p=128),
              rd=[ygd2], wr=[ygt2])
        p.dma("qact", xt4.t[:, 0:D], x.t[i * 128:(i + 1) * 128, :], rd=[x], wr=[xt4])
        ps0 = p.psum()
        ps1 = p.psum()
        for k in range(NE):
            ob = ops2[k % 2]
            p.dma(("qsp", "qpool")[k % 2], ob[:, :],
                  opwbf.t[k * 128:(k + 1) * 128, :], rd=[opwbf], wr=[ob])
            p.op("pe", lambda t, k=k, ps=ps0, ob=ob: t.matmul(
                ps[:, 0:512], ygt2[:, k * 128:(k + 1) * 128], ob[:, 0:512],
                start=(k == 0), stop=(k == NE - 1)), rd=[ygt2, ob], wr=[ps0], tag="outproj")
            p.op("pe", lambda t, k=k, ps=ps1, ob=ob: t.matmul(
                ps[:, 0:512], ygt2[:, k * 128:(k + 1) * 128], ob[:, 512:1024],
                start=(k == 0), stop=(k == NE - 1)), rd=[ygt2, ob], wr=[ps1], tag="outproj")
        for c, ps in ((0, ps0), (1, ps1)):
            p.op("dve", lambda v, ps=ps, c=c: v.tensor_tensor(
                xt4.t[:, c * 512:(c + 1) * 512], xt4.t[:, c * 512:(c + 1) * 512],
                ps[:, 0:512], ALU.add), rd=[ps, xt4], wr=[xt4])
        p.dma("qpool", xout.t[i * 128:(i + 1) * 128, :], xt4.t[:, 0:D], rd=[xt4], wr=[xout])


def build_mamba(nc):
    p = Prog(nc)
    x = Buf("x", nc.dram_tensor("x", [T, D], F32, kind="ExternalInput").ap())
    names = dict(in_proj_w=[D, 4096], conv_w=[2048, 4], conv_b=[2048],
                 x_proj_w=[2048, 96], dt_proj_w=[64, 2048], dt_proj_b=[2048],
                 A_log=[2048, 16], D_param=[2048], out_proj_w=[2048, D])
    bufs = {k: Buf(k, nc.dram_tensor(k, s, F32, kind="ExternalInput").ap())
            for k, s in names.items()}
    xout = Buf("xout", nc.dram_tensor("xout", [T, D], F32, kind="ExternalOutput").ap())
    p.make_psum_pool(6, 2)
    emit_mamba(p, x, xout, **bufs)
    p.emit()
    return p



def retnet_tables():
    H, dk, dv, half = 4, 256, 512, 128
    scale = 1.0 / np.sqrt(dk)
    logdec = np.log1p(-np.exp2(-5.0 - np.arange(H, dtype=np.float64)))
    gam = np.exp(logdec)
    inv_freq = 1.0 / 10000.0 ** (np.arange(0, dk, 2, dtype=np.float64) / dk)
    t = np.arange(T, dtype=np.float64)
    ang = inv_freq[:, None] * t[None, :]
    cosT = np.cos(ang).astype(np.float32)
    sinT = np.sin(ang).astype(np.float32)
    i_ = np.arange(128, dtype=np.float64)
    m0 = np.zeros((128, 4 * 128), np.float64)
    grow = np.zeros((128, 4 * 128), np.float64)
    kdec = np.zeros((128, 4), np.float64)
    for h in range(H):
        diff = i_[None, :] - i_[:, None]
        m0[:, h * 128:(h + 1) * 128] = np.where(diff >= 0, scale * gam[h] ** diff, 0.0)
        grow[:, h * 128:(h + 1) * 128] = scale * gam[h] ** (i_[None, :] + 1.0)
        kdec[:, h] = gam[h] ** (127.0 - i_)
    g128 = gam ** 128.0
    return (cosT, sinT, m0.astype(np.float32), grow.astype(np.float32),
            kdec.astype(np.float32), [float(v) for v in g128])


def emit_retnet(p, x, xout, wqT, wkT, wvT, wgT, woT, gnw, cosT, sinT, m0, grow, kdec,
                g128, ident):
    H, dv = 4, 512
    xlnf = p.sbuf("rn_xlnf", [128, 8 * T], BF16, is_transpose_dest=True)
    ln_into_fm(p, x, xlnf, pool_n=2)
    cosb = p.sbuf("rn_cosb", [128, 512], F32)
    sinb = p.sbuf("rn_sinb", [128, 512], F32)
    m0b = p.sbuf("rn_m0b", [128, 512], F32)
    grb = p.sbuf("rn_grb", [128, 512], F32)
    kdb = p.sbuf("rn_kdb", [128, 4], F32)
    gnr = p.sbuf("rn_gnr", [128, dv], F32)
    p.dma("qsp", m0b[:, :], m0.t[:, :], rd=[m0], wr=[m0b])
    p.dma("qsp", grb[:, :], grow.t[:, :], rd=[grow], wr=[grb])
    p.dma("qsp", kdb[:, :], kdec.t[:, :], rd=[kdec], wr=[kdb])
    p.dma("qsp", gnr[:, :], gnw.t[:].partition_broadcast(128), rd=[gnw], wr=[gnr])
    idb = p.sbuf("rn_idb", [128, 128], BF16)
    idf = p.sbuf("rn_idf", [128, 128], F32)
    p.dma("qsp", idf[:, :], ident.t[:, :], rd=[ident], wr=[idf])
    p.op("act", lambda a: a.copy(idb[:, :], idf[:, :]), rd=[idf], wr=[idb])

    import os as _os
    _dbg = _os.environ.get("RN_DBG", "0") == "1"
    if _dbg:
        ogd = Buf("rn_ogd", p.nc.dram_tensor("rn_ogd_d", [2048, T], BF16,
                                             kind="ExternalOutput").ap())
    else:
        ogd = p.dram("rn_ogd", [2048, T], BF16)
    if _dbg:
        qkdump = Buf("qkdump", p.nc.dram_tensor("qkdump", [8, 128, 2 * T], BF16,
                                                kind="ExternalOutput").ap())
        emit_retnet.qkdump = qkdump
        emit_retnet.ogd = ogd
    raw = p.sbuf("rn_raw", [128, 2 * T], BF16)
    qro = p.sbuf("rn_qro", [128, 2 * T], BF16)
    kro = p.sbuf("rn_kro", [128, 2 * T], BF16)
    panq = p.sbuf("rn_panq", [128, 8 * 256], BF16)
    panvg = p.sbuf("rn_panvg", [128, 8 * 512], BF16)
    pangg = p.sbuf("rn_pangg", [128, 8 * 512], BF16)
    panf = p.sbuf("rn_panf", [128, 1024], F32)
    panfA = Buf("rn_panfA", panf.t, root=panf)
    panfB = Buf("rn_panfB", panf.t, root=panf)
    Sf = p.sbuf("rn_Sf", [128, 2 * 512], F32)
    Sb = p.sbuf("rn_Sb", [128, 2 * 512], BF16)
    vt = p.sbuf("rn_vt", [128, 512], BF16)
    vdec = p.sbuf("rn_vdec", [128, 512], BF16)
    mk = p.sbuf("rn_mk", [128, 128], BF16)
    qt = p.sbuf("rn_qt", [128, 256], BF16)
    gs = p.sbuf("rn_gs", [128, 512], F32)
    sq = p.sbuf("rn_sq", [128, 512], F32)
    o1 = p.sbuf("rn_o1", [128, 512], F32)
    msc = p.sbuf("rn_msc", [128, 1], F32)
    rms = p.sbuf("rn_rms", [128, 1], F32)
    ogt = p.sbuf("rn_ogt", [128, 512], BF16)
    ogfc = p.sbuf("rn_ogfc", [128, 512], BF16, is_transpose_dest=True)
    ktm = p.sbuf("rn_ktm", [128, 256], BF16, is_transpose_dest=True)
    epsc = p.cst(1e-5)

    def load_panel(dst, wsrc, col0, ncols, q="qact"):
        for k in range(8):
            st = panfA if k % 2 == 0 else panfB
            off = (k % 2) * 512
            p.dma(q, st.t[:, off:off + ncols],
                  wsrc.t[k * 128:(k + 1) * 128, col0:col0 + ncols], rd=[wsrc], wr=[st])
            p.op("act", lambda a, k=k, ncols=ncols, st=st, off=off: a.copy(
                dst[:, k * ncols:(k + 1) * ncols], st.t[:, off:off + ncols]),
                 rd=[st], wr=[dst])

    for h in range(H):
        for which, dst in (("q", qro), ("k", kro)):
            wsrc = wqT if which == "q" else wkT
            load_panel(panq, wsrc, h * 256, 256)
            for m in range(2):
                for n in range(8):
                    ps = p.psum()
                    for k in range(8):
                        p.op("pe", lambda t, m=m, n=n, k=k, ps=ps: t.matmul(
                            ps[:, 0:512], panq[:, k * 256 + m * 128: k * 256 + m * 128 + 128],
                            xlnf[:, k * T + n * 512: k * T + (n + 1) * 512],
                            start=(k == 0), stop=(k == 7)), rd=[panq, xlnf], wr=[ps], tag="qk")
                    p.op("dve", lambda v, ps=ps, m=m, n=n: v.tensor_copy(
                        raw.t[:, m * T + n * 512: m * T + (n + 1) * 512], ps[:, 0:512]),
                         rd=[ps], wr=[raw])
            for n in range(8):
                sl = slice(n * 512, (n + 1) * 512)
                p.dma("qsp", cosb[:, :], cosT.t[:, n * 512:(n + 1) * 512], rd=[cosT], wr=[cosb])
                p.dma("qact", sinb[:, :], sinT.t[:, n * 512:(n + 1) * 512], rd=[sinT], wr=[sinb])
                r1 = raw.t[:, n * 512:(n + 1) * 512]
                r2 = raw.t[:, T + n * 512: T + (n + 1) * 512]
                p.op("dve", lambda v, r1=r1: v.tensor_tensor(
                    o1[:, 0:512], r1, cosb.t[:, :], ALU.mult), rd=[raw, cosb], wr=[o1])
                p.op("pool", lambda g, r2=r2: g.tensor_tensor(
                    sq[:, 0:512], r2, sinb.t[:, :], ALU.mult), rd=[raw, sinb], wr=[sq])
                p.op("dve", lambda v, sl=sl, dst=dst: v.tensor_tensor(
                    dst.t[:, sl], o1[:, 0:512], sq[:, 0:512], ALU.subtract),
                     rd=[o1, sq], wr=[dst])
                p.op("dve", lambda v, r1=r1: v.tensor_tensor(
                    o1[:, 0:512], r1, sinb.t[:, :], ALU.mult), rd=[raw, sinb], wr=[o1])
                p.op("pool", lambda g, r2=r2: g.tensor_tensor(
                    sq[:, 0:512], r2, cosb.t[:, :], ALU.mult), rd=[raw, cosb], wr=[sq])
                p.op("dve", lambda v, n=n, dst=dst: v.tensor_tensor(
                    dst.t[:, T + n * 512: T + (n + 1) * 512], o1[:, 0:512], sq[:, 0:512],
                    ALU.add), rd=[o1, sq], wr=[dst])
        if _dbg:
            p.dma("qpool", qkdump.t[2 * h], qro.t[:, :], rd=[qro], wr=[qkdump])
            p.dma("qpool", qkdump.t[2 * h + 1], kro.t[:, :], rd=[kro], wr=[qkdump])
        load_panel(panvg, wvT, h * 512, 512, q="qsp")
        load_panel(pangg, wgT, h * 512, 512, q="qact")
        p.op("dve", lambda v: v.memset(Sf[:, :], 0.0), wr=[Sf])
        p.op("dve", lambda v: v.memset(Sb[:, :], 0.0), wr=[Sb])
        for c in range(NT):
            csl = slice(c * 128, (c + 1) * 128)
            psV = p.psum()
            for k in range(8):
                p.op("pe", lambda t, c=c, k=k, ps=psV: t.matmul(
                    ps[:, 0:512], xlnf[:, k * T + c * 128: k * T + (c + 1) * 128],
                    panvg[:, k * 512:(k + 1) * 512], start=(k == 0), stop=(k == 7)),
                     rd=[xlnf, panvg], wr=[psV], tag="vproj")
            p.op("dve", lambda v, ps=psV: v.tensor_copy(vt[:, :], ps[:, 0:512]),
                 rd=[psV], wr=[vt])
            p.op("dve", lambda v, ps=psV, h=h: v.tensor_scalar(
                vdec[:, :], ps[:, 0:512], kdb.t[:, h:h + 1], None, ALU.mult, ALU.bypass),
                 rd=[psV, kdb], wr=[vdec])
            psS = p.psum()
            for kc in range(2):
                p.op("pe", lambda t, kc=kc, c=c, ps=psS: t.matmul(
                    ps[:, 0:128], kro[:, kc * T + c * 128: kc * T + (c + 1) * 128],
                    qro[:, kc * T + c * 128: kc * T + (c + 1) * 128],
                    start=(kc == 0), stop=(kc == 1)), rd=[kro, qro], wr=[psS], tag="scores")
            p.op("dve", lambda v, ps=psS, h=h: v.tensor_tensor(
                mk[:, :], ps[:, 0:128], m0b.t[:, h * 128:(h + 1) * 128], ALU.mult),
                 rd=[psS, m0b], wr=[mk])
            for kc in range(2):
                p.op("dve", lambda v, kc=kc, c=c, h=h: v.tensor_tensor(
                    qt.t[:, kc * 128:(kc + 1) * 128],
                    qro.t[:, kc * T + c * 128: kc * T + (c + 1) * 128],
                    grb.t[:, h * 128:(h + 1) * 128], ALU.mult), rd=[qro, grb], wr=[qt])
            psO = p.psum()
            for kc in range(2):
                p.op("pe", lambda t, kc=kc, ps=psO: t.matmul(
                    ps[:, 0:512], qt[:, kc * 128:(kc + 1) * 128], Sb[:, kc * 512:(kc + 1) * 512],
                    start=(kc == 0), stop=False), rd=[qt, Sb], wr=[psO], tag="ointer")
            p.op("pe", lambda t, ps=psO: t.matmul(
                ps[:, 0:512], mk[:, :], vt[:, :], start=False, stop=True),
                 rd=[mk, vt], wr=[psO], tag="ointra")
            p.op("act", lambda a, ps=psO: a.activation(
                sq[:, 0:512], ps[:, 0:512], AF.Square, accum_out=msc[:, :]),
                 rd=[psO], wr=[sq, msc])
            p.op("act", lambda a: a.copy(msc[:, :], msc[:, :]), rd=[msc], wr=[msc])
            p.op("dve", lambda v: v.tensor_scalar(
                rms[:, :], msc[:, :], 1.0 / dv, None, ALU.mult, ALU.bypass),
                 rd=[msc], wr=[rms])
            p.op("act", lambda a: a.activation(rms[:, :], rms[:, :], AF.Sqrt, bias=epsc[:, :]),
                 rd=[rms, epsc], wr=[rms])
            p.op("dve", lambda v: v.reciprocal(rms[:, :], rms[:, :]), rd=[rms], wr=[rms])
            psG = p.psum()
            for k in range(8):
                p.op("pe", lambda t, c=c, k=k, ps=psG: t.matmul(
                    ps[:, 0:512], xlnf[:, k * T + c * 128: k * T + (c + 1) * 128],
                    pangg[:, k * 512:(k + 1) * 512], start=(k == 0), stop=(k == 7)),
                     rd=[xlnf, pangg], wr=[psG], tag="gproj")
            p.op("act", lambda a, ps=psG: a.activation(gs[:, 0:512], ps[:, 0:512], AF.Silu),
                 rd=[psG], wr=[gs])
            p.op("dve", lambda v, ps=psO: v.scalar_tensor_tensor(
                o1[:, 0:512], ps[:, 0:512], rms[:, :], gnr[:, :], ALU.mult, ALU.mult),
                 rd=[psO, rms, gnr], wr=[o1])
            import os as _os2
            if "ogtcopy" in _os2.environ.get("RN_ABLATE", ""):
                p.op("dve", lambda v: v.tensor_copy(ogt[:, :], vt[:, :]), rd=[vt], wr=[ogt])
            else:
                p.op("dve", lambda v: v.tensor_tensor(ogt[:, :], o1[:, 0:512], gs[:, 0:512],
                                                      ALU.mult), rd=[o1, gs], wr=[ogt])
            for j in range(4):
                pb = p.psum_bf()
                p.op("pe", lambda t, j=j, pb=pb: t.transpose(
                    pb[:, 0:128], ogt[:, j * 128:(j + 1) * 128], idb[:, :]),
                     rd=[ogt, idb], wr=[pb], tag="ogtr")
                p.op("dve", lambda v, j=j, pb=pb: v.tensor_copy(
                    ogfc.t[:, j * 128:(j + 1) * 128], pb[:, 0:128]), rd=[pb], wr=[ogfc])
            for j in range(4):
                p.dma("qpool", ogd.t[h * 512 + j * 128: h * 512 + (j + 1) * 128, csl],
                      ogfc.t[:, j * 128:(j + 1) * 128], rd=[ogfc], wr=[ogd])
            for kc in range(2):
                pb = p.psum_bf()
                p.op("pe", lambda t, kc=kc, c=c, pb=pb: t.transpose(
                    pb[:, 0:128], kro.t[:, kc * T + c * 128: kc * T + (c + 1) * 128],
                    idb[:, :]), rd=[kro, idb], wr=[pb], tag="ktr")
                p.op("dve", lambda v, kc=kc, pb=pb: v.tensor_copy(
                    ktm.t[:, kc * 128:(kc + 1) * 128], pb[:, 0:128]), rd=[pb], wr=[ktm])
            for kc in range(2):
                psU = p.psum()
                p.op("pe", lambda t, kc=kc, ps=psU: t.matmul(
                    ps[:, 0:512], ktm[:, kc * 128:(kc + 1) * 128], vdec[:, :],
                    start=True, stop=True), rd=[ktm, vdec], wr=[psU], tag="supd")
                p.op("dve", lambda v, kc=kc, h=h: v.tensor_scalar(
                    Sf.t[:, kc * 512:(kc + 1) * 512], Sf.t[:, kc * 512:(kc + 1) * 512],
                    g128[h], None, ALU.mult, ALU.bypass), rd=[Sf], wr=[Sf])
                p.op("dve", lambda v, kc=kc, ps=psU: v.tensor_tensor(
                    Sf.t[:, kc * 512:(kc + 1) * 512], Sf.t[:, kc * 512:(kc + 1) * 512],
                    ps[:, 0:512], ALU.add), rd=[Sf, psU], wr=[Sf])
                p.op("act", lambda a, kc=kc: a.copy(
                    Sb.t[:, kc * 512:(kc + 1) * 512], Sf.t[:, kc * 512:(kc + 1) * 512]),
                     rd=[Sf], wr=[Sb])

    wob = Buf("rn_wob", xlnf.t, root=xlnf)
    for k in range(16):
        p.dma("qsp", panf[:, 0:1024], woT.t[k * 128:(k + 1) * 128, :], rd=[woT], wr=[panf])
        p.op("act", lambda a, k=k: a.copy(wob.t[:, k * 1024:(k + 1) * 1024], panf[:, 0:1024]),
             rd=[panf], wr=[wob])
    ogt2 = Buf("rn_ogt2", raw.t, root=raw)
    xt5 = p.sbuf("rn_xt5", [128, D], F32)
    for i in range(NT):
        p.dma("qsp", ogt2.t[:, 0:2048].rearrange("p (c n) -> p c n", c=16),
              ogd.t[:, i * 128:(i + 1) * 128].rearrange("(c p) n -> p c n", p=128),
              rd=[ogd], wr=[ogt2])
        p.dma("qact", xt5[:, :], x.t[i * 128:(i + 1) * 128, :], rd=[x], wr=[xt5])
        for cc in range(2):
            ps = p.psum()
            for k in range(16):
                p.op("pe", lambda t, k=k, cc=cc, ps=ps: t.matmul(
                    ps[:, 0:512], ogt2.t[:, k * 128:(k + 1) * 128],
                    wob.t[:, k * 1024 + cc * 512: k * 1024 + (cc + 1) * 512],
                    start=(k == 0), stop=(k == 15)), rd=[ogt2, wob], wr=[ps], tag="wo")
            p.op("dve", lambda v, ps=ps, cc=cc: v.tensor_tensor(
                xt5[:, cc * 512:(cc + 1) * 512], xt5[:, cc * 512:(cc + 1) * 512],
                ps[:, 0:512], ALU.add), rd=[ps, xt5], wr=[xt5])
        p.dma("qpool", xout.t[i * 128:(i + 1) * 128, :], xt5[:, :], rd=[xt5], wr=[xout])


def build_retnet(nc):
    p = Prog(nc)
    x = Buf("x", nc.dram_tensor("x", [T, D], F32, kind="ExternalInput").ap())
    wqT = Buf("wqT", nc.dram_tensor("wqT", [D, 1024], F32, kind="ExternalInput").ap())
    wkT = Buf("wkT", nc.dram_tensor("wkT", [D, 1024], F32, kind="ExternalInput").ap())
    wvT = Buf("wvT", nc.dram_tensor("wvT", [D, 2048], F32, kind="ExternalInput").ap())
    wgT = Buf("wgT", nc.dram_tensor("wgT", [D, 2048], F32, kind="ExternalInput").ap())
    woT = Buf("woT", nc.dram_tensor("woT", [2048, 1024], F32, kind="ExternalInput").ap())
    gnw = Buf("gnw", nc.dram_tensor("gnw", [512], F32, kind="ExternalInput").ap())
    cosT = Buf("cosT", nc.dram_tensor("cosT", [128, T], F32, kind="ExternalInput").ap())
    sinT = Buf("sinT", nc.dram_tensor("sinT", [128, T], F32, kind="ExternalInput").ap())
    m0 = Buf("m0", nc.dram_tensor("m0", [128, 512], F32, kind="ExternalInput").ap())
    grow = Buf("grow", nc.dram_tensor("grow", [128, 512], F32, kind="ExternalInput").ap())
    kdec = Buf("kdec", nc.dram_tensor("kdec", [128, 4], F32, kind="ExternalInput").ap())
    ident = Buf("ident", nc.dram_tensor("ident", [128, 128], F32, kind="ExternalInput").ap())
    xout = Buf("xout", nc.dram_tensor("xout", [T, D], F32, kind="ExternalOutput").ap())
    g128 = retnet_tables()[5]
    p.make_psum_pool(6, 2)
    emit_retnet(p, x, xout, wqT, wkT, wvT, wgT, woT, gnw, cosT, sinT, m0, grow, kdec, g128,
                ident)
    p.emit()
    return p



def _tr(a):
    return np.ascontiguousarray(np.asarray(a).T)


def kernel(**ins):
    return kernel_fused(**ins)


def kernel_multilaunch(**ins):
    xs = [np.ascontiguousarray(np.asarray(ins["L0_attn_a_x"])[b]) for b in range(N_CORES)]
    cores = list(range(N_CORES))

    cosT, sinT, m0, grow, kdec, g128 = retnet_tables()

    nc = bass.Bass("TRN2", target_bir_lowering=False, debug=False)
    build_retnet(nc)
    wmap = {"wqT": _tr(ins["L0_attn_a_Wq"]), "wkT": _tr(ins["L0_attn_a_Wk"]),
            "wvT": _tr(ins["L0_attn_a_Wv"]), "wgT": _tr(ins["L0_attn_a_Wg"]),
            "woT": _tr(ins["L0_attn_a_Wo"]), "gnw": np.asarray(ins["L0_attn_a_g_norm_w"]),
            "cosT": cosT, "sinT": sinT, "m0": m0, "grow": grow, "kdec": kdec,
            "ident": np.eye(128, dtype=np.float32)}
    res = run_bass_kernel_spmd(nc, [dict(x=xs[b], **wmap) for b in cores], core_ids=cores)
    xs = [res.results[b]["xout"] for b in cores]

    nc = bass.Bass("TRN2", target_bir_lowering=False, debug=False)
    build_moe(nc)
    wmap = {"wr": np.asarray(ins["L0_ffn_Wr"]), "w1": np.asarray(ins["L0_ffn_W1"]),
            "w2": np.asarray(ins["L0_ffn_W2"]), "w3": np.asarray(ins["L0_ffn_W3"])}
    res = run_bass_kernel_spmd(nc, [dict(x=xs[b], **wmap) for b in cores], core_ids=cores)
    xs = [res.results[b]["xout"] for b in cores]

    nc = bass.Bass("TRN2", target_bir_lowering=False, debug=False)
    build_mamba(nc)
    wmap = {"in_proj_w": np.asarray(ins["L0_attn_b_in_proj_w"]),
            "conv_w": np.ascontiguousarray(np.asarray(ins["L0_attn_b_conv_w"]).reshape(2048, 4)),
            "conv_b": np.asarray(ins["L0_attn_b_conv_b"]),
            "x_proj_w": np.asarray(ins["L0_attn_b_x_proj_w"]),
            "dt_proj_w": np.asarray(ins["L0_attn_b_dt_proj_w"]),
            "dt_proj_b": np.asarray(ins["L0_attn_b_dt_proj_b"]),
            "A_log": np.asarray(ins["L0_attn_b_A_log"]),
            "D_param": np.asarray(ins["L0_attn_b_D_param"]),
            "out_proj_w": np.asarray(ins["L0_attn_b_out_proj_w"])}
    res = run_bass_kernel_spmd(nc, [dict(x=xs[b], **wmap) for b in cores], core_ids=cores)
    xs = [res.results[b]["xout"] for b in cores]

    nc = bass.Bass("TRN2", target_bir_lowering=False, debug=False)
    build_moe(nc)
    wmap = {"wr": np.asarray(ins["L0_ffn_Wr"]), "w1": np.asarray(ins["L0_ffn_W1"]),
            "w2": np.asarray(ins["L0_ffn_W2"]), "w3": np.asarray(ins["L0_ffn_W3"])}
    res = run_bass_kernel_spmd(nc, [dict(x=xs[b], **wmap) for b in cores], core_ids=cores)
    xs = [res.results[b]["xout"] for b in cores]

    nc = bass.Bass("TRN2", target_bir_lowering=False, debug=False)
    build_l1attna(nc)
    wmap = {"lng": np.asarray(ins["L1_attn_a_ln_g"]), "lnb": np.asarray(ins["L1_attn_a_ln_b"]),
            "w_inT": _tr(ins["L1_attn_a_W_in"]), "b_in": np.asarray(ins["L1_attn_a_b_in"]),
            "convw": np.ascontiguousarray(np.asarray(ins["L1_attn_a_conv_w"]).reshape(2048, 4)),
            "convb": np.asarray(ins["L1_attn_a_conv_b"]),
            "A": np.asarray(ins["L1_attn_a_A"]), "Bm": np.asarray(ins["L1_attn_a_Bm"]),
            "w_cT": _tr(ins["L1_attn_a_W_c"]), "b_c": np.asarray(ins["L1_attn_a_b_c"]),
            "w_dT": _tr(ins["L1_attn_a_W_d"]), "b_d": np.asarray(ins["L1_attn_a_b_d"]),
            "log_delta": np.asarray(ins["L1_attn_a_log_delta"]),
            "w_outT": _tr(ins["L1_attn_a_W_out"]), "b_out": np.asarray(ins["L1_attn_a_b_out"])}
    res = run_bass_kernel_spmd(nc, [dict(x=xs[b], **wmap) for b in cores], core_ids=cores)
    xs = [res.results[b]["xout"] for b in cores]

    ffn_map = {"k0": np.asarray(ins["L1_ffn_k0"]).view(np.uint8),
               "k1": np.asarray(ins["L1_ffn_k1"]).view(np.uint8),
               "k2": np.asarray(ins["L1_ffn_k2"]).view(np.uint8),
               "s0": np.asarray(ins["L1_ffn_s0"]).reshape(1),
               "s1": np.asarray(ins["L1_ffn_s1"]).reshape(1),
               "s2": np.asarray(ins["L1_ffn_s2"]).reshape(1)}
    nc = bass.Bass("TRN2", target_bir_lowering=False, debug=False)
    build_l1ffn(nc, None)
    res = run_bass_kernel_spmd(nc, [dict(x=xs[b], **ffn_map) for b in cores], core_ids=cores)
    xs = [res.results[b]["xout"] for b in cores]

    nc = bass.Bass("TRN2", target_bir_lowering=False, debug=False)
    build_l1attnb(nc)
    wmap = {"wq": np.asarray(ins["L1_attn_b_Wq"]), "wk": np.asarray(ins["L1_attn_b_Wk"]),
            "wvT": _tr(ins["L1_attn_b_Wv"]), "proj": np.asarray(ins["L1_attn_b_proj"])}
    res = run_bass_kernel_spmd(nc, [dict(x=xs[b], **wmap) for b in cores], core_ids=cores)
    xs = [res.results[b]["xout"] for b in cores]

    nc = bass.Bass("TRN2", target_bir_lowering=False, debug=False)
    build_l1ffn(nc, None)
    res = run_bass_kernel_spmd(nc, [dict(x=xs[b], **ffn_map) for b in cores], core_ids=cores)
    xs = [res.results[b]["xout"] for b in cores]

    return np.stack(xs, axis=0)



def build_fused(nc):
    p = Prog(nc)
    p.make_psum_pool(6, 2)
    p.cst(1e-5)
    p.cst(1.0)
    p.idb16()
    di = lambda n, s, dt=F32: Buf(n, nc.dram_tensor(n, s, dt, kind="ExternalInput").ap())
    x0 = di("x", [T, D])
    xout = Buf("xout", nc.dram_tensor("xout", [T, D], F32, kind="ExternalOutput").ap())
    xc = [x0] + [p.dram(f"xchain{i}", [T, D], F32) for i in range(7)] + [xout]

    rn = dict(wqT=di("wqT", [D, 1024]), wkT=di("wkT", [D, 1024]), wvT=di("rn_wvT", [D, 2048]),
              wgT=di("wgT", [D, 2048]), woT=di("woT", [2048, 1024]), gnw=di("gnw", [512]),
              cosT=di("cosT", [128, T]), sinT=di("sinT", [128, T]), m0=di("m0", [128, 512]),
              grow=di("grow", [128, 512]), kdec=di("kdec", [128, 4]),
              g128=retnet_tables()[5], ident=di("ident", [128, 128]))
    mo = dict(wr=di("wr", [D, 8]), w1=di("w1", [8, D, 1536]), w2=di("w2", [8, 1536, D]),
              w3=di("w3", [8, D, 1536]))
    mb = dict(in_proj_w=di("in_proj_w", [D, 4096]), conv_w=di("mb_conv_w", [2048, 4]),
              conv_b=di("mb_conv_b", [2048]), x_proj_w=di("x_proj_w", [2048, 96]),
              dt_proj_w=di("dt_proj_w", [64, 2048]), dt_proj_b=di("dt_proj_b", [2048]),
              A_log=di("A_log", [2048, 16]), D_param=di("D_param", [2048]),
              out_proj_w=di("out_proj_w", [2048, D]))
    la = dict(lng=di("lng", [D]), lnb=di("lnb", [D]), w_inT=di("w_inT", [D, 4096]),
              b_in=di("b_in", [4096]), convw=di("la_convw", [2048, 4]),
              convb=di("la_convb", [2048]), A=di("A", [2048, 16]), Bm=di("Bm", [2048, 16]),
              w_cT=di("w_cT", [2048, 2048]), b_c=di("b_c", [2048]),
              w_dT=di("w_dT", [2048, 2048]), b_d=di("b_d", [2048]),
              log_delta=di("log_delta", [2048]), w_outT=di("w_outT", [2048, D]),
              b_out=di("b_out", [D]))
    lb = dict(wq=di("lb_wq", [D, D]), wk=di("lb_wk", [D, D]), wvT=di("lb_wvT", [D, D]),
              proj=di("proj", [D, 256]))
    lf = dict(k0=di("k0", [2 * D, 4096], U8), k1=di("k1", [4096, 4096], U8),
              k2=di("k2", [4096, D], U8), s0=di("s0", [1]), s1=di("s1", [1]),
              s2=di("s2", [1]))

    def scoped(name, fn, *a, **kw):
        p.set_scope(name)
        p.push_scope()
        fn(*a, **kw)
        p.pop_scope()
        p.barrier()

    scoped("s1_retnet", emit_retnet, p, xc[0], xc[1], **rn)
    scoped("s2_moe", emit_moe, p, xc[1], xc[2], **mo)
    scoped("s3_mamba", emit_mamba, p, xc[2], xc[3], **mb)
    scoped("s4_moe", emit_moe, p, xc[3], xc[4], **mo)
    scoped("s5_l1aa", emit_l1attna, p, xc[4], xc[5], **la)
    p.set_scope("s6_l1ffn")
    p.push_scope()
    wds = build_sign_weights(p, lf["k0"], lf["k1"], lf["k2"])
    emit_l1ffn(p, xc[5], xc[6], *wds, lf["s0"], lf["s1"], lf["s2"])
    p.pop_scope()
    p.barrier()
    scoped("s7_l1ab", emit_l1attnb, p, xc[6], xc[7], lb["wq"], lb["wk"], lb["wvT"], lb["proj"])
    p.set_scope("s8_l1ffn")
    p.push_scope()
    emit_l1ffn(p, xc[7], xc[8], *wds, lf["s0"], lf["s1"], lf["s2"])
    p.pop_scope()
    p.emit()
    return p


def kernel_fused(**ins):
    xs = [np.ascontiguousarray(np.asarray(ins["L0_attn_a_x"])[b]) for b in range(N_CORES)]
    cores = list(range(N_CORES))
    cosT, sinT, m0, grow, kdec, _ = retnet_tables()
    nc = bass.Bass("TRN2", target_bir_lowering=False, debug=False)
    build_fused(nc)
    wmap = {
        "wqT": _tr(ins["L0_attn_a_Wq"]), "wkT": _tr(ins["L0_attn_a_Wk"]),
        "rn_wvT": _tr(ins["L0_attn_a_Wv"]), "wgT": _tr(ins["L0_attn_a_Wg"]),
        "woT": _tr(ins["L0_attn_a_Wo"]), "gnw": np.asarray(ins["L0_attn_a_g_norm_w"]),
        "cosT": cosT, "sinT": sinT, "m0": m0, "grow": grow, "kdec": kdec,
        "ident": np.eye(128, dtype=np.float32),
        "wr": np.asarray(ins["L0_ffn_Wr"]), "w1": np.asarray(ins["L0_ffn_W1"]),
        "w2": np.asarray(ins["L0_ffn_W2"]), "w3": np.asarray(ins["L0_ffn_W3"]),
        "in_proj_w": np.asarray(ins["L0_attn_b_in_proj_w"]),
        "mb_conv_w": np.ascontiguousarray(np.asarray(ins["L0_attn_b_conv_w"]).reshape(2048, 4)),
        "mb_conv_b": np.asarray(ins["L0_attn_b_conv_b"]),
        "x_proj_w": np.asarray(ins["L0_attn_b_x_proj_w"]),
        "dt_proj_w": np.asarray(ins["L0_attn_b_dt_proj_w"]),
        "dt_proj_b": np.asarray(ins["L0_attn_b_dt_proj_b"]),
        "A_log": np.asarray(ins["L0_attn_b_A_log"]),
        "D_param": np.asarray(ins["L0_attn_b_D_param"]),
        "out_proj_w": np.asarray(ins["L0_attn_b_out_proj_w"]),
        "lng": np.asarray(ins["L1_attn_a_ln_g"]), "lnb": np.asarray(ins["L1_attn_a_ln_b"]),
        "w_inT": _tr(ins["L1_attn_a_W_in"]), "b_in": np.asarray(ins["L1_attn_a_b_in"]),
        "la_convw": np.ascontiguousarray(np.asarray(ins["L1_attn_a_conv_w"]).reshape(2048, 4)),
        "la_convb": np.asarray(ins["L1_attn_a_conv_b"]),
        "A": np.asarray(ins["L1_attn_a_A"]), "Bm": np.asarray(ins["L1_attn_a_Bm"]),
        "w_cT": _tr(ins["L1_attn_a_W_c"]), "b_c": np.asarray(ins["L1_attn_a_b_c"]),
        "w_dT": _tr(ins["L1_attn_a_W_d"]), "b_d": np.asarray(ins["L1_attn_a_b_d"]),
        "log_delta": np.asarray(ins["L1_attn_a_log_delta"]),
        "w_outT": _tr(ins["L1_attn_a_W_out"]), "b_out": np.asarray(ins["L1_attn_a_b_out"]),
        "lb_wq": np.asarray(ins["L1_attn_b_Wq"]), "lb_wk": np.asarray(ins["L1_attn_b_Wk"]),
        "lb_wvT": _tr(ins["L1_attn_b_Wv"]), "proj": np.asarray(ins["L1_attn_b_proj"]),
        "k0": np.asarray(ins["L1_ffn_k0"]).view(np.uint8),
        "k1": np.asarray(ins["L1_ffn_k1"]).view(np.uint8),
        "k2": np.asarray(ins["L1_ffn_k2"]).view(np.uint8),
        "s0": np.asarray(ins["L1_ffn_s0"]).reshape(1),
        "s1": np.asarray(ins["L1_ffn_s1"]).reshape(1),
        "s2": np.asarray(ins["L1_ffn_s2"]).reshape(1),
    }
    res = run_bass_kernel_spmd(nc, [dict(x=xs[b], **wmap) for b in cores], core_ids=cores)
    return np.stack([res.results[b]["xout"] for b in cores], axis=0)
